# version 07
# baseline (speedup 1.0000x reference)
import math
from contextlib import ExitStack

import numpy as np

import concourse.bass as bass
import concourse.mybir as mybir
from concourse.bass_utils import run_bass_kernel_spmd

F32 = mybir.dt.float32
BF16 = mybir.dt.bfloat16
I32 = mybir.dt.int32
U8 = mybir.dt.uint8
AF = mybir.ActivationFunctionType
ALU = mybir.AluOpType
AX = mybir.AxisListType

S = 2048
D = 1024
H = 16
HD = 64
DL = 256
LAM = 0.01
NSEQ = 2
TT = 16
KT = 16
NQG = 2
QG = 1024
EPS = 1e-5
VST = H * 65 + 1


class Sched:
    ENGS = ("sync", "tensor", "vector", "scalar", "gpsimd")

    def __init__(self):
        self.ops = {e: [] for e in self.ENGS}
        self.cnt = {}

    def emit(self, eng, fn):
        self.ops[eng].append(fn)

    def inc(self, sem_name, k=1):
        self.cnt[sem_name] = self.cnt.get(sem_name, 0) + k
        return self.cnt[sem_name]

    def val(self, sem_name):
        return self.cnt.get(sem_name, 0)


DEBUG = False
STRICT = False


def dbg_on(name):
    return DEBUG is True or (isinstance(DEBUG, (set, frozenset)) and name in DEBUG)

STRICT_PHASES = set()


def build_nc():
    nc = bass.Bass("TRN2", target_bir_lowering=False, detect_race_conditions=False)

    x_in = nc.dram_tensor("x_in", [NSEQ, S, D], F32, kind="ExternalInput").ap()
    wq_in = nc.dram_tensor("wq_in", [D, D], F32, kind="ExternalInput").ap()
    wkT_in = nc.dram_tensor("wkT_in", [D, D], F32, kind="ExternalInput").ap()
    wvT_in = nc.dram_tensor("wvT_in", [D, D], F32, kind="ExternalInput").ap()
    wdn_in = nc.dram_tensor("wdn_in", [D, DL], F32, kind="ExternalInput").ap()
    wupk_in = nc.dram_tensor("wupk_in", [DL, D], F32, kind="ExternalInput").ap()
    wupv_in = nc.dram_tensor("wupv_in", [DL, D], F32, kind="ExternalInput").ap()
    wo_in = nc.dram_tensor("wo_in", [D, D], F32, kind="ExternalInput").ap()
    wt8_in = nc.dram_tensor("wt8_in", [D, D], U8, kind="ExternalInput").ap()
    wsc_in = nc.dram_tensor("wsc_in", [1, 1], F32, kind="ExternalInput").ap()
    bias_in = nc.dram_tensor("bias_in", [1, D], F32, kind="ExternalInput").ap()
    y_out = nc.dram_tensor("y_out", [NSEQ, S, D], F32, kind="ExternalOutput").ap()
    if DEBUG:
        d_xn1T = nc.dram_tensor("d_xn1T", [128, 8 * 2048], BF16, kind="ExternalOutput").ap()
        d_latT = nc.dram_tensor("d_latT", [128, 2 * 2048], BF16, kind="ExternalOutput").ap()
        d_vrec = nc.dram_tensor("d_vrec", [128, KT * VST], BF16, kind="ExternalOutput").ap()
        d_qA = nc.dram_tensor("d_qA", [128, 2048], BF16, kind="ExternalOutput").ap()
        d_qB = nc.dram_tensor("d_qB", [128, 2048], BF16, kind="ExternalOutput").ap()
        d_krT = nc.dram_tensor("d_krT", [128, 2048], BF16, kind="ExternalOutput").ap()
        d_aoutT = nc.dram_tensor("d_aoutT", [128, 8 * 2048], BF16, kind="ExternalOutput").ap()
        d_wt = nc.dram_tensor("d_wt", [128, 8 * 1024], BF16, kind="ExternalOutput").ap()
        d_weff = nc.dram_tensor("d_weff", [128, 8 * 256], BF16, kind="ExternalOutput").ap()
        d_ab = nc.dram_tensor("d_ab", [128, 1024], BF16, kind="ExternalOutput").ap()
        d_pq = nc.dram_tensor("d_pq", [128, 1024], F32, kind="ExternalOutput").ap()
        d_ftab = nc.dram_tensor("d_ftab", [128, 128], F32, kind="ExternalOutput").ap()
        d_colA = nc.dram_tensor("d_colA", [128, KT], F32, kind="ExternalOutput").ap()
        d_ab16 = nc.dram_tensor("d_ab16", [128, KT * 128], BF16, kind="ExternalOutput").ap()
        d_pv = nc.dram_tensor("d_pv", [128, 1024], F32, kind="ExternalOutput").ap()

    sc = Sched()

    with ExitStack() as st:
        def sb(name, shape, dtype):
            return st.enter_context(nc.sbuf_tensor(name, shape, dtype))

        def ps(name, shape, dtype=F32):
            return st.enter_context(nc.psum_tensor(name, shape, dtype))

        def sem(name):
            return st.enter_context(nc.semaphore(name))

        xn1T = sb("xn1T", [128, 8 * 2048], BF16)
        latT = sb("latT", [128, 2 * 2048], BF16)
        vrec = sb("vrec", [128, KT * VST], BF16)
        aoutT = sb("aoutT", [128, 8 * 2048], BF16)
        w_big = sb("w_big", [128, 8 * 1024], BF16)
        wupk_sb = sb("wupk_sb", [128, 2 * 1024], BF16)
        wupv_sb = sb("wupv_sb", [128, 2 * 1024], BF16)
        weff_sb = sb("weff_sb", [128, 8 * 256], BF16)
        wt_sb = sb("wt_sb", [128, 8 * 1024], BF16)
        eneg = sb("eneg", [128, 2048], F32)
        epos = sb("epos", [128, 2048], F32)
        colA = sb("colA", [128, KT], F32)
        colB = sb("colB", [128, KT], F32)
        ftab = sb("ftab", [128, 128], F32)
        ident = sb("ident", [128, 128], BF16)
        ones_col = sb("ones_col", [128, 1], BF16)
        bias_b = sb("bias_b", [128, 1024], F32)
        wsc_col = sb("wsc_col", [128, 1], F32)
        qA = sb("qA", [128, 2048], BF16)
        qB = sb("qB", [128, 2048], BF16)
        krT = sb("krT", [128, 2048], BF16)
        xt0 = sb("xt0", [128, 1024], F32)
        xt1 = sb("xt1", [128, 1024], F32)
        xnw0 = sb("xnw0", [128, 1024], F32)
        xnw1 = sb("xnw1", [128, 1024], F32)
        xn_stage = sb("xn_stage", [128, 1024], BF16)
        scr_y = sb("scr_y", [128, 1024], F32)
        scr_t = sb("scr_t", [128, 1024], F32)
        onec = sb("onec", [128, 1], F32)
        stat = sb("stat", [128, 16], F32)
        iot_c = sb("iot_c", [128, 1], F32)
        rr = scr_y
        rb = scr_t
        wkvT = aoutT
        abuf0 = xnw0[:, :].bitcast(BF16)[:, 0:1024]
        abuf1 = xnw1[:, :].bitcast(BF16)[:, 0:1024]
        si1 = qA[:, :].bitcast(I32)
        si2 = qB[:, :].bitcast(I32)
        si3 = krT[:, :].bitcast(I32)
        i2db = si2[:, 0:128]
        f2d = krT[:, :].bitcast(F32)[:, 0:128]

        def pad():
            I("vector", lambda e: e.tensor_copy(out=stat[:, 14:15], in_=onec[:]), "sV")

        def recip_safe(out_ap, in_ap):
            I("vector", lambda e: e.tensor_copy(out=stat[:, 14:15], in_=onec[:]), "sV")
            I("vector", lambda e: e.tensor_copy(out=stat[:, 15:16], in_=onec[:]), "sV")
            I("vector", lambda e, o=out_ap, i=in_ap: e.reciprocal(o, i), "sV")
            I("vector", lambda e: e.tensor_copy(out=stat[:, 14:15], in_=onec[:]), "sV")
            I("vector", lambda e: e.tensor_copy(out=stat[:, 15:16], in_=onec[:]), "sV")

        def ones_row(srow):
            a = onec[srow:srow + 1, 0:1]
            return bass.AP(a.tensor, a.offset, [[int(a.ap[0][0]), 1], [0, 128]])

        P0 = ps("P0", [128, 1024])
        P1 = ps("P1", [128, 1024])
        P2 = ps("P2", [128, 512])
        P3 = ps("P3", [128, 512])
        P5 = ps("P5", [128, 512])
        P4 = ps("P4", [128, 1024], BF16)

        sL = sem("sL")
        sG2 = sem("sG2")
        sT = sem("sT")
        sV = sem("sV")
        sC = sem("sC")
        sG = sem("sG")
        sS = sem("sS")
        sP = sem("sP")
        SEMS = {"sL": sL, "sG2": sG2, "sT": sT, "sV": sV, "sC": sC, "sG": sG, "sS": sS, "sP": sP}

        def Wt(eng_name, sem_name, tok):
            if tok:
                s = SEMS[sem_name]
                sc.emit(eng_name, lambda e, s=s, t=tok: e.wait_ge(s, t))

        def W(eng_name, sem_name, v=None):
            target = sc.val(sem_name) if v is None else v
            s = SEMS[sem_name]
            if target > 0:
                sc.emit(eng_name, lambda e, s=s, t=target: e.wait_ge(s, t))

        phase = ["prep"]

        def I(eng_name, fn, sem_name=None, k=1):
            strict_here = (STRICT or phase[0] in STRICT_PHASES
                           or (phase[0] + ":" + eng_name) in STRICT_PHASES)
            if strict_here and eng_name in ("tensor", "vector", "scalar"):
                for other in ("sT", "sV", "sC"):
                    own = {"tensor": "sT", "vector": "sV", "scalar": "sC"}[eng_name]
                    if other != own:
                        W(eng_name, other)
                if sem_name is None:
                    sem_name = {"tensor": "sT", "vector": "sV", "scalar": "sC"}[eng_name]
            if sem_name is None:
                sc.emit(eng_name, fn)
                return None
            s = SEMS[sem_name]
            tok = sc.inc(sem_name, k)
            sc.emit(eng_name, lambda e, s=s, k=k, fn=fn: fn(e).then_inc(s, k))
            return tok

        I("sync", lambda e: e.dma_start(out=bias_b[0:1, :], in_=bias_in[:]), "sL", 16)
        I("sync", lambda e: e.dma_start(out=wsc_col[0:1, :], in_=wsc_in[:]), "sL", 16)
        for c in range(8):
            ta = scr_y if c % 2 == 0 else xt0
            tb = scr_t if c % 2 == 0 else xt1
            I("sync", lambda e, c=c, ta=ta: e.dma_start(out=ta[:], in_=wkT_in[128 * c:128 * (c + 1), :]), "sL", 16)
            I("sync", lambda e, c=c, tb=tb: e.dma_start(out=tb[:], in_=wvT_in[128 * c:128 * (c + 1), :]), "sL", 16)
            W("vector", "sL")
            I("vector", lambda e, c=c, ta=ta, tb=tb: e.tensor_add(
                wkvT[:, 1024 * c:1024 * (c + 1)], ta[:], tb[:]), "sV")
            W("sync", "sV")

        I("gpsimd", lambda e: e.dma_start(out=wupk_sb[:, :].rearrange("p (c n) -> p c n", n=1024), in_=wupk_in.rearrange("(c p) n -> p c n", p=128)), "sG2", 16)
        I("gpsimd", lambda e: e.dma_start(out=wupv_sb[:, :].rearrange("p (c n) -> p c n", n=1024), in_=wupv_in.rearrange("(c p) n -> p c n", p=128)), "sG2", 16)
        I("gpsimd", lambda e: e.dma_start(out=abuf0.rearrange("p (c n) -> p c n", n=256), in_=wdn_in[0:512, :].rearrange("(c p) n -> p c n", p=128)), "sG2", 16)
        I("gpsimd", lambda e: e.dma_start(out=abuf1.rearrange("p (c n) -> p c n", n=256), in_=wdn_in[512:1024, :].rearrange("(c p) n -> p c n", p=128)), "sG2", 16)

        I("gpsimd", lambda e: e.iota(eneg[0:1, :], [[1, 2048]], base=0, channel_multiplier=0,
                                     allow_small_or_imprecise_dtypes=True), "sG")
        I("gpsimd", lambda e: e.iota(iot_c[:], [[1, 1]], base=0, channel_multiplier=1,
                                     allow_small_or_imprecise_dtypes=True), "sG")
        I("gpsimd", lambda e: e.iota(i2db, [[1, 128]], base=0, channel_multiplier=-1,
                                     allow_small_or_imprecise_dtypes=True), "sG")

        W("vector", "sG")
        I("vector", lambda e: e.tensor_scalar(si1[:, 0:128], i2db, 0, None, ALU.is_equal), "sV")
        I("vector", lambda e: e.tensor_copy(out=ident[:], in_=si1[:, 0:128]), "sV")
        I("vector", lambda e: e.tensor_copy(out=f2d, in_=i2db), "sV")
        I("vector", lambda e: e.tensor_scalar(f2d, f2d, 0.0, None, ALU.min), "sV")
        I("vector", lambda e: e.tensor_copy(out=epos[0:1, :], in_=eneg[0:1, :]), "sV")
        W("scalar", "sV")
        I("scalar", lambda e: e.activation(out=ftab[:], in_=f2d[:], func=AF.Exp, scale=2.0 * LAM), "sC")
        I("scalar", lambda e: e.activation(out=eneg[0:1, :], in_=eneg[0:1, :], func=AF.Exp, scale=-LAM), "sC")
        I("scalar", lambda e: e.activation(out=epos[0:1, :], in_=epos[0:1, :], func=AF.Exp, scale=LAM), "sC")
        I("scalar", lambda e: e.activation(out=stat[:, 0:1], in_=iot_c[:], func=AF.Exp, scale=LAM), "sC")
        I("scalar", lambda e: e.activation(out=stat[:, 1:2], in_=iot_c[:], func=AF.Exp, scale=-LAM), "sC")
        W("vector", "sC")
        I("vector", lambda e: e.tensor_scalar(eneg[0:1, :], eneg[0:1, :], 1.0 / 32.0, None, ALU.mult), "sV")
        I("vector", lambda e: e.tensor_scalar(epos[0:1, :], epos[0:1, :], 1.0 / 32.0, None, ALU.mult), "sV")
        for kt in range(KT):
            I("vector", lambda e, kt=kt: e.tensor_scalar(colA[:, kt:kt + 1], stat[:, 0:1],
                                                         float(math.exp(LAM * 128 * kt)), None, ALU.mult), "sV")
            I("vector", lambda e, kt=kt: e.tensor_scalar(colB[:, kt:kt + 1], stat[:, 1:2],
                                                         float(math.exp(-LAM * 128 * kt)), None, ALU.mult), "sV")
        I("gpsimd", lambda e: e.memset(onec[:], 1.0), "sG")
        I("gpsimd", lambda e: e.memset(ones_col[:], 1.0), "sG")
        W("tensor", "sV")
        W("tensor", "sG")
        W("tensor", "sL", 32)
        bcast_jobs = []
        for j in range(4):
            bcast_jobs.append((eneg, 512 * j, 512))
            bcast_jobs.append((epos, 512 * j, 512))
        bcast_jobs.append((bias_b, 0, 512))
        bcast_jobs.append((bias_b, 512, 512))
        bcast_jobs.append((wsc_col, 0, 1))
        for (dst, off, wid) in bcast_jobs:
            I("tensor", lambda e, dst=dst, off=off, wid=wid: e.matmul(
                P5[:, 0:wid], ones_row(0), dst[0:1, off:off + wid],
                start=True, stop=True), "sT")
            W("vector", "sT")
            I("vector", lambda e, dst=dst, off=off, wid=wid: e.tensor_copy(
                out=dst[:, off:off + wid], in_=P5[:, 0:wid]), "sV")
            W("tensor", "sV")
        I("gpsimd", lambda e: e.memset(
            vrec[:, :].rearrange("p (k q) -> p k q", q=VST)[:, :, 0:H * 65].rearrange(
                "p k (h j) -> p k h j", j=65)[:, :, :, 64:65], 1.0), "sG")

        if DEBUG and dbg_on("prepb"):
            W("gpsimd", "sV")
            I("gpsimd", lambda e: e.dma_start(out=d_pq[:, 0:1], in_=wsc_col[:, :], allow_slow_non_contiguous=True), "sS", 16)
            I("gpsimd", lambda e: e.dma_start(out=d_pq[:, 1:2], in_=eneg[:, 777:778], allow_slow_non_contiguous=True), "sS", 16)
            I("gpsimd", lambda e: e.dma_start(out=d_pq[:, 2:3], in_=epos[:, 777:778], allow_slow_non_contiguous=True), "sS", 16)
            I("gpsimd", lambda e: e.dma_start(out=d_pq[:, 3:4], in_=bias_b[:, 333:334], allow_slow_non_contiguous=True), "sS", 16)
        scr_i4 = scr_t[:, :].bitcast(I32)
        for c in range(8):
            cs = slice(1024 * c, 1024 * (c + 1))
            W("gpsimd", "sV")
            I("gpsimd", lambda e, c=c: e.dma_start(
                out=scr_y[:], in_=wt8_in.rearrange("(c p) n -> c p n", p=128)[c]), "sG2", 16)
            W("vector", "sG2")
            I("vector", lambda e: e.tensor_copy(out=si1, in_=scr_y[:]), "sV")
            I("vector", lambda e: e.tensor_scalar(si2, si1, 3, 15, ALU.logical_shift_right, ALU.bitwise_and), "sV")
            I("vector", lambda e: e.tensor_scalar(si3, si1, 7, None, ALU.bitwise_and), "sV")
            I("vector", lambda e: e.tensor_scalar(si3, si3, 8, None, ALU.add), "sV")
            I("vector", lambda e: e.tensor_scalar(si1, si1, 7, None, ALU.logical_shift_right), "sV")
            I("vector", lambda e: e.tensor_scalar(si1, si1, -2, 1, ALU.mult, ALU.add), "sV")
            I("vector", lambda e: e.tensor_tensor(scr_i4, si3, si2, ALU.arith_shift_left), "sV")
            I("vector", lambda e: e.tensor_scalar(si3, si3, 2, -16, ALU.mult, ALU.add), "sV")
            I("vector", lambda e: e.tensor_scalar(si2, si2, 0, None, ALU.is_equal), "sV")
            I("vector", lambda e: e.tensor_tensor(si3, si3, si2, ALU.mult), "sV")
            I("vector", lambda e: e.tensor_scalar(si2, si2, -1, 1, ALU.mult, ALU.add), "sV")
            I("vector", lambda e: e.tensor_tensor(scr_i4, scr_i4, si2, ALU.mult), "sV")
            I("vector", lambda e: e.tensor_tensor(scr_i4, scr_i4, si3, ALU.add), "sV")
            I("vector", lambda e: e.tensor_tensor(scr_i4, scr_i4, si1, ALU.mult), "sV")
            I("vector", lambda e: e.tensor_copy(out=scr_t[:], in_=scr_i4), "sV")
            I("vector", lambda e, cs=cs: e.tensor_scalar(wt_sb[:, cs], scr_t[:], float(2.0 ** -10), None, ALU.mult), "sV")

        W("tensor", "sG2", 64)
        W("tensor", "sV")
        for dt_ in range(8):
            for dd in range(8):
                wdn_src = abuf0 if dd < 4 else abuf1
                I("tensor", lambda e, dt_=dt_, dd=dd, wdn_src=wdn_src: e.matmul(
                    P5[:, 0:256],
                    wkvT[:, 1024 * dd + 128 * dt_:1024 * dd + 128 * (dt_ + 1)],
                    wdn_src[:, 256 * (dd % 4):256 * (dd % 4 + 1)],
                    start=(dd == 0), stop=(dd == 7)), "sT" if dd == 7 else None)
            W("vector", "sT")
            I("vector", lambda e, dt_=dt_: e.tensor_copy(out=weff_sb[:, 256 * dt_:256 * (dt_ + 1)], in_=P5[:, 0:256]), "sV")
            W("tensor", "sV")

        PBF = {0: None, 1: None}

        def WB(eng_name, bid):
            ent = PBF[bid]
            if ent:
                Wt(eng_name, ent[0], ent[1])

        for sq in range(NSEQ):
            phase[0] = "A"
            for tt in range(TT):
                xt = xt0 if tt % 2 == 0 else xt1
                W("sync", "sV")
                W("sync", "sC")
                I("sync", lambda e, sq=sq, tt=tt, xt=xt: e.dma_start(
                    out=xt[:], in_=x_in[sq, 128 * tt:128 * (tt + 1), :]), "sL", 16)
                W("vector", "sL")
                I("vector", lambda e, xt=xt: e.reduce_sum(stat[:, 0:1], xt[:], AX.X), "sV")
                W("scalar", "sL")
                W("scalar", "sV")
                I("scalar", lambda e, xt=xt: e.activation(out=scr_y[:], in_=xt[:], func=AF.Square,
                                                          accum_out=stat[:, 1:2]), "sC")
                W("vector", "sC")
                I("vector", lambda e: e.tensor_scalar(stat[:, 0:1], stat[:, 0:1], 1.0 / D, None, ALU.mult), "sV")
                I("vector", lambda e: e.tensor_scalar(stat[:, 1:2], stat[:, 1:2], 1.0 / D, None, ALU.mult), "sV")
                pad()
                I("vector", lambda e: e.tensor_tensor(stat[:, 2:3], stat[:, 0:1], stat[:, 0:1], ALU.mult), "sV")
                pad(); pad()
                I("vector", lambda e: e.tensor_tensor(stat[:, 1:2], stat[:, 1:2], stat[:, 2:3], ALU.subtract), "sV")
                pad(); pad()
                I("vector", lambda e: e.tensor_scalar(stat[:, 1:2], stat[:, 1:2], EPS, None, ALU.add), "sV")
                W("scalar", "sV")
                I("scalar", lambda e: e.activation(out=stat[:, 2:3], in_=stat[:, 1:2], func=AF.Sqrt), "sC")
                W("vector", "sC")
                recip_safe(stat[:, 3:4], stat[:, 2:3])
                I("vector", lambda e, xt=xt: e.tensor_scalar(
                    xn_stage[:], xt[:], stat[:, 0:1], stat[:, 3:4], ALU.subtract, ALU.mult), "sV")
                W("tensor", "sV")
                W("tensor", "sC")
                for cdx in range(8):
                    I("tensor", lambda e, cdx=cdx: e.transpose(
                        P4[:, 128 * cdx:128 * (cdx + 1)], xn_stage[:, 128 * cdx:128 * (cdx + 1)], ident[:]),
                        "sT" if cdx == 7 else None)
                W("vector", "sT")
                I("vector", lambda e, tt=tt: e.tensor_copy(
                    out=xn1T[:, :].rearrange("p (c t) -> p c t", c=8)[:, :, 128 * tt:128 * (tt + 1)],
                    in_=P4[:, :].rearrange("p (c t) -> p c t", c=8)), "sV")
                W("tensor", "sV")

            if DEBUG and sq == 0 and dbg_on("xn1T"):
                W("gpsimd", "sV")
                I("gpsimd", lambda e: e.dma_start(out=d_xn1T[:], in_=xn1T[:, :]), "sS", 16)
            W("gpsimd", "sT")
            I("gpsimd", lambda e: e.dma_start(out=w_big[:, :].rearrange("p (c n) -> p c n", n=1024), in_=wq_in.rearrange("(c p) n -> p c n", p=128)), "sG2", 16)

            phase[0] = "B"
            W("tensor", "sG")
            for lc in range(2):
                for tq in range(4):
                    bid = (lc * 4 + tq) % 2
                    P = P0 if bid == 0 else P1
                    WB("tensor", bid)
                    mm_tok = None
                    for dc in range(8):
                        mm_tok = I("tensor", lambda e, lc=lc, tq=tq, dc=dc, P=P: e.matmul(
                            P[:, 0:512],
                            weff_sb[:, 256 * dc + 128 * lc:256 * dc + 128 * (lc + 1)],
                            xn1T[:, 2048 * dc + 512 * tq:2048 * dc + 512 * (tq + 1)],
                            start=(dc == 0), stop=(dc == 7)), "sT" if dc == 7 else None) or mm_tok
                    Wt("vector", "sT", mm_tok)
                    PBF[bid] = ("sV", I("vector", lambda e, lc=lc, tq=tq, P=P: e.tensor_copy(
                        out=latT[:, 2048 * lc + 512 * tq:2048 * lc + 512 * (tq + 1)], in_=P[:, 0:512]), "sV"))
            for tt in range(TT):
                for half in range(2):
                    bid = (tt * 2 + half) % 2
                    P = P0 if bid == 0 else P1
                    WB("tensor", bid)
                    mm_tok = None
                    for lc in range(2):
                        mm_tok = I("tensor", lambda e, tt=tt, half=half, lc=lc, P=P: e.matmul(
                            P[:, 0:512],
                            latT[:, 2048 * lc + 128 * tt:2048 * lc + 128 * (tt + 1)],
                            wupv_sb[:, 1024 * lc + 512 * half:1024 * lc + 512 * (half + 1)],
                            start=(lc == 0), stop=(lc == 1)), "sT" if lc == 1 else None) or mm_tok
                    Wt("vector", "sT", mm_tok)
                    PBF[bid] = ("sV", I("vector", lambda e, tt=tt, half=half, P=P: e.tensor_copy(
                        out=vrec[:, :].rearrange("p (k q) -> p k q", q=VST)[
                            :, tt, 65 * 8 * half:65 * 8 * (half + 1)].rearrange(
                            "p (h j) -> p h j", j=65)[:, :, 0:64],
                        in_=P[:, 0:512].rearrange("p (h j) -> p h j", j=64)), "sV"))

            if DEBUG and sq == 0 and dbg_on("latT"):
                W("gpsimd", "sV")
                I("gpsimd", lambda e: e.dma_start(out=d_latT[:], in_=latT[:, :]), "sS", 16)
                I("gpsimd", lambda e: e.dma_start(out=d_vrec[:], in_=vrec[:, :]), "sS", 16)
                I("gpsimd", lambda e: e.dma_start(out=d_weff[:], in_=weff_sb[:, :]), "sS", 16)
                I("gpsimd", lambda e: e.dma_start(out=d_wt[:], in_=wt_sb[:, :]), "sS", 16)
            phase[0] = "attn"
            att_prev_pv = {}
            att_last_qk = {0: None}
            pending_norm = [None]
            att_prev_evac = {0: sc.val("sV")}
            att_prev_p5 = {0: sc.val("sV")}
            W("tensor", "sG2")
            W("scalar", "sS")
            W("tensor", "sS")
            W("vector", "sS")
            for hp in range(8):
                prev_qk = att_last_qk[0]
                for tq in range(4):
                    bid = tq % 2
                    P = P0 if bid == 0 else P1
                    WB("tensor", bid)
                    mm_tok = None
                    for lc in range(2):
                        mm_tok = I("tensor", lambda e, hp=hp, tq=tq, lc=lc, P=P: e.matmul(
                            P[:, 0:512],
                            wupk_sb[:, 1024 * lc + 128 * hp:1024 * lc + 128 * (hp + 1)],
                            latT[:, 2048 * lc + 512 * tq:2048 * lc + 512 * (tq + 1)],
                            start=(lc == 0), stop=(lc == 1)), "sT" if lc == 1 else None) or mm_tok
                    Wt("vector", "sT", mm_tok)
                    Wt("vector", "sT", prev_qk)
                    PBF[bid] = ("sV", I("vector", lambda e, tq=tq, P=P: e.tensor_copy(
                        out=krT[:, 512 * tq:512 * (tq + 1)], in_=P[:, 0:512]), "sV"))
                for tq in range(4):
                    bid = tq % 2
                    P = P0 if bid == 0 else P1
                    WB("tensor", bid)
                    mm_tok = None
                    for dc in range(8):
                        mm_tok = I("tensor", lambda e, hp=hp, tq=tq, dc=dc, P=P: e.matmul(
                            P[:, 0:512],
                            w_big[:, 1024 * dc + 128 * hp:1024 * dc + 128 * (hp + 1)],
                            xn1T[:, 2048 * dc + 512 * tq:2048 * dc + 512 * (tq + 1)],
                            start=(dc == 0), stop=(dc == 7)), "sT" if dc == 7 else None) or mm_tok
                    Wt("vector", "sT", mm_tok)
                    Wt("vector", "sT", prev_qk)
                    I("vector", lambda e, tq=tq, P=P: e.tensor_mul(
                        qA[:, 512 * tq:512 * (tq + 1)], P[:, 0:512], eneg[:, 512 * tq:512 * (tq + 1)]), "sV")
                    PBF[bid] = ("sV", I("vector", lambda e, tq=tq, P=P: e.tensor_mul(
                        qB[:, 512 * tq:512 * (tq + 1)], P[:, 0:512], epos[:, 512 * tq:512 * (tq + 1)]), "sV"))
                pp_tok = sc.val("sV")

                if DEBUG and sq == 0 and hp == 0 and dbg_on("qA"):
                    W("gpsimd", "sV")
                    I("gpsimd", lambda e: e.dma_start(out=d_qA[:], in_=qA[:, :]), "sS", 16)
                    I("gpsimd", lambda e: e.dma_start(out=d_qB[:], in_=qB[:, :]), "sS", 16)
                    I("gpsimd", lambda e: e.dma_start(out=d_krT[:], in_=krT[:, :]), "sS", 16)
                    W("tensor", "sS")
                for head in range(2):
                    ha = 2 * hp + head
                    hr = slice(64 * head, 64 * (head + 1))
                    vlo, vhi = (0, 64) if head == 0 else (64, 128)
                    srow = 64 if head == 0 else 0
                    for qg in range(NQG):
                        qgs = QG * qg
                        t_qk = {}
                        c_exp = {}
                        t_pv = {}
                        first_qk = (head == 0 and qg == 0)

                        def emit_qk(kt, qgs=qgs, hr=hr, first_qk=first_qk):
                            k0 = 128 * kt
                            Pq = P0 if kt % 2 == 0 else P1
                            ab = abuf0 if kt % 2 == 0 else abuf1
                            regs = []
                            if k0 + 128 <= qgs:
                                regs.append(("A", 0, QG))
                            elif k0 >= qgs + QG:
                                regs.append(("B", 0, QG))
                            else:
                                rel = k0 - qgs
                                if rel > 0:
                                    regs.append(("B", 0, rel))
                                regs.append(("Ad", rel, QG - rel))
                            WB("tensor", kt % 2)
                            if kt == 0 and first_qk:
                                Wt("tensor", "sV", pp_tok)
                            mms = []
                            for (side, rel, wid) in regs:
                                src = qA if side in ("A", "Ad") else qB
                                off = rel
                                end = rel + wid
                                while off < end:
                                    nb = min(end, (off // 512 + 1) * 512)
                                    mms.append((off, nb - off, src))
                                    off = nb
                            for j, (rel, w_, src) in enumerate(mms):
                                tok = I("tensor", lambda e, kt=kt, rel=rel, w_=w_, src=src, Pq=Pq, qgs=qgs, hr=hr: e.matmul(
                                    Pq[:, rel:rel + w_],
                                    krT[hr, 128 * kt:128 * (kt + 1)],
                                    src[hr, qgs + rel:qgs + rel + w_],
                                    start=True, stop=True),
                                    "sT" if j == len(mms) - 1 else None)
                            t_qk[kt] = tok
                            att_last_qk[0] = tok
                            has_diag = any(s == "Ad" for (s, _, _) in regs)
                            v_diag = None
                            if has_diag:
                                rel = k0 - qgs
                                Wt("vector", "sT", t_qk[kt])
                                v_diag = I("vector", lambda e, rel=rel, Pq=Pq: e.tensor_mul(
                                    Pq[:, rel:rel + 128], Pq[:, rel:rel + 128], ftab[:]), "sV")
                            Wt("scalar", "sT", t_qk[kt])
                            Wt("scalar", "sT", att_prev_pv.get(kt % 2))
                            if has_diag:
                                Wt("scalar", "sV", v_diag)
                            for (side, rel, wid) in regs:
                                col = colA if side in ("A", "Ad") else colB
                                tok = I("scalar", lambda e, rel=rel, wid=wid, kt=kt, col=col, Pq=Pq, ab=ab: e.activation(
                                    out=ab[:, rel:rel + wid], in_=Pq[:, rel:rel + wid],
                                    func=AF.Exp, scale=col[:, kt:kt + 1]), "sC")
                            c_exp[kt] = tok
                            PBF[kt % 2] = ("sC", tok)

                        def emit_pv(kt, ha=ha):
                            ab = abuf0 if kt % 2 == 0 else abuf1
                            Wt("tensor", "sC", c_exp[kt])
                            if kt == 0:
                                Wt("tensor", "sV", att_prev_evac[0])
                            for half in range(2):
                                Pv = P2 if half == 0 else P3
                                tok = I("tensor", lambda e, ha=ha, kt=kt, Pv=Pv, ab=ab, half=half: e.matmul(
                                    Pv[0:65, :],
                                    vrec[:, VST * kt + 65 * ha:VST * kt + 65 * ha + 65],
                                    ab[:, 512 * half:512 * (half + 1)],
                                    start=(kt == 0), stop=(kt == KT - 1)),
                                    "sT" if half == 1 else None)
                            t_pv[kt] = tok
                            att_prev_pv[kt % 2] = tok

                        emit_qk(0)
                        if pending_norm[0] is not None:
                            pending_norm[0]()
                            pending_norm[0] = None
                        for kt in range(1, KT):
                            emit_qk(kt)
                            emit_pv(kt - 1)
                        emit_pv(KT - 1)
                        Wt("vector", "sT", t_pv[KT - 1])
                        rtoks = []
                        for half in range(2):
                            Pv = P2 if half == 0 else P3
                            recip_safe(rr[64:65, 512 * half:512 * (half + 1)], Pv[64:65, :])
                            rtoks.append(sc.val("sV"))

                        def emit_norm(hp=hp, qgs=qgs, hr=hr, rtoks=rtoks):
                            for half in range(2):
                                Pv = P2 if half == 0 else P3
                                Wt("tensor", "sV", rtoks[half])
                                Wt("tensor", "sV", att_prev_p5[0])
                                btok = I("tensor", lambda e, half=half: e.matmul(
                                    P5[:, 0:512], ones_row(64),
                                    rr[64:65, 512 * half:512 * (half + 1)],
                                    start=True, stop=True), "sT")
                                Wt("vector", "sT", btok)
                                att_prev_p5[0] = I("vector", lambda e, half=half: e.tensor_copy(
                                    out=rb[0:64, 512 * half:512 * (half + 1)], in_=P5[0:64, 0:512]), "sV")
                                att_prev_evac[0] = I("vector", lambda e, Pv=Pv, hp=hp, qgs=qgs, half=half, hr=hr: e.tensor_mul(
                                    aoutT[hr, 2048 * hp + qgs + 512 * half:2048 * hp + qgs + 512 * (half + 1)],
                                    Pv[0:64, :], rb[0:64, 512 * half:512 * (half + 1)]), "sV")
                        pending_norm[0] = emit_norm
            if pending_norm[0] is not None:
                pending_norm[0]()
                pending_norm[0] = None
            if DEBUG and sq == 0 and dbg_on("aoutT"):
                W("gpsimd", "sV")
                I("gpsimd", lambda e: e.dma_start(out=d_aoutT[:], in_=aoutT[:, :]), "sS", 16)
            W("gpsimd", "sT")
            I("gpsimd", lambda e: e.dma_start(out=w_big[:, :].rearrange("p (c n) -> p c n", n=1024), in_=wo_in.rearrange("(c p) n -> p c n", p=128)), "sG2", 16)

            phase[0] = "cffn_wo"
            W("tensor", "sG2")
            if sq == 0:
                store_tok = {}
            p4_free = {0: None}
            adds_tok = {}
            stt_tok = {}
            quant_tok = {}
            t_wo_tok = {}

            def emit_wo(tt):
                xt = xt0 if tt % 2 == 0 else xt1
                W("sync", "sV")
                I("sync", lambda e, sq=sq, tt=tt, xt=xt: e.dma_start(
                    out=xt[:], in_=x_in[sq, 128 * tt:128 * (tt + 1), :]), "sL", 16)
                if tt == 0:
                    WB("tensor", 0)
                else:
                    Wt("tensor", "sV", adds_tok.get(tt - 1))
                tok = None
                for half in range(2):
                    for dc in range(8):
                        tok = I("tensor", lambda e, tt=tt, half=half, dc=dc: e.matmul(
                            P0[:, 512 * half:512 * (half + 1)],
                            aoutT[:, 2048 * dc + 128 * tt:2048 * dc + 128 * (tt + 1)],
                            w_big[:, 1024 * dc + 512 * half:1024 * dc + 512 * (half + 1)],
                            start=(dc == 0), stop=(dc == 7)),
                            "sT" if (half == 1 and dc == 7) else None) or tok
                t_wo_tok[tt] = tok

            emit_wo(0)
            for tt in range(TT):
                xt = xt0 if tt % 2 == 0 else xt1
                xnw = xnw0 if tt % 2 == 0 else xnw1
                phase[0] = "cffn_wo"
                Wt("vector", "sT", t_wo_tok[tt])
                W("vector", "sL")
                Wt("vector", "sS", store_tok.get(tt % 2))
                atok = None
                for half in range(2):
                    atok = I("vector", lambda e, half=half, xt=xt, xnw=xnw: e.tensor_add(
                        xnw[:, 512 * half:512 * (half + 1)],
                        P0[:, 512 * half:512 * (half + 1)], xt[:, 512 * half:512 * (half + 1)]), "sV")
                adds_tok[tt] = atok
                if tt + 1 < TT:
                    emit_wo(tt + 1)
                phase[0] = "cffn_ln"
                I("vector", lambda e, xnw=xnw: e.tensor_reduce(stat[:, 4:5], xnw[:], AX.X, ALU.max), "sV")
                I("vector", lambda e, xnw=xnw: e.tensor_reduce(stat[:, 5:6], xnw[:], AX.X, ALU.min), "sV")
                W("scalar", "sV")
                W("scalar", "sP")
                I("scalar", lambda e, xnw=xnw: e.activation(out=scr_y[:], in_=xnw[:], func=AF.Copy,
                                                            accum_out=stat[:, 0:1]), "sC")
                I("scalar", lambda e, xnw=xnw: e.activation(out=scr_y[:], in_=xnw[:], func=AF.Square,
                                                            accum_out=stat[:, 1:2]), "sC")
                W("vector", "sC")
                I("vector", lambda e: e.tensor_scalar(stat[:, 0:1], stat[:, 0:1], 1.0 / D, None, ALU.mult), "sV")
                I("vector", lambda e: e.tensor_scalar(stat[:, 1:2], stat[:, 1:2], 1.0 / D, None, ALU.mult), "sV")
                pad()
                I("vector", lambda e: e.tensor_tensor(stat[:, 2:3], stat[:, 0:1], stat[:, 0:1], ALU.mult), "sV")
                I("vector", lambda e: e.tensor_tensor(stat[:, 4:5], stat[:, 4:5], stat[:, 0:1], ALU.subtract), "sV")
                I("vector", lambda e: e.tensor_tensor(stat[:, 5:6], stat[:, 0:1], stat[:, 5:6], ALU.subtract), "sV")
                I("vector", lambda e: e.tensor_tensor(stat[:, 1:2], stat[:, 1:2], stat[:, 2:3], ALU.subtract), "sV")
                I("vector", lambda e: e.tensor_tensor(stat[:, 4:5], stat[:, 4:5], stat[:, 5:6], ALU.max), "sV")
                I("vector", lambda e: e.tensor_scalar(stat[:, 1:2], stat[:, 1:2], EPS, None, ALU.add), "sV")
                W("scalar", "sV")
                I("scalar", lambda e: e.activation(out=stat[:, 2:3], in_=stat[:, 1:2], func=AF.Sqrt), "sC")
                W("vector", "sC")
                recip_safe(stat[:, 3:4], stat[:, 2:3])
                pad(); pad()
                I("vector", lambda e: e.tensor_scalar(stat[:, 4:5], stat[:, 4:5], stat[:, 3:4], 1e-12, ALU.mult, ALU.max), "sV")
                recip_safe(stat[:, 6:7], stat[:, 4:5])
                I("vector", lambda e: e.tensor_scalar(stat[:, 8:9], stat[:, 4:5], wsc_col[:, 0:1], 1.0 / 448.0, ALU.mult, ALU.mult), "sV")
                pad()
                I("vector", lambda e: e.tensor_scalar(stat[:, 9:10], stat[:, 6:7], stat[:, 3:4], 448.0, ALU.mult, ALU.mult), "sV")
                pad(); pad()
                I("vector", lambda e: e.tensor_scalar(stat[:, 10:11], stat[:, 9:10], stat[:, 0:1], -1.0, ALU.mult, ALU.mult), "sV")
                pad(); pad()
                W("scalar", "sV")
                I("scalar", lambda e, xnw=xnw: e.activation(out=scr_y[:], in_=xnw[:], func=AF.Identity,
                                                            bias=stat[:, 10:11], scale=stat[:, 9:10]), "sC")
                phase[0] = "cffn_q"
                W("vector", "sC")
                I("vector", lambda e: e.tensor_scalar(si1, scr_y[:, :].bitcast(I32), 0x7F800000, None, ALU.bitwise_and), "sV")
                I("vector", lambda e: e.tensor_scalar(si1, si1, 0x3C800000, None, ALU.max), "sV")
                I("vector", lambda e: e.tensor_scalar(si2, si1, -1, 0x44000000, ALU.mult, ALU.add), "sV")
                I("vector", lambda e: e.tensor_scalar(si2, si2, 0x3C800000, None, ALU.add), "sV")
                I("vector", lambda e: e.tensor_tensor(scr_t[:], scr_y[:], si2.bitcast(F32), ALU.mult), "sV")
                I("vector", lambda e: e.tensor_copy(out=si3, in_=scr_t[:]), "sV")
                I("vector", lambda e: e.tensor_scalar(si2, si1, -(3 << 23), None, ALU.add), "sV")
                I("vector", lambda e: e.tensor_copy(out=scr_t[:], in_=si3), "sV")
                quant_tok[tt] = I("vector", lambda e: e.tensor_tensor(xn_stage[:], scr_t[:], si2.bitcast(F32), ALU.mult), "sV")
                phase[0] = "cffn_mm"
                Wt("tensor", "sV", quant_tok[tt])
                Wt("tensor", "sC", p4_free[0])
                t_tr = None
                for cdx in range(8):
                    t_tr = I("tensor", lambda e, cdx=cdx: e.transpose(
                        P4[:, 128 * cdx:128 * (cdx + 1)], xn_stage[:, 128 * cdx:128 * (cdx + 1)], ident[:]),
                        "sT" if cdx == 7 else None) or t_tr
                Wt("scalar", "sT", t_tr)
                p4_free[0] = I("scalar", lambda e, tt=tt: e.activation(
                    out=xn1T[:, :].rearrange("p (c t) -> p c t", c=8)[:, :, 128 * tt:128 * (tt + 1)],
                    in_=P4[:, :].rearrange("p (c t) -> p c t", c=8), func=AF.Copy), "sC")
                t_ffn = None
                Wt("tensor", "sC", p4_free[0])
                Wt("tensor", "sV", stt_tok.get(tt - 1))
                for half in range(2):
                    for kc in range(8):
                        t_ffn = I("tensor", lambda e, tt=tt, half=half, kc=kc: e.matmul(
                            P1[:, 512 * half:512 * (half + 1)],
                            xn1T[:, 2048 * kc + 128 * tt:2048 * kc + 128 * (tt + 1)],
                            wt_sb[:, 1024 * kc + 512 * half:1024 * kc + 512 * (half + 1)],
                            start=(kc == 0), stop=(kc == 7)),
                            "sT" if (half == 1 and kc == 7) else None) or t_ffn
                Wt("vector", "sT", t_ffn)
                stok = None
                for half in range(2):
                    stok = I("vector", lambda e, half=half: e.scalar_tensor_tensor(
                        scr_y[:, 512 * half:512 * (half + 1)],
                        P1[:, 512 * half:512 * (half + 1)], stat[:, 8:9], bias_b[:, 512 * half:512 * (half + 1)],
                        ALU.mult, ALU.add), "sV")
                stt_tok[tt] = stok
                Wt("gpsimd", "sV", stok)
                for half in range(2):
                    I("gpsimd", lambda e, half=half, xnw=xnw: e.tensor_add(
                        xnw[:, 512 * half:512 * (half + 1)],
                        xnw[:, 512 * half:512 * (half + 1)],
                        scr_y[:, 512 * half:512 * (half + 1)]), "sP")
                store_tok[tt % 2] = I("gpsimd", lambda e, sq=sq, tt=tt, xnw=xnw: e.dma_start(
                    out=y_out[sq, 128 * tt:128 * (tt + 1), :], in_=xnw[:]), "sS", 16)
            PBF[0] = ("sV", adds_tok[TT - 1])
            PBF[1] = ("sV", stt_tok[TT - 1])
        W("gpsimd", "sS")

        block = st.enter_context(nc.Block())

        @block.sync
        def _(eng):
            for fn in sc.ops["sync"]:
                fn(eng)

        @block.tensor
        def _(eng):
            for fn in sc.ops["tensor"]:
                fn(eng)

        @block.vector
        def _(eng):
            for fn in sc.ops["vector"]:
                fn(eng)

        @block.scalar
        def _(eng):
            for fn in sc.ops["scalar"]:
                fn(eng)

        @block.gpsimd
        def _(eng):
            for fn in sc.ops["gpsimd"]:
                fn(eng)

    return nc


def kernel(L0_attn_x, L0_attn_W_q, L0_attn_W_k, L0_attn_W_v, L0_attn_W_o,
           L0_attn_W_down_kv, L0_attn_W_up_k, L0_attn_W_up_v,
           L0_ffn_weight, L0_ffn_weight_scale, L0_ffn_bias):
    x = np.ascontiguousarray(np.asarray(L0_attn_x, dtype=np.float32))
    wq = np.ascontiguousarray(np.asarray(L0_attn_W_q, dtype=np.float32))
    wkT = np.ascontiguousarray(np.asarray(L0_attn_W_k, dtype=np.float32).T)
    wvT = np.ascontiguousarray(np.asarray(L0_attn_W_v, dtype=np.float32).T)
    wdn = np.ascontiguousarray(np.asarray(L0_attn_W_down_kv, dtype=np.float32))
    wupk = np.ascontiguousarray(np.asarray(L0_attn_W_up_k, dtype=np.float32))
    wupv = np.ascontiguousarray(np.asarray(L0_attn_W_up_v, dtype=np.float32))
    wo = np.ascontiguousarray(np.asarray(L0_attn_W_o, dtype=np.float32))
    wt8 = np.ascontiguousarray(np.asarray(L0_ffn_weight).view(np.uint8).T)
    wsc = np.asarray(L0_ffn_weight_scale, dtype=np.float32).reshape(1, 1)
    bias = np.asarray(L0_ffn_bias, dtype=np.float32).reshape(1, D)

    nc = build_nc()
    in_maps = []
    for c in range(8):
        in_maps.append({
            "x_in": np.ascontiguousarray(x[2 * c:2 * c + 2]),
            "wq_in": wq, "wkT_in": wkT, "wvT_in": wvT, "wdn_in": wdn,
            "wupk_in": wupk, "wupv_in": wupv, "wo_in": wo,
            "wt8_in": wt8, "wsc_in": wsc, "bias_in": bias,
        })
    res = run_bass_kernel_spmd(nc, in_maps, core_ids=list(range(8)))
    out = np.empty((16, S, D), dtype=np.float32)
    for c in range(8):
        out[2 * c:2 * c + 2] = res.results[c]["y_out"]
    return out


# version 08
# speedup vs baseline: 1.1179x; 1.1179x over previous
import math
from contextlib import ExitStack

import numpy as np

import concourse.bass as bass
import concourse.mybir as mybir
from concourse.bass_utils import run_bass_kernel_spmd

F32 = mybir.dt.float32
BF16 = mybir.dt.bfloat16
I32 = mybir.dt.int32
U8 = mybir.dt.uint8
AF = mybir.ActivationFunctionType
ALU = mybir.AluOpType
AX = mybir.AxisListType

S = 2048
D = 1024
H = 16
HD = 64
DL = 256
LAM = 0.01
NSEQ = 2
TT = 16
KT = 16
NQG = 4
QG = 512
EPS = 1e-5
VST = H * 65 + 1


class Sched:
    ENGS = ("sync", "tensor", "vector", "scalar", "gpsimd")

    def __init__(self):
        self.ops = {e: [] for e in self.ENGS}
        self.cnt = {}

    def emit(self, eng, fn):
        self.ops[eng].append(fn)

    def inc(self, sem_name, k=1):
        self.cnt[sem_name] = self.cnt.get(sem_name, 0) + k
        return self.cnt[sem_name]

    def val(self, sem_name):
        return self.cnt.get(sem_name, 0)


DEBUG = False
STRICT = False


def dbg_on(name):
    return DEBUG is True or (isinstance(DEBUG, (set, frozenset)) and name in DEBUG)

STRICT_PHASES = set()


def build_nc():
    nc = bass.Bass("TRN2", target_bir_lowering=False, detect_race_conditions=False)

    x_in = nc.dram_tensor("x_in", [NSEQ, S, D], F32, kind="ExternalInput").ap()
    wq_in = nc.dram_tensor("wq_in", [D, D], F32, kind="ExternalInput").ap()
    wkT_in = nc.dram_tensor("wkT_in", [D, D], F32, kind="ExternalInput").ap()
    wvT_in = nc.dram_tensor("wvT_in", [D, D], F32, kind="ExternalInput").ap()
    wdn_in = nc.dram_tensor("wdn_in", [D, DL], F32, kind="ExternalInput").ap()
    wupk_in = nc.dram_tensor("wupk_in", [DL, D], F32, kind="ExternalInput").ap()
    wupv_in = nc.dram_tensor("wupv_in", [DL, D], F32, kind="ExternalInput").ap()
    wo_in = nc.dram_tensor("wo_in", [D, D], F32, kind="ExternalInput").ap()
    wt8_in = nc.dram_tensor("wt8_in", [D, D], U8, kind="ExternalInput").ap()
    wsc_in = nc.dram_tensor("wsc_in", [1, 1], F32, kind="ExternalInput").ap()
    bias_in = nc.dram_tensor("bias_in", [1, D], F32, kind="ExternalInput").ap()
    y_out = nc.dram_tensor("y_out", [NSEQ, S, D], F32, kind="ExternalOutput").ap()
    if DEBUG:
        d_xn1T = nc.dram_tensor("d_xn1T", [128, 8 * 2048], BF16, kind="ExternalOutput").ap()
        d_latT = nc.dram_tensor("d_latT", [128, 2 * 2048], BF16, kind="ExternalOutput").ap()
        d_vrec = nc.dram_tensor("d_vrec", [128, KT * VST], BF16, kind="ExternalOutput").ap()
        d_qA = nc.dram_tensor("d_qA", [128, 2048], BF16, kind="ExternalOutput").ap()
        d_qB = nc.dram_tensor("d_qB", [128, 2048], BF16, kind="ExternalOutput").ap()
        d_krT = nc.dram_tensor("d_krT", [128, 2048], BF16, kind="ExternalOutput").ap()
        d_aoutT = nc.dram_tensor("d_aoutT", [128, 8 * 2048], BF16, kind="ExternalOutput").ap()
        d_wt = nc.dram_tensor("d_wt", [128, 8 * 1024], BF16, kind="ExternalOutput").ap()
        d_weff = nc.dram_tensor("d_weff", [128, 8 * 256], BF16, kind="ExternalOutput").ap()
        d_ab = nc.dram_tensor("d_ab", [128, 1024], BF16, kind="ExternalOutput").ap()
        d_pq = nc.dram_tensor("d_pq", [128, 1024], F32, kind="ExternalOutput").ap()
        d_ftab = nc.dram_tensor("d_ftab", [128, 128], F32, kind="ExternalOutput").ap()
        d_colA = nc.dram_tensor("d_colA", [128, KT], F32, kind="ExternalOutput").ap()
        d_ab16 = nc.dram_tensor("d_ab16", [128, KT * 128], BF16, kind="ExternalOutput").ap()
        d_pv = nc.dram_tensor("d_pv", [128, 1024], F32, kind="ExternalOutput").ap()

    sc = Sched()

    with ExitStack() as st:
        def sb(name, shape, dtype):
            return st.enter_context(nc.sbuf_tensor(name, shape, dtype))

        def ps(name, shape, dtype=F32):
            return st.enter_context(nc.psum_tensor(name, shape, dtype))

        def sem(name):
            return st.enter_context(nc.semaphore(name))

        xn1T = sb("xn1T", [128, 8 * 2048], BF16)
        latT = sb("latT", [128, 2 * 2048], BF16)
        vrec = sb("vrec", [128, KT * VST], BF16)
        aoutT = sb("aoutT", [128, 8 * 2048], BF16)
        w_big = sb("w_big", [128, 8 * 1024], BF16)
        wupk_sb = sb("wupk_sb", [128, 2 * 1024], BF16)
        wupv_sb = sb("wupv_sb", [128, 2 * 1024], BF16)
        weff_sb = sb("weff_sb", [128, 8 * 256], BF16)
        wt_sb = sb("wt_sb", [128, 8 * 1024], BF16)
        eneg = sb("eneg", [128, 2048], F32)
        epos = sb("epos", [128, 2048], F32)
        colA = sb("colA", [128, KT], F32)
        colB = sb("colB", [128, KT], F32)
        ftab = sb("ftab", [128, 128], F32)
        ident = sb("ident", [128, 128], BF16)
        ones_col = sb("ones_col", [128, 1], BF16)
        bias_b = sb("bias_b", [128, 1024], F32)
        wsc_col = sb("wsc_col", [128, 1], F32)
        qA = sb("qA", [128, 2048], BF16)
        qB = sb("qB", [128, 2048], BF16)
        krT = sb("krT", [128, 2048], BF16)
        xt0 = sb("xt0", [128, 1024], F32)
        xt1 = sb("xt1", [128, 1024], F32)
        xnw0 = sb("xnw0", [128, 1024], F32)
        xnw1 = sb("xnw1", [128, 1024], F32)
        xn_stage = sb("xn_stage", [128, 1024], BF16)
        scr_y = sb("scr_y", [128, 1024], F32)
        scr_t = sb("scr_t", [128, 1024], F32)
        onec = sb("onec", [128, 1], F32)
        stat = sb("stat", [128, 16], F32)
        iot_c = sb("iot_c", [128, 1], F32)
        rr = scr_y
        rb = scr_t
        wkvT = aoutT
        abuf0 = xnw0[:, :].bitcast(BF16)[:, 0:1024]
        abuf1 = xnw1[:, :].bitcast(BF16)[:, 0:1024]
        si1 = qA[:, :].bitcast(I32)
        si2 = qB[:, :].bitcast(I32)
        si3 = krT[:, :].bitcast(I32)
        i2db = si2[:, 0:128]
        f2d = krT[:, :].bitcast(F32)[:, 0:128]

        def pad():
            I("vector", lambda e: e.tensor_copy(out=stat[:, 14:15], in_=onec[:]), "sV")

        def recip_safe(out_ap, in_ap):
            I("vector", lambda e: e.tensor_copy(out=stat[:, 14:15], in_=onec[:]), "sV")
            I("vector", lambda e: e.tensor_copy(out=stat[:, 15:16], in_=onec[:]), "sV")
            I("vector", lambda e, o=out_ap, i=in_ap: e.reciprocal(o, i), "sV")
            I("vector", lambda e: e.tensor_copy(out=stat[:, 14:15], in_=onec[:]), "sV")
            I("vector", lambda e: e.tensor_copy(out=stat[:, 15:16], in_=onec[:]), "sV")

        def ones_row(srow):
            a = onec[srow:srow + 1, 0:1]
            return bass.AP(a.tensor, a.offset, [[int(a.ap[0][0]), 1], [0, 128]])

        P0 = ps("P0", [128, 1024])
        P1 = ps("P1", [128, 1024])
        P2 = ps("P2", [128, 512])
        P3 = ps("P3", [128, 512])
        P5 = ps("P5", [128, 512])
        P4 = ps("P4", [128, 1024], BF16)

        sL = sem("sL")
        sG2 = sem("sG2")
        sT = sem("sT")
        sV = sem("sV")
        sC = sem("sC")
        sG = sem("sG")
        sS = sem("sS")
        sP = sem("sP")
        SEMS = {"sL": sL, "sG2": sG2, "sT": sT, "sV": sV, "sC": sC, "sG": sG, "sS": sS, "sP": sP}

        def Wt(eng_name, sem_name, tok):
            if tok:
                s = SEMS[sem_name]
                sc.emit(eng_name, lambda e, s=s, t=tok: e.wait_ge(s, t))

        def W(eng_name, sem_name, v=None):
            target = sc.val(sem_name) if v is None else v
            s = SEMS[sem_name]
            if target > 0:
                sc.emit(eng_name, lambda e, s=s, t=target: e.wait_ge(s, t))

        phase = ["prep"]

        def I(eng_name, fn, sem_name=None, k=1):
            strict_here = (STRICT or phase[0] in STRICT_PHASES
                           or (phase[0] + ":" + eng_name) in STRICT_PHASES)
            if strict_here and eng_name in ("tensor", "vector", "scalar"):
                for other in ("sT", "sV", "sC"):
                    own = {"tensor": "sT", "vector": "sV", "scalar": "sC"}[eng_name]
                    if other != own:
                        W(eng_name, other)
                if sem_name is None:
                    sem_name = {"tensor": "sT", "vector": "sV", "scalar": "sC"}[eng_name]
            if sem_name is None:
                sc.emit(eng_name, fn)
                return None
            s = SEMS[sem_name]
            tok = sc.inc(sem_name, k)
            sc.emit(eng_name, lambda e, s=s, k=k, fn=fn: fn(e).then_inc(s, k))
            return tok

        I("sync", lambda e: e.dma_start(out=bias_b[0:1, :], in_=bias_in[:]), "sL", 16)
        I("sync", lambda e: e.dma_start(out=wsc_col[0:1, :], in_=wsc_in[:]), "sL", 16)
        for c in range(8):
            ta = scr_y if c % 2 == 0 else xt0
            tb = scr_t if c % 2 == 0 else xt1
            I("sync", lambda e, c=c, ta=ta: e.dma_start(out=ta[:], in_=wkT_in[128 * c:128 * (c + 1), :]), "sL", 16)
            I("sync", lambda e, c=c, tb=tb: e.dma_start(out=tb[:], in_=wvT_in[128 * c:128 * (c + 1), :]), "sL", 16)
            W("vector", "sL")
            I("vector", lambda e, c=c, ta=ta, tb=tb: e.tensor_add(
                wkvT[:, 1024 * c:1024 * (c + 1)], ta[:], tb[:]), "sV")
            W("sync", "sV")

        I("gpsimd", lambda e: e.dma_start(out=wupk_sb[:, :].rearrange("p (c n) -> p c n", n=1024), in_=wupk_in.rearrange("(c p) n -> p c n", p=128)), "sG2", 16)
        I("gpsimd", lambda e: e.dma_start(out=wupv_sb[:, :].rearrange("p (c n) -> p c n", n=1024), in_=wupv_in.rearrange("(c p) n -> p c n", p=128)), "sG2", 16)
        I("gpsimd", lambda e: e.dma_start(out=abuf0.rearrange("p (c n) -> p c n", n=256), in_=wdn_in[0:512, :].rearrange("(c p) n -> p c n", p=128)), "sG2", 16)
        I("gpsimd", lambda e: e.dma_start(out=abuf1.rearrange("p (c n) -> p c n", n=256), in_=wdn_in[512:1024, :].rearrange("(c p) n -> p c n", p=128)), "sG2", 16)

        I("gpsimd", lambda e: e.iota(eneg[0:1, :], [[1, 2048]], base=0, channel_multiplier=0,
                                     allow_small_or_imprecise_dtypes=True), "sG")
        I("gpsimd", lambda e: e.iota(iot_c[:], [[1, 1]], base=0, channel_multiplier=1,
                                     allow_small_or_imprecise_dtypes=True), "sG")
        I("gpsimd", lambda e: e.iota(i2db, [[1, 128]], base=0, channel_multiplier=-1,
                                     allow_small_or_imprecise_dtypes=True), "sG")

        W("vector", "sG")
        I("vector", lambda e: e.tensor_scalar(si1[:, 0:128], i2db, 0, None, ALU.is_equal), "sV")
        I("vector", lambda e: e.tensor_copy(out=ident[:], in_=si1[:, 0:128]), "sV")
        I("vector", lambda e: e.tensor_copy(out=f2d, in_=i2db), "sV")
        I("vector", lambda e: e.tensor_scalar(f2d, f2d, 0.0, None, ALU.min), "sV")
        I("vector", lambda e: e.tensor_copy(out=epos[0:1, :], in_=eneg[0:1, :]), "sV")
        W("scalar", "sV")
        I("scalar", lambda e: e.activation(out=ftab[:], in_=f2d[:], func=AF.Exp, scale=2.0 * LAM), "sC")
        I("scalar", lambda e: e.activation(out=eneg[0:1, :], in_=eneg[0:1, :], func=AF.Exp, scale=-LAM), "sC")
        I("scalar", lambda e: e.activation(out=epos[0:1, :], in_=epos[0:1, :], func=AF.Exp, scale=LAM), "sC")
        I("scalar", lambda e: e.activation(out=stat[:, 0:1], in_=iot_c[:], func=AF.Exp, scale=LAM), "sC")
        I("scalar", lambda e: e.activation(out=stat[:, 1:2], in_=iot_c[:], func=AF.Exp, scale=-LAM), "sC")
        W("vector", "sC")
        I("vector", lambda e: e.tensor_scalar(eneg[0:1, :], eneg[0:1, :], 1.0 / 32.0, None, ALU.mult), "sV")
        I("vector", lambda e: e.tensor_scalar(epos[0:1, :], epos[0:1, :], 1.0 / 32.0, None, ALU.mult), "sV")
        for kt in range(KT):
            I("vector", lambda e, kt=kt: e.tensor_scalar(colA[:, kt:kt + 1], stat[:, 0:1],
                                                         float(math.exp(LAM * 128 * kt)), None, ALU.mult), "sV")
            I("vector", lambda e, kt=kt: e.tensor_scalar(colB[:, kt:kt + 1], stat[:, 1:2],
                                                         float(math.exp(-LAM * 128 * kt)), None, ALU.mult), "sV")
        I("gpsimd", lambda e: e.memset(onec[:], 1.0), "sG")
        I("gpsimd", lambda e: e.memset(ones_col[:], 1.0), "sG")
        W("tensor", "sV")
        W("tensor", "sG")
        W("tensor", "sL", 32)
        bcast_jobs = []
        for j in range(4):
            bcast_jobs.append((eneg, 512 * j, 512))
            bcast_jobs.append((epos, 512 * j, 512))
        bcast_jobs.append((bias_b, 0, 512))
        bcast_jobs.append((bias_b, 512, 512))
        bcast_jobs.append((wsc_col, 0, 1))
        for (dst, off, wid) in bcast_jobs:
            I("tensor", lambda e, dst=dst, off=off, wid=wid: e.matmul(
                P5[:, 0:wid], ones_row(0), dst[0:1, off:off + wid],
                start=True, stop=True), "sT")
            W("vector", "sT")
            I("vector", lambda e, dst=dst, off=off, wid=wid: e.tensor_copy(
                out=dst[:, off:off + wid], in_=P5[:, 0:wid]), "sV")
            W("tensor", "sV")
        I("gpsimd", lambda e: e.memset(
            vrec[:, :].rearrange("p (k q) -> p k q", q=VST)[:, :, 0:H * 65].rearrange(
                "p k (h j) -> p k h j", j=65)[:, :, :, 64:65], 1.0), "sG")

        if DEBUG and dbg_on("prepb"):
            W("gpsimd", "sV")
            I("gpsimd", lambda e: e.dma_start(out=d_pq[:, 0:1], in_=wsc_col[:, :], allow_slow_non_contiguous=True), "sS", 16)
            I("gpsimd", lambda e: e.dma_start(out=d_pq[:, 1:2], in_=eneg[:, 777:778], allow_slow_non_contiguous=True), "sS", 16)
            I("gpsimd", lambda e: e.dma_start(out=d_pq[:, 2:3], in_=epos[:, 777:778], allow_slow_non_contiguous=True), "sS", 16)
            I("gpsimd", lambda e: e.dma_start(out=d_pq[:, 3:4], in_=bias_b[:, 333:334], allow_slow_non_contiguous=True), "sS", 16)
        scr_i4 = scr_t[:, :].bitcast(I32)
        for c in range(8):
            cs = slice(1024 * c, 1024 * (c + 1))
            W("gpsimd", "sV")
            I("gpsimd", lambda e, c=c: e.dma_start(
                out=scr_y[:], in_=wt8_in.rearrange("(c p) n -> c p n", p=128)[c]), "sG2", 16)
            W("vector", "sG2")
            I("vector", lambda e: e.tensor_copy(out=si1, in_=scr_y[:]), "sV")
            I("vector", lambda e: e.tensor_scalar(si2, si1, 3, 15, ALU.logical_shift_right, ALU.bitwise_and), "sV")
            I("vector", lambda e: e.tensor_scalar(si3, si1, 7, None, ALU.bitwise_and), "sV")
            I("vector", lambda e: e.tensor_scalar(si3, si3, 8, None, ALU.add), "sV")
            I("vector", lambda e: e.tensor_scalar(si1, si1, 7, None, ALU.logical_shift_right), "sV")
            I("vector", lambda e: e.tensor_scalar(si1, si1, -2, 1, ALU.mult, ALU.add), "sV")
            I("vector", lambda e: e.tensor_tensor(scr_i4, si3, si2, ALU.arith_shift_left), "sV")
            I("vector", lambda e: e.tensor_scalar(si3, si3, 2, -16, ALU.mult, ALU.add), "sV")
            I("vector", lambda e: e.tensor_scalar(si2, si2, 0, None, ALU.is_equal), "sV")
            I("vector", lambda e: e.tensor_tensor(si3, si3, si2, ALU.mult), "sV")
            I("vector", lambda e: e.tensor_scalar(si2, si2, -1, 1, ALU.mult, ALU.add), "sV")
            I("vector", lambda e: e.tensor_tensor(scr_i4, scr_i4, si2, ALU.mult), "sV")
            I("vector", lambda e: e.tensor_tensor(scr_i4, scr_i4, si3, ALU.add), "sV")
            I("vector", lambda e: e.tensor_tensor(scr_i4, scr_i4, si1, ALU.mult), "sV")
            I("vector", lambda e: e.tensor_copy(out=scr_t[:], in_=scr_i4), "sV")
            I("vector", lambda e, cs=cs: e.tensor_scalar(wt_sb[:, cs], scr_t[:], float(2.0 ** -10), None, ALU.mult), "sV")

        W("tensor", "sG2", 64)
        W("tensor", "sV")
        for dt_ in range(8):
            for dd in range(8):
                wdn_src = abuf0 if dd < 4 else abuf1
                I("tensor", lambda e, dt_=dt_, dd=dd, wdn_src=wdn_src: e.matmul(
                    P5[:, 0:256],
                    wkvT[:, 1024 * dd + 128 * dt_:1024 * dd + 128 * (dt_ + 1)],
                    wdn_src[:, 256 * (dd % 4):256 * (dd % 4 + 1)],
                    start=(dd == 0), stop=(dd == 7)), "sT" if dd == 7 else None)
            W("vector", "sT")
            I("vector", lambda e, dt_=dt_: e.tensor_copy(out=weff_sb[:, 256 * dt_:256 * (dt_ + 1)], in_=P5[:, 0:256]), "sV")
            W("tensor", "sV")

        PBF = {0: None, 1: None}

        def WB(eng_name, bid):
            ent = PBF[bid]
            if ent:
                Wt(eng_name, ent[0], ent[1])

        for sq in range(NSEQ):
            phase[0] = "A"
            for tt in range(TT):
                xt = xt0 if tt % 2 == 0 else xt1
                W("sync", "sV")
                W("sync", "sC")
                I("sync", lambda e, sq=sq, tt=tt, xt=xt: e.dma_start(
                    out=xt[:], in_=x_in[sq, 128 * tt:128 * (tt + 1), :]), "sL", 16)
                W("vector", "sL")
                I("vector", lambda e, xt=xt: e.reduce_sum(stat[:, 0:1], xt[:], AX.X), "sV")
                W("scalar", "sL")
                W("scalar", "sV")
                I("scalar", lambda e, xt=xt: e.activation(out=scr_y[:], in_=xt[:], func=AF.Square,
                                                          accum_out=stat[:, 1:2]), "sC")
                W("vector", "sC")
                I("vector", lambda e: e.tensor_scalar(stat[:, 0:1], stat[:, 0:1], 1.0 / D, None, ALU.mult), "sV")
                I("vector", lambda e: e.tensor_scalar(stat[:, 1:2], stat[:, 1:2], 1.0 / D, None, ALU.mult), "sV")
                pad()
                I("vector", lambda e: e.tensor_tensor(stat[:, 2:3], stat[:, 0:1], stat[:, 0:1], ALU.mult), "sV")
                pad(); pad()
                I("vector", lambda e: e.tensor_tensor(stat[:, 1:2], stat[:, 1:2], stat[:, 2:3], ALU.subtract), "sV")
                pad(); pad()
                I("vector", lambda e: e.tensor_scalar(stat[:, 1:2], stat[:, 1:2], EPS, None, ALU.add), "sV")
                W("scalar", "sV")
                I("scalar", lambda e: e.activation(out=stat[:, 2:3], in_=stat[:, 1:2], func=AF.Sqrt), "sC")
                W("vector", "sC")
                recip_safe(stat[:, 3:4], stat[:, 2:3])
                I("vector", lambda e, xt=xt: e.tensor_scalar(
                    xn_stage[:], xt[:], stat[:, 0:1], stat[:, 3:4], ALU.subtract, ALU.mult), "sV")
                W("tensor", "sV")
                W("tensor", "sC")
                for cdx in range(8):
                    I("tensor", lambda e, cdx=cdx: e.transpose(
                        P4[:, 128 * cdx:128 * (cdx + 1)], xn_stage[:, 128 * cdx:128 * (cdx + 1)], ident[:]),
                        "sT" if cdx == 7 else None)
                W("vector", "sT")
                I("vector", lambda e, tt=tt: e.tensor_copy(
                    out=xn1T[:, :].rearrange("p (c t) -> p c t", c=8)[:, :, 128 * tt:128 * (tt + 1)],
                    in_=P4[:, :].rearrange("p (c t) -> p c t", c=8)), "sV")
                W("tensor", "sV")

            if DEBUG and sq == 0 and dbg_on("xn1T"):
                W("gpsimd", "sV")
                I("gpsimd", lambda e: e.dma_start(out=d_xn1T[:], in_=xn1T[:, :]), "sS", 16)
            W("gpsimd", "sT")
            I("gpsimd", lambda e: e.dma_start(out=w_big[:, :].rearrange("p (c n) -> p c n", n=1024), in_=wq_in.rearrange("(c p) n -> p c n", p=128)), "sG2", 16)

            phase[0] = "B"
            W("tensor", "sG")
            for lc in range(2):
                for tq in range(4):
                    bid = (lc * 4 + tq) % 2
                    P = P0 if bid == 0 else P1
                    WB("tensor", bid)
                    mm_tok = None
                    for dc in range(8):
                        mm_tok = I("tensor", lambda e, lc=lc, tq=tq, dc=dc, P=P: e.matmul(
                            P[:, 0:512],
                            weff_sb[:, 256 * dc + 128 * lc:256 * dc + 128 * (lc + 1)],
                            xn1T[:, 2048 * dc + 512 * tq:2048 * dc + 512 * (tq + 1)],
                            start=(dc == 0), stop=(dc == 7)), "sT" if dc == 7 else None) or mm_tok
                    Wt("vector", "sT", mm_tok)
                    PBF[bid] = ("sV", I("vector", lambda e, lc=lc, tq=tq, P=P: e.tensor_copy(
                        out=latT[:, 2048 * lc + 512 * tq:2048 * lc + 512 * (tq + 1)], in_=P[:, 0:512]), "sV"))
            for tt in range(TT):
                for half in range(2):
                    bid = (tt * 2 + half) % 2
                    P = P0 if bid == 0 else P1
                    WB("tensor", bid)
                    mm_tok = None
                    for lc in range(2):
                        mm_tok = I("tensor", lambda e, tt=tt, half=half, lc=lc, P=P: e.matmul(
                            P[:, 0:512],
                            latT[:, 2048 * lc + 128 * tt:2048 * lc + 128 * (tt + 1)],
                            wupv_sb[:, 1024 * lc + 512 * half:1024 * lc + 512 * (half + 1)],
                            start=(lc == 0), stop=(lc == 1)), "sT" if lc == 1 else None) or mm_tok
                    Wt("vector", "sT", mm_tok)
                    PBF[bid] = ("sV", I("vector", lambda e, tt=tt, half=half, P=P: e.tensor_copy(
                        out=vrec[:, :].rearrange("p (k q) -> p k q", q=VST)[
                            :, tt, 65 * 8 * half:65 * 8 * (half + 1)].rearrange(
                            "p (h j) -> p h j", j=65)[:, :, 0:64],
                        in_=P[:, 0:512].rearrange("p (h j) -> p h j", j=64)), "sV"))

            if DEBUG and sq == 0 and dbg_on("latT"):
                W("gpsimd", "sV")
                I("gpsimd", lambda e: e.dma_start(out=d_latT[:], in_=latT[:, :]), "sS", 16)
                I("gpsimd", lambda e: e.dma_start(out=d_vrec[:], in_=vrec[:, :]), "sS", 16)
                I("gpsimd", lambda e: e.dma_start(out=d_weff[:], in_=weff_sb[:, :]), "sS", 16)
                I("gpsimd", lambda e: e.dma_start(out=d_wt[:], in_=wt_sb[:, :]), "sS", 16)
            phase[0] = "attn"
            att_prev_pv = {}
            att_last_qk = {0: None}
            pending_norm = [None]
            att_prev_evac = {0: sc.val("sV")}
            att_prev_p5 = {0: sc.val("sV")}
            W("tensor", "sG2")
            W("scalar", "sS")
            W("tensor", "sS")
            W("vector", "sS")
            for hp in range(8):
                prev_qk = att_last_qk[0]
                for tq in range(4):
                    bid = tq % 2
                    P = P0 if bid == 0 else P1
                    WB("tensor", bid)
                    mm_tok = None
                    for lc in range(2):
                        mm_tok = I("tensor", lambda e, hp=hp, tq=tq, lc=lc, P=P: e.matmul(
                            P[:, 0:512],
                            wupk_sb[:, 1024 * lc + 128 * hp:1024 * lc + 128 * (hp + 1)],
                            latT[:, 2048 * lc + 512 * tq:2048 * lc + 512 * (tq + 1)],
                            start=(lc == 0), stop=(lc == 1)), "sT" if lc == 1 else None) or mm_tok
                    Wt("vector", "sT", mm_tok)
                    Wt("vector", "sT", prev_qk)
                    PBF[bid] = ("sV", I("vector", lambda e, tq=tq, P=P: e.tensor_copy(
                        out=krT[:, 512 * tq:512 * (tq + 1)], in_=P[:, 0:512]), "sV"))
                for tq in range(4):
                    bid = tq % 2
                    P = P0 if bid == 0 else P1
                    WB("tensor", bid)
                    mm_tok = None
                    for dc in range(8):
                        mm_tok = I("tensor", lambda e, hp=hp, tq=tq, dc=dc, P=P: e.matmul(
                            P[:, 0:512],
                            w_big[:, 1024 * dc + 128 * hp:1024 * dc + 128 * (hp + 1)],
                            xn1T[:, 2048 * dc + 512 * tq:2048 * dc + 512 * (tq + 1)],
                            start=(dc == 0), stop=(dc == 7)), "sT" if dc == 7 else None) or mm_tok
                    Wt("vector", "sT", mm_tok)
                    Wt("vector", "sT", prev_qk)
                    I("vector", lambda e, tq=tq, P=P: e.tensor_mul(
                        qA[:, 512 * tq:512 * (tq + 1)], P[:, 0:512], eneg[:, 512 * tq:512 * (tq + 1)]), "sV")
                    PBF[bid] = ("sV", I("vector", lambda e, tq=tq, P=P: e.tensor_mul(
                        qB[:, 512 * tq:512 * (tq + 1)], P[:, 0:512], epos[:, 512 * tq:512 * (tq + 1)]), "sV"))
                pp_tok = sc.val("sV")

                if DEBUG and sq == 0 and hp == 0 and dbg_on("qA"):
                    W("gpsimd", "sV")
                    I("gpsimd", lambda e: e.dma_start(out=d_qA[:], in_=qA[:, :]), "sS", 16)
                    I("gpsimd", lambda e: e.dma_start(out=d_qB[:], in_=qB[:, :]), "sS", 16)
                    I("gpsimd", lambda e: e.dma_start(out=d_krT[:], in_=krT[:, :]), "sS", 16)
                    W("tensor", "sS")
                for qg in range(NQG):
                    qgs = QG * qg
                    t_qk = {}
                    c_exp = {}
                    t_pv = {}
                    first_qk = (qg == 0)

                    def emit_qk(kt, qgs=qgs, first_qk=first_qk):
                        k0 = 128 * kt
                        Pq = P0 if kt % 2 == 0 else P1
                        ab = abuf0 if kt % 2 == 0 else abuf1
                        regs = []
                        if k0 + 128 <= qgs:
                            regs.append(("A", 0, QG))
                        elif k0 >= qgs + QG:
                            regs.append(("B", 0, QG))
                        else:
                            rel = k0 - qgs
                            if rel > 0:
                                regs.append(("B", 0, rel))
                            regs.append(("Ad", rel, QG - rel))
                        WB("tensor", kt % 2)
                        if kt == 0 and first_qk:
                            Wt("tensor", "sV", pp_tok)
                        tok = None
                        for (side, rel, wid) in regs:
                            src = qA if side in ("A", "Ad") else qB
                            I("tensor", lambda e, kt=kt, rel=rel, wid=wid, src=src, Pq=Pq, qgs=qgs: e.matmul(
                                Pq[:, rel:rel + wid],
                                krT[0:64, 128 * kt:128 * (kt + 1)],
                                src[0:64, qgs + rel:qgs + rel + wid],
                                start=True, stop=True), None)
                            tok = I("tensor", lambda e, kt=kt, rel=rel, wid=wid, src=src, Pq=Pq, qgs=qgs: e.matmul(
                                Pq[:, 512 + rel:512 + rel + wid],
                                krT[64:128, 128 * kt:128 * (kt + 1)],
                                src[64:128, qgs + rel:qgs + rel + wid],
                                start=True, stop=True, tile_position=(64, 0)), "sT")
                        t_qk[kt] = tok
                        att_last_qk[0] = tok
                        has_diag = any(s == "Ad" for (s, _, _) in regs)
                        v_diag = None
                        if has_diag:
                            rel = k0 - qgs
                            Wt("vector", "sT", t_qk[kt])
                            I("vector", lambda e, rel=rel, Pq=Pq: e.tensor_mul(
                                Pq[:, rel:rel + 128], Pq[:, rel:rel + 128], ftab[:]), "sV")
                            v_diag = I("vector", lambda e, rel=rel, Pq=Pq: e.tensor_mul(
                                Pq[:, 512 + rel:512 + rel + 128], Pq[:, 512 + rel:512 + rel + 128], ftab[:]), "sV")
                        Wt("scalar", "sT", t_qk[kt])
                        Wt("scalar", "sT", att_prev_pv.get(kt % 2))
                        if has_diag:
                            Wt("scalar", "sV", v_diag)
                        if len(regs) == 1:
                            (side, rel, wid) = regs[0]
                            col = colA if side in ("A", "Ad") else colB
                            tok = I("scalar", lambda e, kt=kt, col=col, Pq=Pq, ab=ab: e.activation(
                                out=ab[:, 0:1024], in_=Pq[:, 0:1024],
                                func=AF.Exp, scale=col[:, kt:kt + 1]), "sC")
                        else:
                            for hh in range(2):
                                for (side, rel, wid) in regs:
                                    col = colA if side in ("A", "Ad") else colB
                                    tok = I("scalar", lambda e, rel=rel, wid=wid, kt=kt, col=col, Pq=Pq, ab=ab, hh=hh: e.activation(
                                        out=ab[:, 512 * hh + rel:512 * hh + rel + wid],
                                        in_=Pq[:, 512 * hh + rel:512 * hh + rel + wid],
                                        func=AF.Exp, scale=col[:, kt:kt + 1]), "sC")
                        c_exp[kt] = tok
                        PBF[kt % 2] = ("sC", tok)

                    def emit_pv(kt, hp=hp):
                        ab = abuf0 if kt % 2 == 0 else abuf1
                        Wt("tensor", "sC", c_exp[kt])
                        if kt == 0:
                            Wt("tensor", "sV", att_prev_evac[0])
                        for hh in range(2):
                            Pv = P2 if hh == 0 else P3
                            ha = 2 * hp + hh
                            tok = I("tensor", lambda e, ha=ha, kt=kt, Pv=Pv, ab=ab, hh=hh: e.matmul(
                                Pv[0:65, :],
                                vrec[:, VST * kt + 65 * ha:VST * kt + 65 * ha + 65],
                                ab[:, 512 * hh:512 * (hh + 1)],
                                start=(kt == 0), stop=(kt == KT - 1)),
                                "sT" if hh == 1 else None)
                        t_pv[kt] = tok
                        att_prev_pv[kt % 2] = tok

                    emit_qk(0)
                    if pending_norm[0] is not None:
                        pending_norm[0]()
                        pending_norm[0] = None
                    for kt in range(1, KT):
                        emit_qk(kt)
                        emit_pv(kt - 1)
                    emit_pv(KT - 1)
                    Wt("vector", "sT", t_pv[KT - 1])
                    rtoks = []
                    for hh in range(2):
                        Pv = P2 if hh == 0 else P3
                        recip_safe(rr[64:65, 512 * hh:512 * (hh + 1)], Pv[64:65, :])
                        rtoks.append(sc.val("sV"))

                    def emit_norm(hp=hp, qgs=qgs, rtoks=rtoks):
                        for hh in range(2):
                            Pv = P2 if hh == 0 else P3
                            hr = slice(64 * hh, 64 * (hh + 1))
                            Wt("tensor", "sV", rtoks[hh])
                            Wt("tensor", "sV", att_prev_p5[0])
                            btok = I("tensor", lambda e, hh=hh: e.matmul(
                                P5[:, 0:512], ones_row(64),
                                rr[64:65, 512 * hh:512 * (hh + 1)],
                                start=True, stop=True), "sT")
                            Wt("vector", "sT", btok)
                            att_prev_p5[0] = I("vector", lambda e, hh=hh: e.tensor_copy(
                                out=rb[0:64, 512 * hh:512 * (hh + 1)], in_=P5[0:64, 0:512]), "sV")
                            att_prev_evac[0] = I("vector", lambda e, Pv=Pv, hp=hp, qgs=qgs, hh=hh, hr=hr: e.tensor_mul(
                                aoutT[hr, 2048 * hp + qgs:2048 * hp + qgs + 512],
                                Pv[0:64, :], rb[0:64, 512 * hh:512 * (hh + 1)]), "sV")
                    pending_norm[0] = emit_norm
            if pending_norm[0] is not None:
                pending_norm[0]()
                pending_norm[0] = None
            if DEBUG and sq == 0 and dbg_on("aoutT"):
                W("gpsimd", "sV")
                I("gpsimd", lambda e: e.dma_start(out=d_aoutT[:], in_=aoutT[:, :]), "sS", 16)
            W("gpsimd", "sT")
            I("gpsimd", lambda e: e.dma_start(out=w_big[:, :].rearrange("p (c n) -> p c n", n=1024), in_=wo_in.rearrange("(c p) n -> p c n", p=128)), "sG2", 16)

            phase[0] = "cffn_wo"
            W("tensor", "sG2")
            if sq == 0:
                store_tok = {}
            p4_free = {0: None}
            adds_tok = {}
            stt_tok = {}
            quant_tok = {}
            t_wo_tok = {}

            def emit_wo(tt):
                xt = xt0 if tt % 2 == 0 else xt1
                W("sync", "sV")
                I("sync", lambda e, sq=sq, tt=tt, xt=xt: e.dma_start(
                    out=xt[:], in_=x_in[sq, 128 * tt:128 * (tt + 1), :]), "sL", 16)
                if tt == 0:
                    WB("tensor", 0)
                else:
                    Wt("tensor", "sV", adds_tok.get(tt - 1))
                tok = None
                for half in range(2):
                    for dc in range(8):
                        tok = I("tensor", lambda e, tt=tt, half=half, dc=dc: e.matmul(
                            P0[:, 512 * half:512 * (half + 1)],
                            aoutT[:, 2048 * dc + 128 * tt:2048 * dc + 128 * (tt + 1)],
                            w_big[:, 1024 * dc + 512 * half:1024 * dc + 512 * (half + 1)],
                            start=(dc == 0), stop=(dc == 7)),
                            "sT" if (half == 1 and dc == 7) else None) or tok
                t_wo_tok[tt] = tok

            emit_wo(0)
            for tt in range(TT):
                xt = xt0 if tt % 2 == 0 else xt1
                xnw = xnw0 if tt % 2 == 0 else xnw1
                phase[0] = "cffn_wo"
                Wt("vector", "sT", t_wo_tok[tt])
                W("vector", "sL")
                Wt("vector", "sS", store_tok.get(tt % 2))
                atok = None
                for half in range(2):
                    atok = I("vector", lambda e, half=half, xt=xt, xnw=xnw: e.tensor_add(
                        xnw[:, 512 * half:512 * (half + 1)],
                        P0[:, 512 * half:512 * (half + 1)], xt[:, 512 * half:512 * (half + 1)]), "sV")
                adds_tok[tt] = atok
                if tt + 1 < TT:
                    emit_wo(tt + 1)
                phase[0] = "cffn_ln"
                I("vector", lambda e, xnw=xnw: e.tensor_reduce(stat[:, 4:5], xnw[:], AX.X, ALU.max), "sV")
                I("vector", lambda e, xnw=xnw: e.tensor_reduce(stat[:, 5:6], xnw[:], AX.X, ALU.min), "sV")
                W("scalar", "sV")
                W("scalar", "sP")
                I("scalar", lambda e, xnw=xnw: e.activation(out=scr_y[:], in_=xnw[:], func=AF.Copy,
                                                            accum_out=stat[:, 0:1]), "sC")
                I("scalar", lambda e, xnw=xnw: e.activation(out=scr_y[:], in_=xnw[:], func=AF.Square,
                                                            accum_out=stat[:, 1:2]), "sC")
                W("vector", "sC")
                I("vector", lambda e: e.tensor_scalar(stat[:, 0:1], stat[:, 0:1], 1.0 / D, None, ALU.mult), "sV")
                I("vector", lambda e: e.tensor_scalar(stat[:, 1:2], stat[:, 1:2], 1.0 / D, None, ALU.mult), "sV")
                pad()
                I("vector", lambda e: e.tensor_tensor(stat[:, 2:3], stat[:, 0:1], stat[:, 0:1], ALU.mult), "sV")
                I("vector", lambda e: e.tensor_tensor(stat[:, 4:5], stat[:, 4:5], stat[:, 0:1], ALU.subtract), "sV")
                I("vector", lambda e: e.tensor_tensor(stat[:, 5:6], stat[:, 0:1], stat[:, 5:6], ALU.subtract), "sV")
                I("vector", lambda e: e.tensor_tensor(stat[:, 1:2], stat[:, 1:2], stat[:, 2:3], ALU.subtract), "sV")
                I("vector", lambda e: e.tensor_tensor(stat[:, 4:5], stat[:, 4:5], stat[:, 5:6], ALU.max), "sV")
                I("vector", lambda e: e.tensor_scalar(stat[:, 1:2], stat[:, 1:2], EPS, None, ALU.add), "sV")
                W("scalar", "sV")
                I("scalar", lambda e: e.activation(out=stat[:, 2:3], in_=stat[:, 1:2], func=AF.Sqrt), "sC")
                W("vector", "sC")
                recip_safe(stat[:, 3:4], stat[:, 2:3])
                pad(); pad()
                I("vector", lambda e: e.tensor_scalar(stat[:, 4:5], stat[:, 4:5], stat[:, 3:4], 1e-12, ALU.mult, ALU.max), "sV")
                recip_safe(stat[:, 6:7], stat[:, 4:5])
                I("vector", lambda e: e.tensor_scalar(stat[:, 8:9], stat[:, 4:5], wsc_col[:, 0:1], 1.0 / 448.0, ALU.mult, ALU.mult), "sV")
                pad()
                I("vector", lambda e: e.tensor_scalar(stat[:, 9:10], stat[:, 6:7], stat[:, 3:4], 448.0, ALU.mult, ALU.mult), "sV")
                pad(); pad()
                I("vector", lambda e: e.tensor_scalar(stat[:, 10:11], stat[:, 9:10], stat[:, 0:1], -1.0, ALU.mult, ALU.mult), "sV")
                pad(); pad()
                W("scalar", "sV")
                I("scalar", lambda e, xnw=xnw: e.activation(out=scr_y[:], in_=xnw[:], func=AF.Identity,
                                                            bias=stat[:, 10:11], scale=stat[:, 9:10]), "sC")
                phase[0] = "cffn_q"
                W("vector", "sC")
                I("vector", lambda e: e.tensor_scalar(si1, scr_y[:, :].bitcast(I32), 0x7F800000, None, ALU.bitwise_and), "sV")
                I("vector", lambda e: e.tensor_scalar(si1, si1, 0x3C800000, None, ALU.max), "sV")
                I("vector", lambda e: e.tensor_scalar(si2, si1, -1, 0x44000000, ALU.mult, ALU.add), "sV")
                I("vector", lambda e: e.tensor_scalar(si2, si2, 0x3C800000, None, ALU.add), "sV")
                I("vector", lambda e: e.tensor_tensor(scr_t[:], scr_y[:], si2.bitcast(F32), ALU.mult), "sV")
                I("vector", lambda e: e.tensor_copy(out=si3, in_=scr_t[:]), "sV")
                I("vector", lambda e: e.tensor_scalar(si2, si1, -(3 << 23), None, ALU.add), "sV")
                I("vector", lambda e: e.tensor_copy(out=scr_t[:], in_=si3), "sV")
                quant_tok[tt] = I("vector", lambda e: e.tensor_tensor(xn_stage[:], scr_t[:], si2.bitcast(F32), ALU.mult), "sV")
                phase[0] = "cffn_mm"
                Wt("tensor", "sV", quant_tok[tt])
                Wt("tensor", "sC", p4_free[0])
                t_tr = None
                for cdx in range(8):
                    t_tr = I("tensor", lambda e, cdx=cdx: e.transpose(
                        P4[:, 128 * cdx:128 * (cdx + 1)], xn_stage[:, 128 * cdx:128 * (cdx + 1)], ident[:]),
                        "sT" if cdx == 7 else None) or t_tr
                Wt("scalar", "sT", t_tr)
                p4_free[0] = I("scalar", lambda e, tt=tt: e.activation(
                    out=xn1T[:, :].rearrange("p (c t) -> p c t", c=8)[:, :, 128 * tt:128 * (tt + 1)],
                    in_=P4[:, :].rearrange("p (c t) -> p c t", c=8), func=AF.Copy), "sC")
                t_ffn = None
                Wt("tensor", "sC", p4_free[0])
                Wt("tensor", "sV", stt_tok.get(tt - 1))
                for half in range(2):
                    for kc in range(8):
                        t_ffn = I("tensor", lambda e, tt=tt, half=half, kc=kc: e.matmul(
                            P1[:, 512 * half:512 * (half + 1)],
                            xn1T[:, 2048 * kc + 128 * tt:2048 * kc + 128 * (tt + 1)],
                            wt_sb[:, 1024 * kc + 512 * half:1024 * kc + 512 * (half + 1)],
                            start=(kc == 0), stop=(kc == 7)),
                            "sT" if (half == 1 and kc == 7) else None) or t_ffn
                Wt("vector", "sT", t_ffn)
                stok = None
                for half in range(2):
                    stok = I("vector", lambda e, half=half: e.scalar_tensor_tensor(
                        scr_y[:, 512 * half:512 * (half + 1)],
                        P1[:, 512 * half:512 * (half + 1)], stat[:, 8:9], bias_b[:, 512 * half:512 * (half + 1)],
                        ALU.mult, ALU.add), "sV")
                stt_tok[tt] = stok
                Wt("gpsimd", "sV", stok)
                for half in range(2):
                    I("gpsimd", lambda e, half=half, xnw=xnw: e.tensor_add(
                        xnw[:, 512 * half:512 * (half + 1)],
                        xnw[:, 512 * half:512 * (half + 1)],
                        scr_y[:, 512 * half:512 * (half + 1)]), "sP")
                store_tok[tt % 2] = I("gpsimd", lambda e, sq=sq, tt=tt, xnw=xnw: e.dma_start(
                    out=y_out[sq, 128 * tt:128 * (tt + 1), :], in_=xnw[:]), "sS", 16)
            PBF[0] = ("sV", adds_tok[TT - 1])
            PBF[1] = ("sV", stt_tok[TT - 1])
        W("gpsimd", "sS")

        block = st.enter_context(nc.Block())

        @block.sync
        def _(eng):
            for fn in sc.ops["sync"]:
                fn(eng)

        @block.tensor
        def _(eng):
            for fn in sc.ops["tensor"]:
                fn(eng)

        @block.vector
        def _(eng):
            for fn in sc.ops["vector"]:
                fn(eng)

        @block.scalar
        def _(eng):
            for fn in sc.ops["scalar"]:
                fn(eng)

        @block.gpsimd
        def _(eng):
            for fn in sc.ops["gpsimd"]:
                fn(eng)

    return nc


def kernel(L0_attn_x, L0_attn_W_q, L0_attn_W_k, L0_attn_W_v, L0_attn_W_o,
           L0_attn_W_down_kv, L0_attn_W_up_k, L0_attn_W_up_v,
           L0_ffn_weight, L0_ffn_weight_scale, L0_ffn_bias):
    x = np.ascontiguousarray(np.asarray(L0_attn_x, dtype=np.float32))
    wq = np.ascontiguousarray(np.asarray(L0_attn_W_q, dtype=np.float32))
    wkT = np.ascontiguousarray(np.asarray(L0_attn_W_k, dtype=np.float32).T)
    wvT = np.ascontiguousarray(np.asarray(L0_attn_W_v, dtype=np.float32).T)
    wdn = np.ascontiguousarray(np.asarray(L0_attn_W_down_kv, dtype=np.float32))
    wupk = np.ascontiguousarray(np.asarray(L0_attn_W_up_k, dtype=np.float32))
    wupv = np.ascontiguousarray(np.asarray(L0_attn_W_up_v, dtype=np.float32))
    wo = np.ascontiguousarray(np.asarray(L0_attn_W_o, dtype=np.float32))
    wt8 = np.ascontiguousarray(np.asarray(L0_ffn_weight).view(np.uint8).T)
    wsc = np.asarray(L0_ffn_weight_scale, dtype=np.float32).reshape(1, 1)
    bias = np.asarray(L0_ffn_bias, dtype=np.float32).reshape(1, D)

    nc = build_nc()
    in_maps = []
    for c in range(8):
        in_maps.append({
            "x_in": np.ascontiguousarray(x[2 * c:2 * c + 2]),
            "wq_in": wq, "wkT_in": wkT, "wvT_in": wvT, "wdn_in": wdn,
            "wupk_in": wupk, "wupv_in": wupv, "wo_in": wo,
            "wt8_in": wt8, "wsc_in": wsc, "bias_in": bias,
        })
    res = run_bass_kernel_spmd(nc, in_maps, core_ids=list(range(8)))
    out = np.empty((16, S, D), dtype=np.float32)
    for c in range(8):
        out[2 * c:2 * c + 2] = res.results[c]["y_out"]
    return out


# version 09
# speedup vs baseline: 1.1746x; 1.0507x over previous
import math
from contextlib import ExitStack

import numpy as np

import concourse.bass as bass
import concourse.mybir as mybir
from concourse.bass_utils import run_bass_kernel_spmd

F32 = mybir.dt.float32
BF16 = mybir.dt.bfloat16
I32 = mybir.dt.int32
U8 = mybir.dt.uint8
AF = mybir.ActivationFunctionType
ALU = mybir.AluOpType
AX = mybir.AxisListType

S = 2048
D = 1024
H = 16
HD = 64
DL = 256
LAM = 0.01
NSEQ = 2
TT = 16
KT = 16
NQG = 4
QG = 512
EPS = 1e-5
VST = H * 65 + 1


class Sched:
    ENGS = ("sync", "tensor", "vector", "scalar", "gpsimd")

    def __init__(self):
        self.ops = {e: [] for e in self.ENGS}
        self.cnt = {}

    def emit(self, eng, fn):
        self.ops[eng].append(fn)

    def inc(self, sem_name, k=1):
        self.cnt[sem_name] = self.cnt.get(sem_name, 0) + k
        return self.cnt[sem_name]

    def val(self, sem_name):
        return self.cnt.get(sem_name, 0)


DEBUG = False
STRICT = False


def dbg_on(name):
    return DEBUG is True or (isinstance(DEBUG, (set, frozenset)) and name in DEBUG)

STRICT_PHASES = set()


def build_nc():
    nc = bass.Bass("TRN2", target_bir_lowering=False, detect_race_conditions=False)

    x_in = nc.dram_tensor("x_in", [NSEQ, S, D], F32, kind="ExternalInput").ap()
    wq_in = nc.dram_tensor("wq_in", [D, D], F32, kind="ExternalInput").ap()
    wkT_in = nc.dram_tensor("wkT_in", [D, D], F32, kind="ExternalInput").ap()
    wvT_in = nc.dram_tensor("wvT_in", [D, D], F32, kind="ExternalInput").ap()
    wdn_in = nc.dram_tensor("wdn_in", [D, DL], F32, kind="ExternalInput").ap()
    wupk_in = nc.dram_tensor("wupk_in", [DL, D], F32, kind="ExternalInput").ap()
    wupv_in = nc.dram_tensor("wupv_in", [DL, D], F32, kind="ExternalInput").ap()
    wo_in = nc.dram_tensor("wo_in", [D, D], F32, kind="ExternalInput").ap()
    wt8_in = nc.dram_tensor("wt8_in", [D, D], U8, kind="ExternalInput").ap()
    wsc_in = nc.dram_tensor("wsc_in", [1, 1], F32, kind="ExternalInput").ap()
    bias_in = nc.dram_tensor("bias_in", [1, D], F32, kind="ExternalInput").ap()
    y_out = nc.dram_tensor("y_out", [NSEQ, S, D], F32, kind="ExternalOutput").ap()
    if DEBUG:
        d_xn1T = nc.dram_tensor("d_xn1T", [128, 8 * 2048], BF16, kind="ExternalOutput").ap()
        d_latT = nc.dram_tensor("d_latT", [128, 2 * 2048], BF16, kind="ExternalOutput").ap()
        d_vrec = nc.dram_tensor("d_vrec", [128, KT * VST], BF16, kind="ExternalOutput").ap()
        d_qA = nc.dram_tensor("d_qA", [128, 2048], BF16, kind="ExternalOutput").ap()
        d_qB = nc.dram_tensor("d_qB", [128, 2048], BF16, kind="ExternalOutput").ap()
        d_krT = nc.dram_tensor("d_krT", [128, 2048], BF16, kind="ExternalOutput").ap()
        d_aoutT = nc.dram_tensor("d_aoutT", [128, 8 * 2048], BF16, kind="ExternalOutput").ap()
        d_wt = nc.dram_tensor("d_wt", [128, 8 * 1024], BF16, kind="ExternalOutput").ap()
        d_weff = nc.dram_tensor("d_weff", [128, 8 * 256], BF16, kind="ExternalOutput").ap()
        d_ab = nc.dram_tensor("d_ab", [128, 1024], BF16, kind="ExternalOutput").ap()
        d_pq = nc.dram_tensor("d_pq", [128, 1024], F32, kind="ExternalOutput").ap()
        d_ftab = nc.dram_tensor("d_ftab", [128, 128], F32, kind="ExternalOutput").ap()
        d_colA = nc.dram_tensor("d_colA", [128, KT], F32, kind="ExternalOutput").ap()
        d_ab16 = nc.dram_tensor("d_ab16", [128, KT * 128], BF16, kind="ExternalOutput").ap()
        d_pv = nc.dram_tensor("d_pv", [128, 1024], F32, kind="ExternalOutput").ap()

    sc = Sched()

    with ExitStack() as st:
        def sb(name, shape, dtype):
            return st.enter_context(nc.sbuf_tensor(name, shape, dtype))

        def ps(name, shape, dtype=F32):
            return st.enter_context(nc.psum_tensor(name, shape, dtype))

        def sem(name):
            return st.enter_context(nc.semaphore(name))

        xn1T = sb("xn1T", [128, 8 * 2048], BF16)
        latT = sb("latT", [128, 2 * 2048], BF16)
        vrec = sb("vrec", [128, KT * VST], BF16)
        aoutT = sb("aoutT", [128, 8 * 2048], BF16)
        w_big = sb("w_big", [128, 8 * 1024], BF16)
        wupk_sb = sb("wupk_sb", [128, 2 * 1024], BF16)
        wupv_sb = sb("wupv_sb", [128, 2 * 1024], BF16)
        weff_sb = sb("weff_sb", [128, 8 * 256], BF16)
        wt_sb = sb("wt_sb", [128, 8 * 1024], BF16)
        eneg = sb("eneg", [128, 2048], F32)
        epos = sb("epos", [128, 2048], F32)
        colA = sb("colA", [128, KT], F32)
        colB = sb("colB", [128, KT], F32)
        ftab = sb("ftab", [128, 128], F32)
        ident = sb("ident", [128, 128], BF16)
        ones_col = sb("ones_col", [128, 1], BF16)
        bias_b = sb("bias_b", [128, 1024], F32)
        wsc_col = sb("wsc_col", [128, 1], F32)
        qA = sb("qA", [128, 2048], BF16)
        qB = sb("qB", [128, 2048], BF16)
        krT = sb("krT", [128, 2048], BF16)
        xt0 = sb("xt0", [128, 1024], F32)
        xt1 = sb("xt1", [128, 1024], F32)
        xnw0 = sb("xnw0", [128, 1024], F32)
        xnw1 = sb("xnw1", [128, 1024], F32)
        xn_stage = sb("xn_stage", [128, 1024], BF16)
        scr_y = sb("scr_y", [128, 1024], F32)
        scr_t = sb("scr_t", [128, 1024], F32)
        onec = sb("onec", [128, 1], F32)
        stat = sb("stat", [128, 16], F32)
        iot_c = sb("iot_c", [128, 1], F32)
        rr = scr_y
        rb = scr_t
        wkvT = aoutT
        abuf0 = xnw0[:, :].bitcast(BF16)[:, 0:1024]
        abuf1 = xnw1[:, :].bitcast(BF16)[:, 0:1024]
        si1 = qA[:, :].bitcast(I32)
        si2 = qB[:, :].bitcast(I32)
        si3 = krT[:, :].bitcast(I32)
        i2db = si2[:, 0:128]
        f2d = krT[:, :].bitcast(F32)[:, 0:128]

        def pad():
            I("vector", lambda e: e.tensor_copy(out=stat[:, 14:15], in_=onec[:]), "sV")

        def pad_tok():
            return I("vector", lambda e: e.tensor_copy(out=stat[:, 15:16], in_=onec[:]), "sV")

        def recip_safe(out_ap, in_ap):
            I("vector", lambda e: e.tensor_copy(out=stat[:, 14:15], in_=onec[:]), "sV")
            I("vector", lambda e: e.tensor_copy(out=stat[:, 15:16], in_=onec[:]), "sV")
            I("vector", lambda e, o=out_ap, i=in_ap: e.reciprocal(o, i), "sV")
            I("vector", lambda e: e.tensor_copy(out=stat[:, 14:15], in_=onec[:]), "sV")
            I("vector", lambda e: e.tensor_copy(out=stat[:, 15:16], in_=onec[:]), "sV")

        def ones_row(srow):
            a = onec[srow:srow + 1, 0:1]
            return bass.AP(a.tensor, a.offset, [[int(a.ap[0][0]), 1], [0, 128]])

        P0 = ps("P0", [128, 1024])
        P1 = ps("P1", [128, 1024])
        P2 = ps("P2", [128, 512])
        P3 = ps("P3", [128, 512])
        P5 = ps("P5", [128, 512])
        P4 = ps("P4", [128, 1024], BF16)

        sL = sem("sL")
        sG2 = sem("sG2")
        sT = sem("sT")
        sV = sem("sV")
        sC = sem("sC")
        sG = sem("sG")
        sS = sem("sS")
        sP = sem("sP")
        SEMS = {"sL": sL, "sG2": sG2, "sT": sT, "sV": sV, "sC": sC, "sG": sG, "sS": sS, "sP": sP}

        def Wt(eng_name, sem_name, tok):
            if tok:
                s = SEMS[sem_name]
                sc.emit(eng_name, lambda e, s=s, t=tok: e.wait_ge(s, t))

        def W(eng_name, sem_name, v=None):
            target = sc.val(sem_name) if v is None else v
            s = SEMS[sem_name]
            if target > 0:
                sc.emit(eng_name, lambda e, s=s, t=target: e.wait_ge(s, t))

        phase = ["prep"]

        def I(eng_name, fn, sem_name=None, k=1):
            strict_here = (STRICT or phase[0] in STRICT_PHASES
                           or (phase[0] + ":" + eng_name) in STRICT_PHASES)
            if strict_here and eng_name in ("tensor", "vector", "scalar"):
                for other in ("sT", "sV", "sC"):
                    own = {"tensor": "sT", "vector": "sV", "scalar": "sC"}[eng_name]
                    if other != own:
                        W(eng_name, other)
                if sem_name is None:
                    sem_name = {"tensor": "sT", "vector": "sV", "scalar": "sC"}[eng_name]
            if sem_name is None:
                sc.emit(eng_name, fn)
                return None
            s = SEMS[sem_name]
            tok = sc.inc(sem_name, k)
            sc.emit(eng_name, lambda e, s=s, k=k, fn=fn: fn(e).then_inc(s, k))
            return tok

        I("sync", lambda e: e.dma_start(out=bias_b[0:1, :], in_=bias_in[:]), "sL", 16)
        I("sync", lambda e: e.dma_start(out=wsc_col[0:1, :], in_=wsc_in[:]), "sL", 16)
        for c in range(8):
            ta = scr_y if c % 2 == 0 else xt0
            tb = scr_t if c % 2 == 0 else xt1
            I("sync", lambda e, c=c, ta=ta: e.dma_start(out=ta[:], in_=wkT_in[128 * c:128 * (c + 1), :]), "sL", 16)
            I("sync", lambda e, c=c, tb=tb: e.dma_start(out=tb[:], in_=wvT_in[128 * c:128 * (c + 1), :]), "sL", 16)
            W("vector", "sL")
            I("vector", lambda e, c=c, ta=ta, tb=tb: e.tensor_add(
                wkvT[:, 1024 * c:1024 * (c + 1)], ta[:], tb[:]), "sV")
            W("sync", "sV")

        I("gpsimd", lambda e: e.dma_start(out=wupk_sb[:, :].rearrange("p (c n) -> p c n", n=1024), in_=wupk_in.rearrange("(c p) n -> p c n", p=128)), "sG2", 16)
        I("gpsimd", lambda e: e.dma_start(out=wupv_sb[:, :].rearrange("p (c n) -> p c n", n=1024), in_=wupv_in.rearrange("(c p) n -> p c n", p=128)), "sG2", 16)
        I("gpsimd", lambda e: e.dma_start(out=abuf0.rearrange("p (c n) -> p c n", n=256), in_=wdn_in[0:512, :].rearrange("(c p) n -> p c n", p=128)), "sG2", 16)
        I("gpsimd", lambda e: e.dma_start(out=abuf1.rearrange("p (c n) -> p c n", n=256), in_=wdn_in[512:1024, :].rearrange("(c p) n -> p c n", p=128)), "sG2", 16)

        I("gpsimd", lambda e: e.iota(eneg[0:1, :], [[1, 2048]], base=0, channel_multiplier=0,
                                     allow_small_or_imprecise_dtypes=True), "sG")
        I("gpsimd", lambda e: e.iota(iot_c[:], [[1, 1]], base=0, channel_multiplier=1,
                                     allow_small_or_imprecise_dtypes=True), "sG")
        I("gpsimd", lambda e: e.iota(i2db, [[1, 128]], base=0, channel_multiplier=-1,
                                     allow_small_or_imprecise_dtypes=True), "sG")

        W("vector", "sG")
        I("vector", lambda e: e.tensor_scalar(si1[:, 0:128], i2db, 0, None, ALU.is_equal), "sV")
        I("vector", lambda e: e.tensor_copy(out=ident[:], in_=si1[:, 0:128]), "sV")
        I("vector", lambda e: e.tensor_copy(out=f2d, in_=i2db), "sV")
        I("vector", lambda e: e.tensor_scalar(f2d, f2d, 0.0, None, ALU.min), "sV")
        I("vector", lambda e: e.tensor_copy(out=epos[0:1, :], in_=eneg[0:1, :]), "sV")
        W("scalar", "sV")
        I("scalar", lambda e: e.activation(out=ftab[:], in_=f2d[:], func=AF.Exp, scale=2.0 * LAM), "sC")
        I("scalar", lambda e: e.activation(out=eneg[0:1, :], in_=eneg[0:1, :], func=AF.Exp, scale=-LAM), "sC")
        I("scalar", lambda e: e.activation(out=epos[0:1, :], in_=epos[0:1, :], func=AF.Exp, scale=LAM), "sC")
        I("scalar", lambda e: e.activation(out=stat[:, 0:1], in_=iot_c[:], func=AF.Exp, scale=LAM), "sC")
        I("scalar", lambda e: e.activation(out=stat[:, 1:2], in_=iot_c[:], func=AF.Exp, scale=-LAM), "sC")
        W("vector", "sC")
        I("vector", lambda e: e.tensor_scalar(eneg[0:1, :], eneg[0:1, :], 1.0 / 32.0, None, ALU.mult), "sV")
        I("vector", lambda e: e.tensor_scalar(epos[0:1, :], epos[0:1, :], 1.0 / 32.0, None, ALU.mult), "sV")
        for kt in range(KT):
            I("vector", lambda e, kt=kt: e.tensor_scalar(colA[:, kt:kt + 1], stat[:, 0:1],
                                                         float(math.exp(LAM * 128 * kt)), None, ALU.mult), "sV")
            I("vector", lambda e, kt=kt: e.tensor_scalar(colB[:, kt:kt + 1], stat[:, 1:2],
                                                         float(math.exp(-LAM * 128 * kt)), None, ALU.mult), "sV")
        I("gpsimd", lambda e: e.memset(onec[:], 1.0), "sG")
        I("gpsimd", lambda e: e.memset(ones_col[:], 1.0), "sG")
        W("tensor", "sV")
        W("tensor", "sG")
        W("tensor", "sL", 32)
        bcast_jobs = []
        for j in range(4):
            bcast_jobs.append((eneg, 512 * j, 512))
            bcast_jobs.append((epos, 512 * j, 512))
        bcast_jobs.append((bias_b, 0, 512))
        bcast_jobs.append((bias_b, 512, 512))
        bcast_jobs.append((wsc_col, 0, 1))
        for (dst, off, wid) in bcast_jobs:
            I("tensor", lambda e, dst=dst, off=off, wid=wid: e.matmul(
                P5[:, 0:wid], ones_row(0), dst[0:1, off:off + wid],
                start=True, stop=True), "sT")
            W("vector", "sT")
            I("vector", lambda e, dst=dst, off=off, wid=wid: e.tensor_copy(
                out=dst[:, off:off + wid], in_=P5[:, 0:wid]), "sV")
            W("tensor", "sV")
        I("gpsimd", lambda e: e.memset(
            vrec[:, :].rearrange("p (k q) -> p k q", q=VST)[:, :, 0:H * 65].rearrange(
                "p k (h j) -> p k h j", j=65)[:, :, :, 64:65], 1.0), "sG")

        if DEBUG and dbg_on("prepb"):
            W("gpsimd", "sV")
            I("gpsimd", lambda e: e.dma_start(out=d_pq[:, 0:1], in_=wsc_col[:, :], allow_slow_non_contiguous=True), "sS", 16)
            I("gpsimd", lambda e: e.dma_start(out=d_pq[:, 1:2], in_=eneg[:, 777:778], allow_slow_non_contiguous=True), "sS", 16)
            I("gpsimd", lambda e: e.dma_start(out=d_pq[:, 2:3], in_=epos[:, 777:778], allow_slow_non_contiguous=True), "sS", 16)
            I("gpsimd", lambda e: e.dma_start(out=d_pq[:, 3:4], in_=bias_b[:, 333:334], allow_slow_non_contiguous=True), "sS", 16)
        scr_i4 = scr_t[:, :].bitcast(I32)
        for c in range(8):
            cs = slice(1024 * c, 1024 * (c + 1))
            W("gpsimd", "sV")
            I("gpsimd", lambda e, c=c: e.dma_start(
                out=scr_y[:], in_=wt8_in.rearrange("(c p) n -> c p n", p=128)[c]), "sG2", 16)
            W("vector", "sG2")
            I("vector", lambda e: e.tensor_copy(out=si1, in_=scr_y[:]), "sV")
            I("vector", lambda e: e.tensor_scalar(si2, si1, 3, 15, ALU.logical_shift_right, ALU.bitwise_and), "sV")
            I("vector", lambda e: e.tensor_scalar(si3, si1, 7, None, ALU.bitwise_and), "sV")
            I("vector", lambda e: e.tensor_scalar(si3, si3, 8, None, ALU.add), "sV")
            I("vector", lambda e: e.tensor_scalar(si1, si1, 7, None, ALU.logical_shift_right), "sV")
            I("vector", lambda e: e.tensor_scalar(si1, si1, -2, 1, ALU.mult, ALU.add), "sV")
            I("vector", lambda e: e.tensor_tensor(scr_i4, si3, si2, ALU.arith_shift_left), "sV")
            I("vector", lambda e: e.tensor_scalar(si3, si3, 2, -16, ALU.mult, ALU.add), "sV")
            I("vector", lambda e: e.tensor_scalar(si2, si2, 0, None, ALU.is_equal), "sV")
            I("vector", lambda e: e.tensor_tensor(si3, si3, si2, ALU.mult), "sV")
            I("vector", lambda e: e.tensor_scalar(si2, si2, -1, 1, ALU.mult, ALU.add), "sV")
            I("vector", lambda e: e.tensor_tensor(scr_i4, scr_i4, si2, ALU.mult), "sV")
            I("vector", lambda e: e.tensor_tensor(scr_i4, scr_i4, si3, ALU.add), "sV")
            I("vector", lambda e: e.tensor_tensor(scr_i4, scr_i4, si1, ALU.mult), "sV")
            I("vector", lambda e: e.tensor_copy(out=scr_t[:], in_=scr_i4), "sV")
            I("vector", lambda e, cs=cs: e.tensor_scalar(wt_sb[:, cs], scr_t[:], float(2.0 ** -10), None, ALU.mult), "sV")

        W("tensor", "sG2", 64)
        W("tensor", "sV")
        for dt_ in range(8):
            for dd in range(8):
                wdn_src = abuf0 if dd < 4 else abuf1
                I("tensor", lambda e, dt_=dt_, dd=dd, wdn_src=wdn_src: e.matmul(
                    P5[:, 0:256],
                    wkvT[:, 1024 * dd + 128 * dt_:1024 * dd + 128 * (dt_ + 1)],
                    wdn_src[:, 256 * (dd % 4):256 * (dd % 4 + 1)],
                    start=(dd == 0), stop=(dd == 7)), "sT" if dd == 7 else None)
            W("vector", "sT")
            I("vector", lambda e, dt_=dt_: e.tensor_copy(out=weff_sb[:, 256 * dt_:256 * (dt_ + 1)], in_=P5[:, 0:256]), "sV")
            W("tensor", "sV")

        PBF = {0: None, 1: None}
        xt_last_reader = {0: None, 1: None}

        def WB(eng_name, bid):
            ent = PBF[bid]
            if ent:
                Wt(eng_name, ent[0], ent[1])

        for sq in range(NSEQ):
            phase[0] = "A"
            W("vector", "sT")
            W("scalar", "sP")
            W("scalar", "sV")
            W("scalar", "sT")
            W("tensor", "sC")
            a_apply = {}
            a_sq = {}
            a_evac = {}
            for tt in range(TT):
                xt = xt0 if tt % 2 == 0 else xt1
                stage = xn_stage if tt % 2 == 0 else krT[:, 0:1024]
                so = 4 * (tt % 2)
                dq = "sync" if tt % 2 == 0 else "gpsimd"
                Wt(dq, "sV", a_apply.get(tt - 2) or xt_last_reader[tt % 2])
                Wt(dq, "sC", a_sq.get(tt - 2))
                dl = I(dq, lambda e, sq=sq, tt=tt, xt=xt: e.dma_start(
                    out=xt[:], in_=x_in[sq, 128 * tt:128 * (tt + 1), :]), "sL", 16)
                Wt("vector", "sL", dl)
                I("vector", lambda e, xt=xt, so=so: e.reduce_sum(stat[:, so:so + 1], xt[:], AX.X), "sV")
                Wt("scalar", "sL", dl)
                a_sq[tt] = I("scalar", lambda e, xt=xt, so=so: e.activation(
                    out=scr_y[:], in_=xt[:], func=AF.Square, accum_out=stat[:, so + 1:so + 2]), "sC")
                Wt("vector", "sC", a_sq[tt])
                I("vector", lambda e, so=so: e.tensor_scalar(stat[:, so:so + 1], stat[:, so:so + 1], 1.0 / D, None, ALU.mult), "sV")
                I("vector", lambda e, so=so: e.tensor_scalar(stat[:, so + 1:so + 2], stat[:, so + 1:so + 2], 1.0 / D, None, ALU.mult), "sV")
                pad()
                I("vector", lambda e, so=so: e.tensor_tensor(stat[:, so + 2:so + 3], stat[:, so:so + 1], stat[:, so:so + 1], ALU.mult), "sV")
                pad(); pad()
                I("vector", lambda e, so=so: e.tensor_tensor(stat[:, so + 1:so + 2], stat[:, so + 1:so + 2], stat[:, so + 2:so + 3], ALU.subtract), "sV")
                pad(); pad()
                teps = I("vector", lambda e, so=so: e.tensor_scalar(stat[:, so + 1:so + 2], stat[:, so + 1:so + 2], EPS, None, ALU.add), "sV")
                Wt("scalar", "sV", teps)
                tsq = I("scalar", lambda e, so=so: e.activation(out=stat[:, so + 2:so + 3], in_=stat[:, so + 1:so + 2], func=AF.Sqrt), "sC")
                Wt("vector", "sC", tsq)
                recip_safe(stat[:, so + 3:so + 4], stat[:, so + 2:so + 3])
                a_apply[tt] = I("vector", lambda e, xt=xt, so=so, stage=stage: e.tensor_scalar(
                    stage[:], xt[:], stat[:, so:so + 1], stat[:, so + 3:so + 4], ALU.subtract, ALU.mult), "sV")
                Wt("tensor", "sV", a_apply[tt])
                Wt("tensor", "sC", a_evac.get(tt - 1))
                trt = None
                for cdx in range(8):
                    trt = I("tensor", lambda e, cdx=cdx, stage=stage: e.transpose(
                        P4[:, 128 * cdx:128 * (cdx + 1)], stage[:, 128 * cdx:128 * (cdx + 1)], ident[:]),
                        "sT" if cdx == 7 else None) or trt
                Wt("scalar", "sT", trt)
                a_evac[tt] = I("scalar", lambda e, tt=tt: e.activation(
                    out=xn1T[:, :].rearrange("p (c t) -> p c t", c=8)[:, :, 128 * tt:128 * (tt + 1)],
                    in_=P4[:, :].rearrange("p (c t) -> p c t", c=8), func=AF.Copy), "sC")


            if DEBUG and sq == 0 and dbg_on("xn1T"):
                W("gpsimd", "sV")
                I("gpsimd", lambda e: e.dma_start(out=d_xn1T[:], in_=xn1T[:, :]), "sS", 16)
            W("gpsimd", "sT")
            I("gpsimd", lambda e: e.dma_start(out=w_big[:, :].rearrange("p (c n) -> p c n", n=1024), in_=wq_in.rearrange("(c p) n -> p c n", p=128)), "sG2", 16)

            phase[0] = "B"
            W("tensor", "sG")
            W("tensor", "sC")
            for lc in range(2):
                for tq in range(4):
                    bid = (lc * 4 + tq) % 2
                    P = P0 if bid == 0 else P1
                    WB("tensor", bid)
                    mm_tok = None
                    for dc in range(8):
                        mm_tok = I("tensor", lambda e, lc=lc, tq=tq, dc=dc, P=P: e.matmul(
                            P[:, 0:512],
                            weff_sb[:, 256 * dc + 128 * lc:256 * dc + 128 * (lc + 1)],
                            xn1T[:, 2048 * dc + 512 * tq:2048 * dc + 512 * (tq + 1)],
                            start=(dc == 0), stop=(dc == 7)), "sT" if dc == 7 else None) or mm_tok
                    Wt("vector", "sT", mm_tok)
                    PBF[bid] = ("sV", I("vector", lambda e, lc=lc, tq=tq, P=P: e.tensor_copy(
                        out=latT[:, 2048 * lc + 512 * tq:2048 * lc + 512 * (tq + 1)], in_=P[:, 0:512]), "sV"))
            for tt in range(TT):
                for half in range(2):
                    bid = (tt * 2 + half) % 2
                    P = P0 if bid == 0 else P1
                    WB("tensor", bid)
                    mm_tok = None
                    for lc in range(2):
                        mm_tok = I("tensor", lambda e, tt=tt, half=half, lc=lc, P=P: e.matmul(
                            P[:, 0:512],
                            latT[:, 2048 * lc + 128 * tt:2048 * lc + 128 * (tt + 1)],
                            wupv_sb[:, 1024 * lc + 512 * half:1024 * lc + 512 * (half + 1)],
                            start=(lc == 0), stop=(lc == 1)), "sT" if lc == 1 else None) or mm_tok
                    Wt("vector", "sT", mm_tok)
                    PBF[bid] = ("sV", I("vector", lambda e, tt=tt, half=half, P=P: e.tensor_copy(
                        out=vrec[:, :].rearrange("p (k q) -> p k q", q=VST)[
                            :, tt, 65 * 8 * half:65 * 8 * (half + 1)].rearrange(
                            "p (h j) -> p h j", j=65)[:, :, 0:64],
                        in_=P[:, 0:512].rearrange("p (h j) -> p h j", j=64)), "sV"))

            if DEBUG and sq == 0 and dbg_on("latT"):
                W("gpsimd", "sV")
                I("gpsimd", lambda e: e.dma_start(out=d_latT[:], in_=latT[:, :]), "sS", 16)
                I("gpsimd", lambda e: e.dma_start(out=d_vrec[:], in_=vrec[:, :]), "sS", 16)
                I("gpsimd", lambda e: e.dma_start(out=d_weff[:], in_=weff_sb[:, :]), "sS", 16)
                I("gpsimd", lambda e: e.dma_start(out=d_wt[:], in_=wt_sb[:, :]), "sS", 16)
            phase[0] = "attn"
            att_prev_pv = {}
            att_last_qk = {0: None}
            pending_norm = [None]
            att_prev_evac = {0: sc.val("sV")}
            att_prev_p5 = {0: sc.val("sV")}
            W("tensor", "sG2")
            W("scalar", "sS")
            W("tensor", "sS")
            W("vector", "sS")
            for hp in range(8):
                prev_qk = att_last_qk[0]
                for tq in range(4):
                    bid = tq % 2
                    P = P0 if bid == 0 else P1
                    WB("tensor", bid)
                    mm_tok = None
                    for lc in range(2):
                        mm_tok = I("tensor", lambda e, hp=hp, tq=tq, lc=lc, P=P: e.matmul(
                            P[:, 0:512],
                            wupk_sb[:, 1024 * lc + 128 * hp:1024 * lc + 128 * (hp + 1)],
                            latT[:, 2048 * lc + 512 * tq:2048 * lc + 512 * (tq + 1)],
                            start=(lc == 0), stop=(lc == 1)), "sT" if lc == 1 else None) or mm_tok
                    Wt("vector", "sT", mm_tok)
                    Wt("vector", "sT", prev_qk)
                    PBF[bid] = ("sV", I("vector", lambda e, tq=tq, P=P: e.tensor_copy(
                        out=krT[:, 512 * tq:512 * (tq + 1)], in_=P[:, 0:512]), "sV"))
                for tq in range(4):
                    bid = tq % 2
                    P = P0 if bid == 0 else P1
                    WB("tensor", bid)
                    mm_tok = None
                    for dc in range(8):
                        mm_tok = I("tensor", lambda e, hp=hp, tq=tq, dc=dc, P=P: e.matmul(
                            P[:, 0:512],
                            w_big[:, 1024 * dc + 128 * hp:1024 * dc + 128 * (hp + 1)],
                            xn1T[:, 2048 * dc + 512 * tq:2048 * dc + 512 * (tq + 1)],
                            start=(dc == 0), stop=(dc == 7)), "sT" if dc == 7 else None) or mm_tok
                    Wt("vector", "sT", mm_tok)
                    Wt("vector", "sT", prev_qk)
                    I("vector", lambda e, tq=tq, P=P: e.tensor_mul(
                        qA[:, 512 * tq:512 * (tq + 1)], P[:, 0:512], eneg[:, 512 * tq:512 * (tq + 1)]), "sV")
                    PBF[bid] = ("sV", I("vector", lambda e, tq=tq, P=P: e.tensor_mul(
                        qB[:, 512 * tq:512 * (tq + 1)], P[:, 0:512], epos[:, 512 * tq:512 * (tq + 1)]), "sV"))
                pp_tok = sc.val("sV")

                if DEBUG and sq == 0 and hp == 0 and dbg_on("qA"):
                    W("gpsimd", "sV")
                    I("gpsimd", lambda e: e.dma_start(out=d_qA[:], in_=qA[:, :]), "sS", 16)
                    I("gpsimd", lambda e: e.dma_start(out=d_qB[:], in_=qB[:, :]), "sS", 16)
                    I("gpsimd", lambda e: e.dma_start(out=d_krT[:], in_=krT[:, :]), "sS", 16)
                    W("tensor", "sS")
                for qg in range(NQG):
                    qgs = QG * qg
                    t_qk = {}
                    c_exp = {}
                    t_pv = {}
                    first_qk = (qg == 0)

                    def emit_qk(kt, qgs=qgs, first_qk=first_qk):
                        k0 = 128 * kt
                        Pq = P0 if kt % 2 == 0 else P1
                        ab = abuf0 if kt % 2 == 0 else abuf1
                        regs = []
                        if k0 + 128 <= qgs:
                            regs.append(("A", 0, QG))
                        elif k0 >= qgs + QG:
                            regs.append(("B", 0, QG))
                        else:
                            rel = k0 - qgs
                            if rel > 0:
                                regs.append(("B", 0, rel))
                            regs.append(("Ad", rel, QG - rel))
                        WB("tensor", kt % 2)
                        if kt == 0 and first_qk:
                            Wt("tensor", "sV", pp_tok)
                        tok = None
                        for (side, rel, wid) in regs:
                            src = qA if side in ("A", "Ad") else qB
                            I("tensor", lambda e, kt=kt, rel=rel, wid=wid, src=src, Pq=Pq, qgs=qgs: e.matmul(
                                Pq[:, rel:rel + wid],
                                krT[0:64, 128 * kt:128 * (kt + 1)],
                                src[0:64, qgs + rel:qgs + rel + wid],
                                start=True, stop=True), None)
                            tok = I("tensor", lambda e, kt=kt, rel=rel, wid=wid, src=src, Pq=Pq, qgs=qgs: e.matmul(
                                Pq[:, 512 + rel:512 + rel + wid],
                                krT[64:128, 128 * kt:128 * (kt + 1)],
                                src[64:128, qgs + rel:qgs + rel + wid],
                                start=True, stop=True, tile_position=(64, 0)), "sT")
                        t_qk[kt] = tok
                        att_last_qk[0] = tok
                        has_diag = any(s == "Ad" for (s, _, _) in regs)
                        v_diag = None
                        if has_diag:
                            rel = k0 - qgs
                            Wt("vector", "sT", t_qk[kt])
                            I("vector", lambda e, rel=rel, Pq=Pq: e.tensor_mul(
                                Pq[:, rel:rel + 128], Pq[:, rel:rel + 128], ftab[:]), "sV")
                            v_diag = I("vector", lambda e, rel=rel, Pq=Pq: e.tensor_mul(
                                Pq[:, 512 + rel:512 + rel + 128], Pq[:, 512 + rel:512 + rel + 128], ftab[:]), "sV")
                        Wt("scalar", "sT", t_qk[kt])
                        Wt("scalar", "sT", att_prev_pv.get(kt % 2))
                        if has_diag:
                            Wt("scalar", "sV", v_diag)
                        if len(regs) == 1:
                            (side, rel, wid) = regs[0]
                            col = colA if side in ("A", "Ad") else colB
                            tok = I("scalar", lambda e, kt=kt, col=col, Pq=Pq, ab=ab: e.activation(
                                out=ab[:, 0:1024], in_=Pq[:, 0:1024],
                                func=AF.Exp, scale=col[:, kt:kt + 1]), "sC")
                        else:
                            for hh in range(2):
                                for (side, rel, wid) in regs:
                                    col = colA if side in ("A", "Ad") else colB
                                    tok = I("scalar", lambda e, rel=rel, wid=wid, kt=kt, col=col, Pq=Pq, ab=ab, hh=hh: e.activation(
                                        out=ab[:, 512 * hh + rel:512 * hh + rel + wid],
                                        in_=Pq[:, 512 * hh + rel:512 * hh + rel + wid],
                                        func=AF.Exp, scale=col[:, kt:kt + 1]), "sC")
                        c_exp[kt] = tok
                        PBF[kt % 2] = ("sC", tok)

                    def emit_pv(kt, hp=hp):
                        ab = abuf0 if kt % 2 == 0 else abuf1
                        Wt("tensor", "sC", c_exp[kt])
                        if kt == 0:
                            Wt("tensor", "sV", att_prev_evac[0])
                        for hh in range(2):
                            Pv = P2 if hh == 0 else P3
                            ha = 2 * hp + hh
                            tok = I("tensor", lambda e, ha=ha, kt=kt, Pv=Pv, ab=ab, hh=hh: e.matmul(
                                Pv[0:65, :],
                                vrec[:, VST * kt + 65 * ha:VST * kt + 65 * ha + 65],
                                ab[:, 512 * hh:512 * (hh + 1)],
                                start=(kt == 0), stop=(kt == KT - 1)),
                                "sT" if hh == 1 else None)
                        t_pv[kt] = tok
                        att_prev_pv[kt % 2] = tok

                    emit_qk(0)
                    if pending_norm[0] is not None:
                        pending_norm[0]()
                        pending_norm[0] = None
                    for kt in range(1, KT):
                        emit_qk(kt)
                        emit_pv(kt - 1)
                    emit_pv(KT - 1)
                    Wt("vector", "sT", t_pv[KT - 1])
                    rtoks = []
                    for hh in range(2):
                        Pv = P2 if hh == 0 else P3
                        recip_safe(rr[64:65, 512 * hh:512 * (hh + 1)], Pv[64:65, :])
                        rtoks.append(sc.val("sV"))

                    def emit_norm(hp=hp, qgs=qgs, rtoks=rtoks):
                        for hh in range(2):
                            Pv = P2 if hh == 0 else P3
                            hr = slice(64 * hh, 64 * (hh + 1))
                            Wt("tensor", "sV", rtoks[hh])
                            Wt("tensor", "sV", att_prev_p5[0])
                            btok = I("tensor", lambda e, hh=hh: e.matmul(
                                P5[:, 0:512], ones_row(64),
                                rr[64:65, 512 * hh:512 * (hh + 1)],
                                start=True, stop=True), "sT")
                            Wt("vector", "sT", btok)
                            att_prev_p5[0] = I("vector", lambda e, hh=hh: e.tensor_copy(
                                out=rb[0:64, 512 * hh:512 * (hh + 1)], in_=P5[0:64, 0:512]), "sV")
                            att_prev_evac[0] = I("vector", lambda e, Pv=Pv, hp=hp, qgs=qgs, hh=hh, hr=hr: e.tensor_mul(
                                aoutT[hr, 2048 * hp + qgs:2048 * hp + qgs + 512],
                                Pv[0:64, :], rb[0:64, 512 * hh:512 * (hh + 1)]), "sV")
                    pending_norm[0] = emit_norm
            if pending_norm[0] is not None:
                pending_norm[0]()
                pending_norm[0] = None
            if DEBUG and sq == 0 and dbg_on("aoutT"):
                W("gpsimd", "sV")
                I("gpsimd", lambda e: e.dma_start(out=d_aoutT[:], in_=aoutT[:, :]), "sS", 16)
            W("gpsimd", "sT")
            I("gpsimd", lambda e: e.dma_start(out=w_big[:, :].rearrange("p (c n) -> p c n", n=1024), in_=wo_in.rearrange("(c p) n -> p c n", p=128)), "sG2", 16)

            phase[0] = "cffn_wo"
            W("tensor", "sG2")
            if sq == 0:
                store_tok = {}
            p4_free = {0: None}
            adds_tok = {}
            stt_tok = {}
            quant_tok = {}
            t_wo_tok = {}

            def emit_wo(tt):
                xt = xt0 if tt % 2 == 0 else xt1
                W("sync", "sV")
                I("sync", lambda e, sq=sq, tt=tt, xt=xt: e.dma_start(
                    out=xt[:], in_=x_in[sq, 128 * tt:128 * (tt + 1), :]), "sL", 16)
                if tt == 0:
                    WB("tensor", 0)
                else:
                    Wt("tensor", "sV", adds_tok.get(tt - 1))
                tok = None
                for half in range(2):
                    for dc in range(8):
                        tok = I("tensor", lambda e, tt=tt, half=half, dc=dc: e.matmul(
                            P0[:, 512 * half:512 * (half + 1)],
                            aoutT[:, 2048 * dc + 128 * tt:2048 * dc + 128 * (tt + 1)],
                            w_big[:, 1024 * dc + 512 * half:1024 * dc + 512 * (half + 1)],
                            start=(dc == 0), stop=(dc == 7)),
                            "sT" if (half == 1 and dc == 7) else None) or tok
                t_wo_tok[tt] = tok

            emit_wo(0)
            for tt in range(TT):
                xt = xt0 if tt % 2 == 0 else xt1
                xnw = xnw0 if tt % 2 == 0 else xnw1
                phase[0] = "cffn_wo"
                Wt("vector", "sT", t_wo_tok[tt])
                W("vector", "sL")
                Wt("vector", "sS", store_tok.get(tt % 2))
                atok = None
                for half in range(2):
                    atok = I("vector", lambda e, half=half, xt=xt, xnw=xnw: e.tensor_add(
                        xnw[:, 512 * half:512 * (half + 1)],
                        P0[:, 512 * half:512 * (half + 1)], xt[:, 512 * half:512 * (half + 1)]), "sV")
                adds_tok[tt] = atok
                xt_last_reader[tt % 2] = atok
                if tt + 1 < TT:
                    emit_wo(tt + 1)
                phase[0] = "cffn_ln"
                I("vector", lambda e, xnw=xnw: e.tensor_reduce(stat[:, 4:5], xnw[:], AX.X, ALU.max), "sV")
                I("vector", lambda e, xnw=xnw: e.tensor_reduce(stat[:, 5:6], xnw[:], AX.X, ALU.min), "sV")
                Wt("scalar", "sV", atok)
                W("scalar", "sP")
                I("scalar", lambda e, xnw=xnw: e.activation(out=scr_y[:], in_=xnw[:], func=AF.Copy,
                                                            accum_out=stat[:, 0:1]), "sC")
                I("scalar", lambda e, xnw=xnw: e.activation(out=scr_y[:], in_=xnw[:], func=AF.Square,
                                                            accum_out=stat[:, 1:2]), "sC")
                W("vector", "sC")
                I("vector", lambda e: e.tensor_scalar(stat[:, 0:1], stat[:, 0:1], 1.0 / D, None, ALU.mult), "sV")
                I("vector", lambda e: e.tensor_scalar(stat[:, 1:2], stat[:, 1:2], 1.0 / D, None, ALU.mult), "sV")
                pad()
                I("vector", lambda e: e.tensor_tensor(stat[:, 2:3], stat[:, 0:1], stat[:, 0:1], ALU.mult), "sV")
                I("vector", lambda e: e.tensor_tensor(stat[:, 4:5], stat[:, 4:5], stat[:, 0:1], ALU.subtract), "sV")
                I("vector", lambda e: e.tensor_tensor(stat[:, 5:6], stat[:, 0:1], stat[:, 5:6], ALU.subtract), "sV")
                I("vector", lambda e: e.tensor_tensor(stat[:, 1:2], stat[:, 1:2], stat[:, 2:3], ALU.subtract), "sV")
                I("vector", lambda e: e.tensor_tensor(stat[:, 4:5], stat[:, 4:5], stat[:, 5:6], ALU.max), "sV")
                teps = I("vector", lambda e: e.tensor_scalar(stat[:, 1:2], stat[:, 1:2], EPS, None, ALU.add), "sV")
                Wt("scalar", "sV", teps)
                tsqrt = I("scalar", lambda e: e.activation(out=stat[:, 2:3], in_=stat[:, 1:2], func=AF.Sqrt), "sC")
                Wt("vector", "sC", tsqrt)
                recip_safe(stat[:, 3:4], stat[:, 2:3])
                pad(); pad()
                I("vector", lambda e: e.tensor_scalar(stat[:, 4:5], stat[:, 4:5], stat[:, 3:4], 1e-12, ALU.mult, ALU.max), "sV")
                recip_safe(stat[:, 6:7], stat[:, 4:5])
                I("vector", lambda e: e.tensor_scalar(stat[:, 8:9], stat[:, 4:5], wsc_col[:, 0:1], 1.0 / 448.0, ALU.mult, ALU.mult), "sV")
                pad()
                I("vector", lambda e: e.tensor_scalar(stat[:, 9:10], stat[:, 6:7], stat[:, 3:4], 448.0, ALU.mult, ALU.mult), "sV")
                pad(); pad()
                I("vector", lambda e: e.tensor_scalar(stat[:, 10:11], stat[:, 9:10], stat[:, 0:1], -1.0, ALU.mult, ALU.mult), "sV")
                pad()
                t10 = pad_tok()
                Wt("scalar", "sV", t10)
                tapply = I("scalar", lambda e, xnw=xnw: e.activation(out=scr_y[:], in_=xnw[:], func=AF.Identity,
                                                                    bias=stat[:, 10:11], scale=stat[:, 9:10]), "sC")
                phase[0] = "cffn_q"
                Wt("vector", "sC", tapply)
                I("vector", lambda e: e.tensor_scalar(si1, scr_y[:, :].bitcast(I32), 0x7F800000, None, ALU.bitwise_and), "sV")
                I("vector", lambda e: e.tensor_scalar(si1, si1, 0x3C800000, None, ALU.max), "sV")
                I("vector", lambda e: e.tensor_scalar(si2, si1, -1, 0x44000000, ALU.mult, ALU.add), "sV")
                I("vector", lambda e: e.tensor_scalar(si2, si2, 0x3C800000, None, ALU.add), "sV")
                I("vector", lambda e: e.tensor_tensor(scr_t[:], scr_y[:], si2.bitcast(F32), ALU.mult), "sV")
                I("vector", lambda e: e.tensor_copy(out=si3, in_=scr_t[:]), "sV")
                I("vector", lambda e: e.tensor_scalar(si2, si1, -(3 << 23), None, ALU.add), "sV")
                I("vector", lambda e: e.tensor_copy(out=scr_t[:], in_=si3), "sV")
                quant_tok[tt] = I("vector", lambda e: e.tensor_tensor(xn_stage[:], scr_t[:], si2.bitcast(F32), ALU.mult), "sV")
                phase[0] = "cffn_mm"
                Wt("tensor", "sV", quant_tok[tt])
                Wt("tensor", "sC", p4_free[0])
                t_tr = None
                for cdx in range(8):
                    t_tr = I("tensor", lambda e, cdx=cdx: e.transpose(
                        P4[:, 128 * cdx:128 * (cdx + 1)], xn_stage[:, 128 * cdx:128 * (cdx + 1)], ident[:]),
                        "sT" if cdx == 7 else None) or t_tr
                Wt("scalar", "sT", t_tr)
                p4_free[0] = I("scalar", lambda e, tt=tt: e.activation(
                    out=xn1T[:, :].rearrange("p (c t) -> p c t", c=8)[:, :, 128 * tt:128 * (tt + 1)],
                    in_=P4[:, :].rearrange("p (c t) -> p c t", c=8), func=AF.Copy), "sC")
                t_ffn = None
                Wt("tensor", "sC", p4_free[0])
                Wt("tensor", "sV", stt_tok.get(tt - 1))
                for half in range(2):
                    for kc in range(8):
                        t_ffn = I("tensor", lambda e, tt=tt, half=half, kc=kc: e.matmul(
                            P1[:, 512 * half:512 * (half + 1)],
                            xn1T[:, 2048 * kc + 128 * tt:2048 * kc + 128 * (tt + 1)],
                            wt_sb[:, 1024 * kc + 512 * half:1024 * kc + 512 * (half + 1)],
                            start=(kc == 0), stop=(kc == 7)),
                            "sT" if (half == 1 and kc == 7) else None) or t_ffn
                Wt("vector", "sT", t_ffn)
                stok = None
                for half in range(2):
                    stok = I("vector", lambda e, half=half: e.scalar_tensor_tensor(
                        scr_y[:, 512 * half:512 * (half + 1)],
                        P1[:, 512 * half:512 * (half + 1)], stat[:, 8:9], bias_b[:, 512 * half:512 * (half + 1)],
                        ALU.mult, ALU.add), "sV")
                stt_tok[tt] = stok
                Wt("gpsimd", "sV", stok)
                for half in range(2):
                    I("gpsimd", lambda e, half=half, xnw=xnw: e.tensor_add(
                        xnw[:, 512 * half:512 * (half + 1)],
                        xnw[:, 512 * half:512 * (half + 1)],
                        scr_y[:, 512 * half:512 * (half + 1)]), "sP")
                store_tok[tt % 2] = I("gpsimd", lambda e, sq=sq, tt=tt, xnw=xnw: e.dma_start(
                    out=y_out[sq, 128 * tt:128 * (tt + 1), :], in_=xnw[:]), "sS", 16)
            PBF[0] = ("sV", adds_tok[TT - 1])
            PBF[1] = ("sV", stt_tok[TT - 1])
        W("gpsimd", "sS")

        block = st.enter_context(nc.Block())

        @block.sync
        def _(eng):
            for fn in sc.ops["sync"]:
                fn(eng)

        @block.tensor
        def _(eng):
            for fn in sc.ops["tensor"]:
                fn(eng)

        @block.vector
        def _(eng):
            for fn in sc.ops["vector"]:
                fn(eng)

        @block.scalar
        def _(eng):
            for fn in sc.ops["scalar"]:
                fn(eng)

        @block.gpsimd
        def _(eng):
            for fn in sc.ops["gpsimd"]:
                fn(eng)

    return nc


def kernel(L0_attn_x, L0_attn_W_q, L0_attn_W_k, L0_attn_W_v, L0_attn_W_o,
           L0_attn_W_down_kv, L0_attn_W_up_k, L0_attn_W_up_v,
           L0_ffn_weight, L0_ffn_weight_scale, L0_ffn_bias):
    x = np.ascontiguousarray(np.asarray(L0_attn_x, dtype=np.float32))
    wq = np.ascontiguousarray(np.asarray(L0_attn_W_q, dtype=np.float32))
    wkT = np.ascontiguousarray(np.asarray(L0_attn_W_k, dtype=np.float32).T)
    wvT = np.ascontiguousarray(np.asarray(L0_attn_W_v, dtype=np.float32).T)
    wdn = np.ascontiguousarray(np.asarray(L0_attn_W_down_kv, dtype=np.float32))
    wupk = np.ascontiguousarray(np.asarray(L0_attn_W_up_k, dtype=np.float32))
    wupv = np.ascontiguousarray(np.asarray(L0_attn_W_up_v, dtype=np.float32))
    wo = np.ascontiguousarray(np.asarray(L0_attn_W_o, dtype=np.float32))
    wt8 = np.ascontiguousarray(np.asarray(L0_ffn_weight).view(np.uint8).T)
    wsc = np.asarray(L0_ffn_weight_scale, dtype=np.float32).reshape(1, 1)
    bias = np.asarray(L0_ffn_bias, dtype=np.float32).reshape(1, D)

    nc = build_nc()
    in_maps = []
    for c in range(8):
        in_maps.append({
            "x_in": np.ascontiguousarray(x[2 * c:2 * c + 2]),
            "wq_in": wq, "wkT_in": wkT, "wvT_in": wvT, "wdn_in": wdn,
            "wupk_in": wupk, "wupv_in": wupv, "wo_in": wo,
            "wt8_in": wt8, "wsc_in": wsc, "bias_in": bias,
        })
    res = run_bass_kernel_spmd(nc, in_maps, core_ids=list(range(8)))
    out = np.empty((16, S, D), dtype=np.float32)
    for c in range(8):
        out[2 * c:2 * c + 2] = res.results[c]["y_out"]
    return out


# version 10
# speedup vs baseline: 1.1776x; 1.0025x over previous
import math
from contextlib import ExitStack

import numpy as np

import concourse.bass as bass
import concourse.mybir as mybir
from concourse.bass_utils import run_bass_kernel_spmd

F32 = mybir.dt.float32
BF16 = mybir.dt.bfloat16
I32 = mybir.dt.int32
U8 = mybir.dt.uint8
AF = mybir.ActivationFunctionType
ALU = mybir.AluOpType
AX = mybir.AxisListType

S = 2048
D = 1024
H = 16
HD = 64
DL = 256
LAM = 0.01
NSEQ = 2
TT = 16
KT = 16
NQG = 4
QG = 512
EPS = 1e-5
VST = H * 65 + 1


class Sched:
    ENGS = ("sync", "tensor", "vector", "scalar", "gpsimd")

    def __init__(self):
        self.ops = {e: [] for e in self.ENGS}
        self.cnt = {}

    def emit(self, eng, fn):
        self.ops[eng].append(fn)

    def inc(self, sem_name, k=1):
        self.cnt[sem_name] = self.cnt.get(sem_name, 0) + k
        return self.cnt[sem_name]

    def val(self, sem_name):
        return self.cnt.get(sem_name, 0)


DEBUG = False
STRICT = False


def dbg_on(name):
    return DEBUG is True or (isinstance(DEBUG, (set, frozenset)) and name in DEBUG)

STRICT_PHASES = set()


def build_nc():
    nc = bass.Bass("TRN2", target_bir_lowering=False, detect_race_conditions=False)

    x_in = nc.dram_tensor("x_in", [NSEQ, S, D], F32, kind="ExternalInput").ap()
    wq_in = nc.dram_tensor("wq_in", [D, D], F32, kind="ExternalInput").ap()
    wkT_in = nc.dram_tensor("wkT_in", [D, D], F32, kind="ExternalInput").ap()
    wvT_in = nc.dram_tensor("wvT_in", [D, D], F32, kind="ExternalInput").ap()
    wdn_in = nc.dram_tensor("wdn_in", [D, DL], F32, kind="ExternalInput").ap()
    wupk_in = nc.dram_tensor("wupk_in", [DL, D], F32, kind="ExternalInput").ap()
    wupv_in = nc.dram_tensor("wupv_in", [DL, D], F32, kind="ExternalInput").ap()
    wo_in = nc.dram_tensor("wo_in", [D, D], F32, kind="ExternalInput").ap()
    wt8_in = nc.dram_tensor("wt8_in", [D, D], U8, kind="ExternalInput").ap()
    wsc_in = nc.dram_tensor("wsc_in", [1, 1], F32, kind="ExternalInput").ap()
    bias_in = nc.dram_tensor("bias_in", [1, D], F32, kind="ExternalInput").ap()
    y_out = nc.dram_tensor("y_out", [NSEQ, S, D], F32, kind="ExternalOutput").ap()
    if DEBUG:
        d_xn1T = nc.dram_tensor("d_xn1T", [128, 8 * 2048], BF16, kind="ExternalOutput").ap()
        d_latT = nc.dram_tensor("d_latT", [128, 2 * 2048], BF16, kind="ExternalOutput").ap()
        d_vrec = nc.dram_tensor("d_vrec", [128, KT * VST], BF16, kind="ExternalOutput").ap()
        d_qA = nc.dram_tensor("d_qA", [128, 2048], BF16, kind="ExternalOutput").ap()
        d_qB = nc.dram_tensor("d_qB", [128, 2048], BF16, kind="ExternalOutput").ap()
        d_krT = nc.dram_tensor("d_krT", [128, 2048], BF16, kind="ExternalOutput").ap()
        d_aoutT = nc.dram_tensor("d_aoutT", [128, 8 * 2048], BF16, kind="ExternalOutput").ap()
        d_wt = nc.dram_tensor("d_wt", [128, 8 * 1024], BF16, kind="ExternalOutput").ap()
        d_weff = nc.dram_tensor("d_weff", [128, 8 * 256], BF16, kind="ExternalOutput").ap()
        d_ab = nc.dram_tensor("d_ab", [128, 1024], BF16, kind="ExternalOutput").ap()
        d_pq = nc.dram_tensor("d_pq", [128, 1024], F32, kind="ExternalOutput").ap()
        d_ftab = nc.dram_tensor("d_ftab", [128, 128], F32, kind="ExternalOutput").ap()
        d_colA = nc.dram_tensor("d_colA", [128, KT], F32, kind="ExternalOutput").ap()
        d_ab16 = nc.dram_tensor("d_ab16", [128, KT * 128], BF16, kind="ExternalOutput").ap()
        d_pv = nc.dram_tensor("d_pv", [128, 1024], F32, kind="ExternalOutput").ap()

    sc = Sched()

    with ExitStack() as st:
        def sb(name, shape, dtype):
            return st.enter_context(nc.sbuf_tensor(name, shape, dtype))

        def ps(name, shape, dtype=F32):
            return st.enter_context(nc.psum_tensor(name, shape, dtype))

        def sem(name):
            return st.enter_context(nc.semaphore(name))

        xn1T = sb("xn1T", [128, 8 * 2048], BF16)
        latT = sb("latT", [128, 2 * 2048], BF16)
        vrec = sb("vrec", [128, KT * VST], BF16)
        aoutT = sb("aoutT", [128, 8 * 2048], BF16)
        w_big = sb("w_big", [128, 8 * 1024], BF16)
        wupk_sb = sb("wupk_sb", [128, 2 * 1024], BF16)
        wupv_sb = sb("wupv_sb", [128, 2 * 1024], BF16)
        weff_sb = sb("weff_sb", [128, 8 * 256], BF16)
        wt_sb = sb("wt_sb", [128, 8 * 1024], BF16)
        eneg = sb("eneg", [128, 2048], F32)
        epos = sb("epos", [128, 2048], F32)
        colA = sb("colA", [128, KT], F32)
        colB = sb("colB", [128, KT], F32)
        ftab = sb("ftab", [128, 128], F32)
        ident = sb("ident", [128, 128], BF16)
        ones_col = sb("ones_col", [128, 1], BF16)
        bias_b = sb("bias_b", [128, 1024], F32)
        wsc_col = sb("wsc_col", [128, 1], F32)
        qA = sb("qA", [128, 2048], BF16)
        qB = sb("qB", [128, 2048], BF16)
        krT = sb("krT", [128, 2048], BF16)
        xt0 = sb("xt0", [128, 1024], F32)
        xt1 = sb("xt1", [128, 1024], F32)
        xnw0 = sb("xnw0", [128, 1024], F32)
        xnw1 = sb("xnw1", [128, 1024], F32)
        xn_stage = sb("xn_stage", [128, 1024], BF16)
        scr_y = sb("scr_y", [128, 1024], F32)
        scr_t = sb("scr_t", [128, 1024], F32)
        onec = sb("onec", [128, 1], F32)
        stat = sb("stat", [128, 16], F32)
        iot_c = sb("iot_c", [128, 1], F32)
        rr = scr_y
        rb = scr_t
        wkvT = aoutT
        abuf0 = xnw0[:, :].bitcast(BF16)[:, 0:1024]
        abuf1 = xnw1[:, :].bitcast(BF16)[:, 0:1024]
        si1 = qA[:, :].bitcast(I32)
        si2 = qB[:, :].bitcast(I32)
        si3 = krT[:, :].bitcast(I32)
        i2db = si2[:, 0:128]
        f2d = krT[:, :].bitcast(F32)[:, 0:128]

        def pad():
            I("vector", lambda e: e.tensor_copy(out=stat[:, 14:15], in_=onec[:]), "sV")

        def pad_tok():
            return I("vector", lambda e: e.tensor_copy(out=stat[:, 15:16], in_=onec[:]), "sV")

        def recip_safe(out_ap, in_ap):
            I("vector", lambda e: e.tensor_copy(out=stat[:, 14:15], in_=onec[:]), "sV")
            I("vector", lambda e: e.tensor_copy(out=stat[:, 15:16], in_=onec[:]), "sV")
            I("vector", lambda e, o=out_ap, i=in_ap: e.reciprocal(o, i), "sV")
            I("vector", lambda e: e.tensor_copy(out=stat[:, 14:15], in_=onec[:]), "sV")
            I("vector", lambda e: e.tensor_copy(out=stat[:, 15:16], in_=onec[:]), "sV")

        def ones_row(srow):
            a = onec[srow:srow + 1, 0:1]
            return bass.AP(a.tensor, a.offset, [[int(a.ap[0][0]), 1], [0, 128]])

        P0 = ps("P0", [128, 1024])
        P1 = ps("P1", [128, 1024])
        P2 = ps("P2", [128, 512])
        P3 = ps("P3", [128, 512])
        P5 = ps("P5", [128, 512])
        P4 = ps("P4", [128, 1024], BF16)

        sL = sem("sL")
        sG2 = sem("sG2")
        sT = sem("sT")
        sV = sem("sV")
        sC = sem("sC")
        sG = sem("sG")
        sS = sem("sS")
        sP = sem("sP")
        SEMS = {"sL": sL, "sG2": sG2, "sT": sT, "sV": sV, "sC": sC, "sG": sG, "sS": sS, "sP": sP}

        def Wt(eng_name, sem_name, tok):
            if tok:
                s = SEMS[sem_name]
                sc.emit(eng_name, lambda e, s=s, t=tok: e.wait_ge(s, t))

        def W(eng_name, sem_name, v=None):
            target = sc.val(sem_name) if v is None else v
            s = SEMS[sem_name]
            if target > 0:
                sc.emit(eng_name, lambda e, s=s, t=target: e.wait_ge(s, t))

        phase = ["prep"]

        def I(eng_name, fn, sem_name=None, k=1):
            strict_here = (STRICT or phase[0] in STRICT_PHASES
                           or (phase[0] + ":" + eng_name) in STRICT_PHASES)
            if strict_here and eng_name in ("tensor", "vector", "scalar"):
                for other in ("sT", "sV", "sC"):
                    own = {"tensor": "sT", "vector": "sV", "scalar": "sC"}[eng_name]
                    if other != own:
                        W(eng_name, other)
                if sem_name is None:
                    sem_name = {"tensor": "sT", "vector": "sV", "scalar": "sC"}[eng_name]
            if sem_name is None:
                sc.emit(eng_name, fn)
                return None
            s = SEMS[sem_name]
            tok = sc.inc(sem_name, k)
            sc.emit(eng_name, lambda e, s=s, k=k, fn=fn: fn(e).then_inc(s, k))
            return tok

        I("sync", lambda e: e.dma_start(out=bias_b[0:1, :], in_=bias_in[:]), "sL", 16)
        I("sync", lambda e: e.dma_start(out=wsc_col[0:1, :], in_=wsc_in[:]), "sL", 16)
        for c in range(8):
            ta = scr_y if c % 2 == 0 else xt0
            tb = scr_t if c % 2 == 0 else xt1
            I("sync", lambda e, c=c, ta=ta: e.dma_start(out=ta[:], in_=wkT_in[128 * c:128 * (c + 1), :]), "sL", 16)
            I("sync", lambda e, c=c, tb=tb: e.dma_start(out=tb[:], in_=wvT_in[128 * c:128 * (c + 1), :]), "sL", 16)
            W("vector", "sL")
            I("vector", lambda e, c=c, ta=ta, tb=tb: e.tensor_add(
                wkvT[:, 1024 * c:1024 * (c + 1)], ta[:], tb[:]), "sV")
            W("sync", "sV")

        I("gpsimd", lambda e: e.dma_start(out=wupk_sb[:, :].rearrange("p (c n) -> p c n", n=1024), in_=wupk_in.rearrange("(c p) n -> p c n", p=128)), "sG2", 16)
        I("gpsimd", lambda e: e.dma_start(out=wupv_sb[:, :].rearrange("p (c n) -> p c n", n=1024), in_=wupv_in.rearrange("(c p) n -> p c n", p=128)), "sG2", 16)
        I("gpsimd", lambda e: e.dma_start(out=abuf0.rearrange("p (c n) -> p c n", n=256), in_=wdn_in[0:512, :].rearrange("(c p) n -> p c n", p=128)), "sG2", 16)
        I("gpsimd", lambda e: e.dma_start(out=abuf1.rearrange("p (c n) -> p c n", n=256), in_=wdn_in[512:1024, :].rearrange("(c p) n -> p c n", p=128)), "sG2", 16)

        I("gpsimd", lambda e: e.iota(eneg[0:1, :], [[1, 2048]], base=0, channel_multiplier=0,
                                     allow_small_or_imprecise_dtypes=True), "sG")
        I("gpsimd", lambda e: e.iota(iot_c[:], [[1, 1]], base=0, channel_multiplier=1,
                                     allow_small_or_imprecise_dtypes=True), "sG")
        I("gpsimd", lambda e: e.iota(i2db, [[1, 128]], base=0, channel_multiplier=-1,
                                     allow_small_or_imprecise_dtypes=True), "sG")

        W("vector", "sG")
        I("vector", lambda e: e.tensor_scalar(si1[:, 0:128], i2db, 0, None, ALU.is_equal), "sV")
        I("vector", lambda e: e.tensor_copy(out=ident[:], in_=si1[:, 0:128]), "sV")
        I("vector", lambda e: e.tensor_copy(out=f2d, in_=i2db), "sV")
        I("vector", lambda e: e.tensor_scalar(f2d, f2d, 0.0, None, ALU.min), "sV")
        I("vector", lambda e: e.tensor_copy(out=epos[0:1, :], in_=eneg[0:1, :]), "sV")
        W("scalar", "sV")
        I("scalar", lambda e: e.activation(out=ftab[:], in_=f2d[:], func=AF.Exp, scale=2.0 * LAM), "sC")
        I("scalar", lambda e: e.activation(out=eneg[0:1, :], in_=eneg[0:1, :], func=AF.Exp, scale=-LAM), "sC")
        I("scalar", lambda e: e.activation(out=epos[0:1, :], in_=epos[0:1, :], func=AF.Exp, scale=LAM), "sC")
        I("scalar", lambda e: e.activation(out=stat[:, 0:1], in_=iot_c[:], func=AF.Exp, scale=LAM), "sC")
        I("scalar", lambda e: e.activation(out=stat[:, 1:2], in_=iot_c[:], func=AF.Exp, scale=-LAM), "sC")
        W("vector", "sC")
        I("vector", lambda e: e.tensor_scalar(eneg[0:1, :], eneg[0:1, :], 1.0 / 32.0, None, ALU.mult), "sV")
        I("vector", lambda e: e.tensor_scalar(epos[0:1, :], epos[0:1, :], 1.0 / 32.0, None, ALU.mult), "sV")
        for kt in range(KT):
            I("vector", lambda e, kt=kt: e.tensor_scalar(colA[:, kt:kt + 1], stat[:, 0:1],
                                                         float(math.exp(LAM * 128 * kt)), None, ALU.mult), "sV")
            I("vector", lambda e, kt=kt: e.tensor_scalar(colB[:, kt:kt + 1], stat[:, 1:2],
                                                         float(math.exp(-LAM * 128 * kt)), None, ALU.mult), "sV")
        I("gpsimd", lambda e: e.memset(onec[:], 1.0), "sG")
        I("gpsimd", lambda e: e.memset(ones_col[:], 1.0), "sG")
        W("tensor", "sV")
        W("tensor", "sG")
        W("tensor", "sL", 32)
        bcast_jobs = []
        for j in range(4):
            bcast_jobs.append((eneg, 512 * j, 512))
            bcast_jobs.append((epos, 512 * j, 512))
        bcast_jobs.append((bias_b, 0, 512))
        bcast_jobs.append((bias_b, 512, 512))
        bcast_jobs.append((wsc_col, 0, 1))
        for (dst, off, wid) in bcast_jobs:
            I("tensor", lambda e, dst=dst, off=off, wid=wid: e.matmul(
                P5[:, 0:wid], ones_row(0), dst[0:1, off:off + wid],
                start=True, stop=True), "sT")
            W("vector", "sT")
            I("vector", lambda e, dst=dst, off=off, wid=wid: e.tensor_copy(
                out=dst[:, off:off + wid], in_=P5[:, 0:wid]), "sV")
            W("tensor", "sV")
        I("gpsimd", lambda e: e.memset(
            vrec[:, :].rearrange("p (k q) -> p k q", q=VST)[:, :, 0:H * 65].rearrange(
                "p k (h j) -> p k h j", j=65)[:, :, :, 64:65], 1.0), "sG")

        if DEBUG and dbg_on("prepb"):
            W("gpsimd", "sV")
            I("gpsimd", lambda e: e.dma_start(out=d_pq[:, 0:1], in_=wsc_col[:, :], allow_slow_non_contiguous=True), "sS", 16)
            I("gpsimd", lambda e: e.dma_start(out=d_pq[:, 1:2], in_=eneg[:, 777:778], allow_slow_non_contiguous=True), "sS", 16)
            I("gpsimd", lambda e: e.dma_start(out=d_pq[:, 2:3], in_=epos[:, 777:778], allow_slow_non_contiguous=True), "sS", 16)
            I("gpsimd", lambda e: e.dma_start(out=d_pq[:, 3:4], in_=bias_b[:, 333:334], allow_slow_non_contiguous=True), "sS", 16)
        scr_i4 = scr_t[:, :].bitcast(I32)
        for c in range(8):
            cs = slice(1024 * c, 1024 * (c + 1))
            W("gpsimd", "sV")
            I("gpsimd", lambda e, c=c: e.dma_start(
                out=scr_y[:], in_=wt8_in.rearrange("(c p) n -> c p n", p=128)[c]), "sG2", 16)
            W("vector", "sG2")
            I("vector", lambda e: e.tensor_copy(out=si1, in_=scr_y[:]), "sV")
            I("vector", lambda e: e.tensor_scalar(si2, si1, 3, 15, ALU.logical_shift_right, ALU.bitwise_and), "sV")
            I("vector", lambda e: e.tensor_scalar(si3, si1, 7, None, ALU.bitwise_and), "sV")
            I("vector", lambda e: e.tensor_scalar(si3, si3, 8, None, ALU.add), "sV")
            I("vector", lambda e: e.tensor_scalar(si1, si1, 7, None, ALU.logical_shift_right), "sV")
            I("vector", lambda e: e.tensor_scalar(si1, si1, -2, 1, ALU.mult, ALU.add), "sV")
            I("vector", lambda e: e.tensor_tensor(scr_i4, si3, si2, ALU.arith_shift_left), "sV")
            I("vector", lambda e: e.tensor_scalar(si3, si3, 2, -16, ALU.mult, ALU.add), "sV")
            I("vector", lambda e: e.tensor_scalar(si2, si2, 0, None, ALU.is_equal), "sV")
            I("vector", lambda e: e.tensor_tensor(si3, si3, si2, ALU.mult), "sV")
            I("vector", lambda e: e.tensor_scalar(si2, si2, -1, 1, ALU.mult, ALU.add), "sV")
            I("vector", lambda e: e.tensor_tensor(scr_i4, scr_i4, si2, ALU.mult), "sV")
            I("vector", lambda e: e.tensor_tensor(scr_i4, scr_i4, si3, ALU.add), "sV")
            I("vector", lambda e: e.tensor_tensor(scr_i4, scr_i4, si1, ALU.mult), "sV")
            I("vector", lambda e: e.tensor_copy(out=scr_t[:], in_=scr_i4), "sV")
            I("vector", lambda e, cs=cs: e.tensor_scalar(wt_sb[:, cs], scr_t[:], float(2.0 ** -10), None, ALU.mult), "sV")

        W("tensor", "sG2", 64)
        W("tensor", "sV")
        for dt_ in range(8):
            for dd in range(8):
                wdn_src = abuf0 if dd < 4 else abuf1
                I("tensor", lambda e, dt_=dt_, dd=dd, wdn_src=wdn_src: e.matmul(
                    P5[:, 0:256],
                    wkvT[:, 1024 * dd + 128 * dt_:1024 * dd + 128 * (dt_ + 1)],
                    wdn_src[:, 256 * (dd % 4):256 * (dd % 4 + 1)],
                    start=(dd == 0), stop=(dd == 7)), "sT" if dd == 7 else None)
            W("vector", "sT")
            I("vector", lambda e, dt_=dt_: e.tensor_copy(out=weff_sb[:, 256 * dt_:256 * (dt_ + 1)], in_=P5[:, 0:256]), "sV")
            W("tensor", "sV")

        PBF = {0: None, 1: None}
        xt_last_reader = {0: None, 1: None}

        def WB(eng_name, bid):
            ent = PBF[bid]
            if ent:
                Wt(eng_name, ent[0], ent[1])

        for sq in range(NSEQ):
            phase[0] = "A"
            W("vector", "sT")
            W("scalar", "sP")
            W("scalar", "sV")
            W("scalar", "sT")
            W("tensor", "sC")
            a_apply = {}
            a_sq = {}
            a_evac = {}
            for tt in range(TT):
                xt = xt0 if tt % 2 == 0 else xt1
                stage = xn_stage if tt % 2 == 0 else krT[:, 0:1024]
                so = 4 * (tt % 2)
                dq = "sync" if tt % 2 == 0 else "gpsimd"
                Wt(dq, "sV", a_apply.get(tt - 2) or xt_last_reader[tt % 2])
                Wt(dq, "sC", a_sq.get(tt - 2))
                dl = I(dq, lambda e, sq=sq, tt=tt, xt=xt: e.dma_start(
                    out=xt[:], in_=x_in[sq, 128 * tt:128 * (tt + 1), :]), "sL", 16)
                Wt("vector", "sL", dl)
                I("vector", lambda e, xt=xt, so=so: e.reduce_sum(stat[:, so:so + 1], xt[:], AX.X), "sV")
                Wt("scalar", "sL", dl)
                a_sq[tt] = I("scalar", lambda e, xt=xt, so=so: e.activation(
                    out=scr_y[:], in_=xt[:], func=AF.Square, accum_out=stat[:, so + 1:so + 2]), "sC")
                Wt("vector", "sC", a_sq[tt])
                I("vector", lambda e, so=so: e.tensor_scalar(stat[:, so:so + 1], stat[:, so:so + 1], 1.0 / D, None, ALU.mult), "sV")
                I("vector", lambda e, so=so: e.tensor_scalar(stat[:, so + 1:so + 2], stat[:, so + 1:so + 2], 1.0 / D, None, ALU.mult), "sV")
                pad()
                I("vector", lambda e, so=so: e.tensor_tensor(stat[:, so + 2:so + 3], stat[:, so:so + 1], stat[:, so:so + 1], ALU.mult), "sV")
                pad(); pad()
                I("vector", lambda e, so=so: e.tensor_tensor(stat[:, so + 1:so + 2], stat[:, so + 1:so + 2], stat[:, so + 2:so + 3], ALU.subtract), "sV")
                pad(); pad()
                teps = I("vector", lambda e, so=so: e.tensor_scalar(stat[:, so + 1:so + 2], stat[:, so + 1:so + 2], EPS, None, ALU.add), "sV")
                Wt("scalar", "sV", teps)
                tsq = I("scalar", lambda e, so=so: e.activation(out=stat[:, so + 2:so + 3], in_=stat[:, so + 1:so + 2], func=AF.Sqrt), "sC")
                Wt("vector", "sC", tsq)
                recip_safe(stat[:, so + 3:so + 4], stat[:, so + 2:so + 3])
                a_apply[tt] = I("vector", lambda e, xt=xt, so=so, stage=stage: e.tensor_scalar(
                    stage[:], xt[:], stat[:, so:so + 1], stat[:, so + 3:so + 4], ALU.subtract, ALU.mult), "sV")
                Wt("tensor", "sV", a_apply[tt])
                Wt("tensor", "sC", a_evac.get(tt - 1))
                trt = None
                for cdx in range(8):
                    trt = I("tensor", lambda e, cdx=cdx, stage=stage: e.transpose(
                        P4[:, 128 * cdx:128 * (cdx + 1)], stage[:, 128 * cdx:128 * (cdx + 1)], ident[:]),
                        "sT" if cdx == 7 else None) or trt
                Wt("scalar", "sT", trt)
                a_evac[tt] = I("scalar", lambda e, tt=tt: e.activation(
                    out=xn1T[:, :].rearrange("p (c t) -> p c t", c=8)[:, :, 128 * tt:128 * (tt + 1)],
                    in_=P4[:, :].rearrange("p (c t) -> p c t", c=8), func=AF.Copy), "sC")


            if DEBUG and sq == 0 and dbg_on("xn1T"):
                W("gpsimd", "sV")
                I("gpsimd", lambda e: e.dma_start(out=d_xn1T[:], in_=xn1T[:, :]), "sS", 16)
            W("gpsimd", "sT")
            I("gpsimd", lambda e: e.dma_start(out=w_big[:, :].rearrange("p (c n) -> p c n", n=1024), in_=wq_in.rearrange("(c p) n -> p c n", p=128)), "sG2", 16)

            phase[0] = "B"
            W("tensor", "sG")
            W("tensor", "sC")
            for lc in range(2):
                for tq in range(4):
                    bid = (lc * 4 + tq) % 2
                    P = P0 if bid == 0 else P1
                    WB("tensor", bid)
                    mm_tok = None
                    for dc in range(8):
                        mm_tok = I("tensor", lambda e, lc=lc, tq=tq, dc=dc, P=P: e.matmul(
                            P[:, 0:512],
                            weff_sb[:, 256 * dc + 128 * lc:256 * dc + 128 * (lc + 1)],
                            xn1T[:, 2048 * dc + 512 * tq:2048 * dc + 512 * (tq + 1)],
                            start=(dc == 0), stop=(dc == 7)), "sT" if dc == 7 else None) or mm_tok
                    Wt("vector", "sT", mm_tok)
                    PBF[bid] = ("sV", I("vector", lambda e, lc=lc, tq=tq, P=P: e.tensor_copy(
                        out=latT[:, 2048 * lc + 512 * tq:2048 * lc + 512 * (tq + 1)], in_=P[:, 0:512]), "sV"))
            for tt in range(TT):
                for half in range(2):
                    bid = (tt * 2 + half) % 2
                    P = P0 if bid == 0 else P1
                    WB("tensor", bid)
                    mm_tok = None
                    for lc in range(2):
                        mm_tok = I("tensor", lambda e, tt=tt, half=half, lc=lc, P=P: e.matmul(
                            P[:, 0:512],
                            latT[:, 2048 * lc + 128 * tt:2048 * lc + 128 * (tt + 1)],
                            wupv_sb[:, 1024 * lc + 512 * half:1024 * lc + 512 * (half + 1)],
                            start=(lc == 0), stop=(lc == 1)), "sT" if lc == 1 else None) or mm_tok
                    Wt("vector", "sT", mm_tok)
                    PBF[bid] = ("sV", I("vector", lambda e, tt=tt, half=half, P=P: e.tensor_copy(
                        out=vrec[:, :].rearrange("p (k q) -> p k q", q=VST)[
                            :, tt, 65 * 8 * half:65 * 8 * (half + 1)].rearrange(
                            "p (h j) -> p h j", j=65)[:, :, 0:64],
                        in_=P[:, 0:512].rearrange("p (h j) -> p h j", j=64)), "sV"))

            if DEBUG and sq == 0 and dbg_on("latT"):
                W("gpsimd", "sV")
                I("gpsimd", lambda e: e.dma_start(out=d_latT[:], in_=latT[:, :]), "sS", 16)
                I("gpsimd", lambda e: e.dma_start(out=d_vrec[:], in_=vrec[:, :]), "sS", 16)
                I("gpsimd", lambda e: e.dma_start(out=d_weff[:], in_=weff_sb[:, :]), "sS", 16)
                I("gpsimd", lambda e: e.dma_start(out=d_wt[:], in_=wt_sb[:, :]), "sS", 16)
            phase[0] = "attn"
            att_prev_pv = {}
            att_last_qk = {0: None}
            att_prev_sum = {0: None}
            pending_norm = [None]
            att_prev_evac = {0: sc.val("sV")}
            att_prev_p5 = {0: sc.val("sV")}
            W("tensor", "sG2")
            W("scalar", "sS")
            W("tensor", "sS")
            W("vector", "sS")
            for hp in range(8):
                prev_qk = att_last_qk[0]
                for tq in range(4):
                    bid = tq % 2
                    P = P0 if bid == 0 else P1
                    WB("tensor", bid)
                    mm_tok = None
                    for lc in range(2):
                        mm_tok = I("tensor", lambda e, hp=hp, tq=tq, lc=lc, P=P: e.matmul(
                            P[:, 0:512],
                            wupk_sb[:, 1024 * lc + 128 * hp:1024 * lc + 128 * (hp + 1)],
                            latT[:, 2048 * lc + 512 * tq:2048 * lc + 512 * (tq + 1)],
                            start=(lc == 0), stop=(lc == 1)), "sT" if lc == 1 else None) or mm_tok
                    Wt("vector", "sT", mm_tok)
                    Wt("vector", "sT", prev_qk)
                    PBF[bid] = ("sV", I("vector", lambda e, tq=tq, P=P: e.tensor_copy(
                        out=krT[:, 512 * tq:512 * (tq + 1)], in_=P[:, 0:512]), "sV"))
                for tq in range(4):
                    bid = tq % 2
                    P = P0 if bid == 0 else P1
                    WB("tensor", bid)
                    mm_tok = None
                    for dc in range(8):
                        mm_tok = I("tensor", lambda e, hp=hp, tq=tq, dc=dc, P=P: e.matmul(
                            P[:, 0:512],
                            w_big[:, 1024 * dc + 128 * hp:1024 * dc + 128 * (hp + 1)],
                            xn1T[:, 2048 * dc + 512 * tq:2048 * dc + 512 * (tq + 1)],
                            start=(dc == 0), stop=(dc == 7)), "sT" if dc == 7 else None) or mm_tok
                    Wt("vector", "sT", mm_tok)
                    Wt("vector", "sT", prev_qk)
                    I("vector", lambda e, tq=tq, P=P: e.tensor_mul(
                        qA[:, 512 * tq:512 * (tq + 1)], P[:, 0:512], eneg[:, 512 * tq:512 * (tq + 1)]), "sV")
                    PBF[bid] = ("sV", I("vector", lambda e, tq=tq, P=P: e.tensor_mul(
                        qB[:, 512 * tq:512 * (tq + 1)], P[:, 0:512], epos[:, 512 * tq:512 * (tq + 1)]), "sV"))
                pp_tok = sc.val("sV")

                if DEBUG and sq == 0 and hp == 0 and dbg_on("qA"):
                    W("gpsimd", "sV")
                    I("gpsimd", lambda e: e.dma_start(out=d_qA[:], in_=qA[:, :]), "sS", 16)
                    I("gpsimd", lambda e: e.dma_start(out=d_qB[:], in_=qB[:, :]), "sS", 16)
                    I("gpsimd", lambda e: e.dma_start(out=d_krT[:], in_=krT[:, :]), "sS", 16)
                    W("tensor", "sS")
                for qg in range(NQG):
                    qgs = QG * qg
                    t_qk = {}
                    c_exp = {}
                    t_pv = {}
                    first_qk = (qg == 0)

                    def emit_qk(kt, qgs=qgs, first_qk=first_qk):
                        k0 = 128 * kt
                        Pq = P0 if kt % 2 == 0 else P1
                        ab = abuf0 if kt % 2 == 0 else abuf1
                        regs = []
                        if k0 + 128 <= qgs:
                            regs.append(("A", 0, QG))
                        elif k0 >= qgs + QG:
                            regs.append(("B", 0, QG))
                        else:
                            rel = k0 - qgs
                            if rel > 0:
                                regs.append(("B", 0, rel))
                            regs.append(("Ad", rel, QG - rel))
                        WB("tensor", kt % 2)
                        if kt == 0 and first_qk:
                            Wt("tensor", "sV", pp_tok)
                        tok = None
                        for (side, rel, wid) in regs:
                            src = qA if side in ("A", "Ad") else qB
                            I("tensor", lambda e, kt=kt, rel=rel, wid=wid, src=src, Pq=Pq, qgs=qgs: e.matmul(
                                Pq[:, rel:rel + wid],
                                krT[0:64, 128 * kt:128 * (kt + 1)],
                                src[0:64, qgs + rel:qgs + rel + wid],
                                start=True, stop=True), None)
                            tok = I("tensor", lambda e, kt=kt, rel=rel, wid=wid, src=src, Pq=Pq, qgs=qgs: e.matmul(
                                Pq[:, 512 + rel:512 + rel + wid],
                                krT[64:128, 128 * kt:128 * (kt + 1)],
                                src[64:128, qgs + rel:qgs + rel + wid],
                                start=True, stop=True, tile_position=(64, 0)), "sT")
                        t_qk[kt] = tok
                        att_last_qk[0] = tok
                        has_diag = any(s == "Ad" for (s, _, _) in regs)
                        v_diag = None
                        if has_diag:
                            rel = k0 - qgs
                            Wt("vector", "sT", t_qk[kt])
                            I("vector", lambda e, rel=rel, Pq=Pq: e.tensor_mul(
                                Pq[:, rel:rel + 128], Pq[:, rel:rel + 128], ftab[:]), "sV")
                            v_diag = I("vector", lambda e, rel=rel, Pq=Pq: e.tensor_mul(
                                Pq[:, 512 + rel:512 + rel + 128], Pq[:, 512 + rel:512 + rel + 128], ftab[:]), "sV")
                        Wt("scalar", "sT", t_qk[kt])
                        Wt("scalar", "sT", att_prev_pv.get(kt % 2))
                        if has_diag:
                            Wt("scalar", "sV", v_diag)
                        if len(regs) == 1:
                            (side, rel, wid) = regs[0]
                            col = colA if side in ("A", "Ad") else colB
                            tok = I("scalar", lambda e, kt=kt, col=col, Pq=Pq, ab=ab: e.activation(
                                out=ab[:, 0:1024], in_=Pq[:, 0:1024],
                                func=AF.Exp, scale=col[:, kt:kt + 1]), "sC")
                        else:
                            for hh in range(2):
                                for (side, rel, wid) in regs:
                                    col = colA if side in ("A", "Ad") else colB
                                    tok = I("scalar", lambda e, rel=rel, wid=wid, kt=kt, col=col, Pq=Pq, ab=ab, hh=hh: e.activation(
                                        out=ab[:, 512 * hh + rel:512 * hh + rel + wid],
                                        in_=Pq[:, 512 * hh + rel:512 * hh + rel + wid],
                                        func=AF.Exp, scale=col[:, kt:kt + 1]), "sC")
                        c_exp[kt] = tok
                        PBF[kt % 2] = ("sC", tok)

                    def emit_pv(kt, hp=hp):
                        ab = abuf0 if kt % 2 == 0 else abuf1
                        Wt("tensor", "sC", c_exp[kt])
                        if kt == 0:
                            Wt("tensor", "sV", att_prev_evac[0])
                            Wt("tensor", "sC", att_prev_sum[0])
                        for hh in range(2):
                            Pv = P2 if hh == 0 else P3
                            ha = 2 * hp + hh
                            tok = I("tensor", lambda e, ha=ha, kt=kt, Pv=Pv, ab=ab, hh=hh: e.matmul(
                                Pv[0:65, :],
                                vrec[:, VST * kt + 65 * ha:VST * kt + 65 * ha + 65],
                                ab[:, 512 * hh:512 * (hh + 1)],
                                start=(kt == 0), stop=(kt == KT - 1)),
                                "sT" if hh == 1 else None)
                        t_pv[kt] = tok
                        att_prev_pv[kt % 2] = tok

                    emit_qk(0)
                    if pending_norm[0] is not None:
                        pending_norm[0]()
                        pending_norm[0] = None
                    for kt in range(1, KT):
                        emit_qk(kt)
                        emit_pv(kt - 1)
                    emit_pv(KT - 1)
                    stoks = []
                    for hh in range(2):
                        Pv = P2 if hh == 0 else P3
                        Wt("scalar", "sT", t_pv[KT - 1])
                        stoks.append(I("scalar", lambda e, hh=hh, Pv=Pv: e.activation(
                            out=rr[64:65, 512 * hh:512 * (hh + 1)], in_=Pv[64:65, :], func=AF.Copy), "sC"))
                    att_prev_sum[0] = stoks[-1]

                    def emit_norm(hp=hp, qgs=qgs, stoks=stoks):
                        for hh in range(2):
                            Pv = P2 if hh == 0 else P3
                            hr = slice(64 * hh, 64 * (hh + 1))
                            Wt("tensor", "sC", stoks[hh])
                            Wt("tensor", "sV", att_prev_p5[0])
                            btok = I("tensor", lambda e, hh=hh: e.matmul(
                                P5[:, 0:512], ones_row(64),
                                rr[64:65, 512 * hh:512 * (hh + 1)],
                                start=True, stop=True), "sT")
                            Wt("vector", "sT", btok)
                            recip_safe(rb[0:64, 512 * hh:512 * (hh + 1)], P5[0:64, 0:512])
                            att_prev_p5[0] = sc.val("sV")
                            att_prev_evac[0] = I("vector", lambda e, Pv=Pv, hp=hp, qgs=qgs, hh=hh, hr=hr: e.tensor_mul(
                                aoutT[hr, 2048 * hp + qgs:2048 * hp + qgs + 512],
                                Pv[0:64, :], rb[0:64, 512 * hh:512 * (hh + 1)]), "sV")
                    pending_norm[0] = emit_norm
            if pending_norm[0] is not None:
                pending_norm[0]()
                pending_norm[0] = None
            if DEBUG and sq == 0 and dbg_on("aoutT"):
                W("gpsimd", "sV")
                I("gpsimd", lambda e: e.dma_start(out=d_aoutT[:], in_=aoutT[:, :]), "sS", 16)
            W("gpsimd", "sT")
            I("gpsimd", lambda e: e.dma_start(out=w_big[:, :].rearrange("p (c n) -> p c n", n=1024), in_=wo_in.rearrange("(c p) n -> p c n", p=128)), "sG2", 16)

            phase[0] = "cffn_wo"
            W("tensor", "sG2")
            if sq == 0:
                store_tok = {}
            p4_free = {0: None}
            adds_tok = {}
            stt_tok = {}
            quant_tok = {}
            t_wo_tok = {}

            def emit_wo(tt):
                xt = xt0 if tt % 2 == 0 else xt1
                W("sync", "sV")
                I("sync", lambda e, sq=sq, tt=tt, xt=xt: e.dma_start(
                    out=xt[:], in_=x_in[sq, 128 * tt:128 * (tt + 1), :]), "sL", 16)
                if tt == 0:
                    WB("tensor", 0)
                else:
                    Wt("tensor", "sV", adds_tok.get(tt - 1))
                tok = None
                for half in range(2):
                    for dc in range(8):
                        tok = I("tensor", lambda e, tt=tt, half=half, dc=dc: e.matmul(
                            P0[:, 512 * half:512 * (half + 1)],
                            aoutT[:, 2048 * dc + 128 * tt:2048 * dc + 128 * (tt + 1)],
                            w_big[:, 1024 * dc + 512 * half:1024 * dc + 512 * (half + 1)],
                            start=(dc == 0), stop=(dc == 7)),
                            "sT" if (half == 1 and dc == 7) else None) or tok
                t_wo_tok[tt] = tok

            emit_wo(0)
            for tt in range(TT):
                xt = xt0 if tt % 2 == 0 else xt1
                xnw = xnw0 if tt % 2 == 0 else xnw1
                phase[0] = "cffn_wo"
                Wt("vector", "sT", t_wo_tok[tt])
                W("vector", "sL")
                Wt("vector", "sS", store_tok.get(tt % 2))
                atok = None
                for half in range(2):
                    atok = I("vector", lambda e, half=half, xt=xt, xnw=xnw: e.tensor_add(
                        xnw[:, 512 * half:512 * (half + 1)],
                        P0[:, 512 * half:512 * (half + 1)], xt[:, 512 * half:512 * (half + 1)]), "sV")
                adds_tok[tt] = atok
                xt_last_reader[tt % 2] = atok
                if tt + 1 < TT:
                    emit_wo(tt + 1)
                phase[0] = "cffn_ln"
                I("vector", lambda e, xnw=xnw: e.tensor_reduce(stat[:, 4:5], xnw[:], AX.X, ALU.max), "sV")
                I("vector", lambda e, xnw=xnw: e.tensor_reduce(stat[:, 5:6], xnw[:], AX.X, ALU.min), "sV")
                Wt("scalar", "sV", atok)
                W("scalar", "sP")
                I("scalar", lambda e, xnw=xnw: e.activation(out=scr_y[:], in_=xnw[:], func=AF.Copy,
                                                            accum_out=stat[:, 0:1]), "sC")
                I("scalar", lambda e, xnw=xnw: e.activation(out=scr_y[:], in_=xnw[:], func=AF.Square,
                                                            accum_out=stat[:, 1:2]), "sC")
                W("vector", "sC")
                I("vector", lambda e: e.tensor_scalar(stat[:, 0:1], stat[:, 0:1], 1.0 / D, None, ALU.mult), "sV")
                I("vector", lambda e: e.tensor_scalar(stat[:, 1:2], stat[:, 1:2], 1.0 / D, None, ALU.mult), "sV")
                pad()
                I("vector", lambda e: e.tensor_tensor(stat[:, 2:3], stat[:, 0:1], stat[:, 0:1], ALU.mult), "sV")
                I("vector", lambda e: e.tensor_tensor(stat[:, 4:5], stat[:, 4:5], stat[:, 0:1], ALU.subtract), "sV")
                I("vector", lambda e: e.tensor_tensor(stat[:, 5:6], stat[:, 0:1], stat[:, 5:6], ALU.subtract), "sV")
                I("vector", lambda e: e.tensor_tensor(stat[:, 1:2], stat[:, 1:2], stat[:, 2:3], ALU.subtract), "sV")
                I("vector", lambda e: e.tensor_tensor(stat[:, 4:5], stat[:, 4:5], stat[:, 5:6], ALU.max), "sV")
                teps = I("vector", lambda e: e.tensor_scalar(stat[:, 1:2], stat[:, 1:2], EPS, None, ALU.add), "sV")
                Wt("scalar", "sV", teps)
                tsqrt = I("scalar", lambda e: e.activation(out=stat[:, 2:3], in_=stat[:, 1:2], func=AF.Sqrt), "sC")
                Wt("vector", "sC", tsqrt)
                recip_safe(stat[:, 3:4], stat[:, 2:3])
                pad(); pad()
                I("vector", lambda e: e.tensor_scalar(stat[:, 4:5], stat[:, 4:5], stat[:, 3:4], 1e-12, ALU.mult, ALU.max), "sV")
                recip_safe(stat[:, 6:7], stat[:, 4:5])
                I("vector", lambda e: e.tensor_scalar(stat[:, 8:9], stat[:, 4:5], wsc_col[:, 0:1], 1.0 / 448.0, ALU.mult, ALU.mult), "sV")
                pad()
                I("vector", lambda e: e.tensor_scalar(stat[:, 9:10], stat[:, 6:7], stat[:, 3:4], 448.0, ALU.mult, ALU.mult), "sV")
                pad(); pad()
                I("vector", lambda e: e.tensor_scalar(stat[:, 10:11], stat[:, 9:10], stat[:, 0:1], -1.0, ALU.mult, ALU.mult), "sV")
                pad()
                t10 = pad_tok()
                Wt("scalar", "sV", t10)
                tapply = I("scalar", lambda e, xnw=xnw: e.activation(out=scr_y[:], in_=xnw[:], func=AF.Identity,
                                                                    bias=stat[:, 10:11], scale=stat[:, 9:10]), "sC")
                phase[0] = "cffn_q"
                Wt("vector", "sC", tapply)
                I("vector", lambda e: e.tensor_scalar(si1, scr_y[:, :].bitcast(I32), 0x7F800000, None, ALU.bitwise_and), "sV")
                I("vector", lambda e: e.tensor_scalar(si1, si1, 0x3C800000, None, ALU.max), "sV")
                I("vector", lambda e: e.tensor_scalar(si2, si1, -1, 0x44000000, ALU.mult, ALU.add), "sV")
                I("vector", lambda e: e.tensor_scalar(si2, si2, 0x3C800000, None, ALU.add), "sV")
                I("vector", lambda e: e.tensor_tensor(scr_t[:], scr_y[:], si2.bitcast(F32), ALU.mult), "sV")
                I("vector", lambda e: e.tensor_copy(out=si3, in_=scr_t[:]), "sV")
                I("vector", lambda e: e.tensor_scalar(si2, si1, -(3 << 23), None, ALU.add), "sV")
                I("vector", lambda e: e.tensor_copy(out=scr_t[:], in_=si3), "sV")
                quant_tok[tt] = I("vector", lambda e: e.tensor_tensor(xn_stage[:], scr_t[:], si2.bitcast(F32), ALU.mult), "sV")
                phase[0] = "cffn_mm"
                Wt("tensor", "sV", quant_tok[tt])
                Wt("tensor", "sC", p4_free[0])
                t_tr = None
                for cdx in range(8):
                    t_tr = I("tensor", lambda e, cdx=cdx: e.transpose(
                        P4[:, 128 * cdx:128 * (cdx + 1)], xn_stage[:, 128 * cdx:128 * (cdx + 1)], ident[:]),
                        "sT" if cdx == 7 else None) or t_tr
                Wt("scalar", "sT", t_tr)
                p4_free[0] = I("scalar", lambda e, tt=tt: e.activation(
                    out=xn1T[:, :].rearrange("p (c t) -> p c t", c=8)[:, :, 128 * tt:128 * (tt + 1)],
                    in_=P4[:, :].rearrange("p (c t) -> p c t", c=8), func=AF.Copy), "sC")
                t_ffn = None
                Wt("tensor", "sC", p4_free[0])
                Wt("tensor", "sV", stt_tok.get(tt - 1))
                for half in range(2):
                    for kc in range(8):
                        t_ffn = I("tensor", lambda e, tt=tt, half=half, kc=kc: e.matmul(
                            P1[:, 512 * half:512 * (half + 1)],
                            xn1T[:, 2048 * kc + 128 * tt:2048 * kc + 128 * (tt + 1)],
                            wt_sb[:, 1024 * kc + 512 * half:1024 * kc + 512 * (half + 1)],
                            start=(kc == 0), stop=(kc == 7)),
                            "sT" if (half == 1 and kc == 7) else None) or t_ffn
                Wt("vector", "sT", t_ffn)
                stok = None
                for half in range(2):
                    stok = I("vector", lambda e, half=half: e.scalar_tensor_tensor(
                        scr_y[:, 512 * half:512 * (half + 1)],
                        P1[:, 512 * half:512 * (half + 1)], stat[:, 8:9], bias_b[:, 512 * half:512 * (half + 1)],
                        ALU.mult, ALU.add), "sV")
                stt_tok[tt] = stok
                Wt("gpsimd", "sV", stok)
                for half in range(2):
                    I("gpsimd", lambda e, half=half, xnw=xnw: e.tensor_add(
                        xnw[:, 512 * half:512 * (half + 1)],
                        xnw[:, 512 * half:512 * (half + 1)],
                        scr_y[:, 512 * half:512 * (half + 1)]), "sP")
                store_tok[tt % 2] = I("gpsimd", lambda e, sq=sq, tt=tt, xnw=xnw: e.dma_start(
                    out=y_out[sq, 128 * tt:128 * (tt + 1), :], in_=xnw[:]), "sS", 16)
            PBF[0] = ("sV", adds_tok[TT - 1])
            PBF[1] = ("sV", stt_tok[TT - 1])
        W("gpsimd", "sS")

        block = st.enter_context(nc.Block())

        @block.sync
        def _(eng):
            for fn in sc.ops["sync"]:
                fn(eng)

        @block.tensor
        def _(eng):
            for fn in sc.ops["tensor"]:
                fn(eng)

        @block.vector
        def _(eng):
            for fn in sc.ops["vector"]:
                fn(eng)

        @block.scalar
        def _(eng):
            for fn in sc.ops["scalar"]:
                fn(eng)

        @block.gpsimd
        def _(eng):
            for fn in sc.ops["gpsimd"]:
                fn(eng)

    return nc


def kernel(L0_attn_x, L0_attn_W_q, L0_attn_W_k, L0_attn_W_v, L0_attn_W_o,
           L0_attn_W_down_kv, L0_attn_W_up_k, L0_attn_W_up_v,
           L0_ffn_weight, L0_ffn_weight_scale, L0_ffn_bias):
    x = np.ascontiguousarray(np.asarray(L0_attn_x, dtype=np.float32))
    wq = np.ascontiguousarray(np.asarray(L0_attn_W_q, dtype=np.float32))
    wkT = np.ascontiguousarray(np.asarray(L0_attn_W_k, dtype=np.float32).T)
    wvT = np.ascontiguousarray(np.asarray(L0_attn_W_v, dtype=np.float32).T)
    wdn = np.ascontiguousarray(np.asarray(L0_attn_W_down_kv, dtype=np.float32))
    wupk = np.ascontiguousarray(np.asarray(L0_attn_W_up_k, dtype=np.float32))
    wupv = np.ascontiguousarray(np.asarray(L0_attn_W_up_v, dtype=np.float32))
    wo = np.ascontiguousarray(np.asarray(L0_attn_W_o, dtype=np.float32))
    wt8 = np.ascontiguousarray(np.asarray(L0_ffn_weight).view(np.uint8).T)
    wsc = np.asarray(L0_ffn_weight_scale, dtype=np.float32).reshape(1, 1)
    bias = np.asarray(L0_ffn_bias, dtype=np.float32).reshape(1, D)

    nc = build_nc()
    in_maps = []
    for c in range(8):
        in_maps.append({
            "x_in": np.ascontiguousarray(x[2 * c:2 * c + 2]),
            "wq_in": wq, "wkT_in": wkT, "wvT_in": wvT, "wdn_in": wdn,
            "wupk_in": wupk, "wupv_in": wupv, "wo_in": wo,
            "wt8_in": wt8, "wsc_in": wsc, "bias_in": bias,
        })
    res = run_bass_kernel_spmd(nc, in_maps, core_ids=list(range(8)))
    out = np.empty((16, S, D), dtype=np.float32)
    for c in range(8):
        out[2 * c:2 * c + 2] = res.results[c]["y_out"]
    return out


# version 11
# speedup vs baseline: 1.2486x; 1.0603x over previous
import math
from contextlib import ExitStack

import numpy as np

import concourse.bass as bass
import concourse.mybir as mybir
from concourse.bass_utils import run_bass_kernel_spmd

F32 = mybir.dt.float32
BF16 = mybir.dt.bfloat16
I32 = mybir.dt.int32
U8 = mybir.dt.uint8
AF = mybir.ActivationFunctionType
ALU = mybir.AluOpType
AX = mybir.AxisListType

S = 2048
D = 1024
H = 16
HD = 64
DL = 256
LAM = 0.01
NSEQ = 2
TT = 16
KT = 16
NQG = 4
QG = 512
EPS = 1e-5
VST = H * 65 + 1


class Sched:
    ENGS = ("sync", "tensor", "vector", "scalar", "gpsimd")

    def __init__(self):
        self.ops = {e: [] for e in self.ENGS}
        self.cnt = {}

    def emit(self, eng, fn):
        self.ops[eng].append(fn)

    def inc(self, sem_name, k=1):
        self.cnt[sem_name] = self.cnt.get(sem_name, 0) + k
        return self.cnt[sem_name]

    def val(self, sem_name):
        return self.cnt.get(sem_name, 0)


DEBUG = False
STRICT = False


def dbg_on(name):
    return DEBUG is True or (isinstance(DEBUG, (set, frozenset)) and name in DEBUG)

STRICT_PHASES = set()


def build_nc():
    nc = bass.Bass("TRN2", target_bir_lowering=False, detect_race_conditions=False)

    x_in = nc.dram_tensor("x_in", [NSEQ, S, D], F32, kind="ExternalInput").ap()
    wq_in = nc.dram_tensor("wq_in", [D, D], F32, kind="ExternalInput").ap()
    wkT_in = nc.dram_tensor("wkT_in", [D, D], F32, kind="ExternalInput").ap()
    wvT_in = nc.dram_tensor("wvT_in", [D, D], F32, kind="ExternalInput").ap()
    wdn_in = nc.dram_tensor("wdn_in", [D, DL], F32, kind="ExternalInput").ap()
    wupk_in = nc.dram_tensor("wupk_in", [DL, D], F32, kind="ExternalInput").ap()
    wupv_in = nc.dram_tensor("wupv_in", [DL, D], F32, kind="ExternalInput").ap()
    wo_in = nc.dram_tensor("wo_in", [D, D], F32, kind="ExternalInput").ap()
    wt8_in = nc.dram_tensor("wt8_in", [D, D], U8, kind="ExternalInput").ap()
    wsc_in = nc.dram_tensor("wsc_in", [1, 1], F32, kind="ExternalInput").ap()
    bias_in = nc.dram_tensor("bias_in", [1, D], F32, kind="ExternalInput").ap()
    y_out = nc.dram_tensor("y_out", [NSEQ, S, D], F32, kind="ExternalOutput").ap()
    if DEBUG:
        d_xn1T = nc.dram_tensor("d_xn1T", [128, 8 * 2048], BF16, kind="ExternalOutput").ap()
        d_latT = nc.dram_tensor("d_latT", [128, 2 * 2048], BF16, kind="ExternalOutput").ap()
        d_vrec = nc.dram_tensor("d_vrec", [128, KT * VST], BF16, kind="ExternalOutput").ap()
        d_qA = nc.dram_tensor("d_qA", [128, 2048], BF16, kind="ExternalOutput").ap()
        d_qB = nc.dram_tensor("d_qB", [128, 2048], BF16, kind="ExternalOutput").ap()
        d_krT = nc.dram_tensor("d_krT", [128, 2048], BF16, kind="ExternalOutput").ap()
        d_aoutT = nc.dram_tensor("d_aoutT", [128, 8 * 2048], BF16, kind="ExternalOutput").ap()
        d_wt = nc.dram_tensor("d_wt", [128, 8 * 1024], BF16, kind="ExternalOutput").ap()
        d_weff = nc.dram_tensor("d_weff", [128, 8 * 256], BF16, kind="ExternalOutput").ap()
        d_ab = nc.dram_tensor("d_ab", [128, 1024], BF16, kind="ExternalOutput").ap()
        d_pq = nc.dram_tensor("d_pq", [128, 1024], F32, kind="ExternalOutput").ap()
        d_ftab = nc.dram_tensor("d_ftab", [128, 128], F32, kind="ExternalOutput").ap()
        d_colA = nc.dram_tensor("d_colA", [128, KT], F32, kind="ExternalOutput").ap()
        d_ab16 = nc.dram_tensor("d_ab16", [128, KT * 128], BF16, kind="ExternalOutput").ap()
        d_pv = nc.dram_tensor("d_pv", [128, 1024], F32, kind="ExternalOutput").ap()

    sc = Sched()

    with ExitStack() as st:
        def sb(name, shape, dtype):
            return st.enter_context(nc.sbuf_tensor(name, shape, dtype))

        def ps(name, shape, dtype=F32):
            return st.enter_context(nc.psum_tensor(name, shape, dtype))

        def sem(name):
            return st.enter_context(nc.semaphore(name))

        xn1T = sb("xn1T", [128, 8 * 2048], BF16)
        latT = sb("latT", [128, 2 * 2048], BF16)
        vrec = sb("vrec", [128, KT * VST], BF16)
        aoutT = sb("aoutT", [128, 8 * 2048], BF16)
        w_big = sb("w_big", [128, 8 * 1024], BF16)
        wupk_sb = sb("wupk_sb", [128, 2 * 1024], BF16)
        wupv_sb = sb("wupv_sb", [128, 2 * 1024], BF16)
        weff_sb = sb("weff_sb", [128, 8 * 256], BF16)
        wt_sb = sb("wt_sb", [128, 8 * 1024], BF16)
        eneg = sb("eneg", [128, 2048], F32)
        epos = sb("epos", [128, 2048], F32)
        colA = sb("colA", [128, KT], F32)
        colB = sb("colB", [128, KT], F32)
        ftab = sb("ftab", [128, 128], F32)
        ident = sb("ident", [128, 128], BF16)
        ones_col = sb("ones_col", [128, 1], BF16)
        bias_b = sb("bias_b", [128, 1024], F32)
        wsc_col = sb("wsc_col", [128, 1], F32)
        qA = sb("qA", [128, 2048], BF16)
        qB = sb("qB", [128, 2048], BF16)
        krT = sb("krT", [128, 2048], BF16)
        xt0 = sb("xt0", [128, 1024], F32)
        xt1 = sb("xt1", [128, 1024], F32)
        xnw0 = sb("xnw0", [128, 1024], F32)
        xnw1 = sb("xnw1", [128, 1024], F32)
        xn_stage = sb("xn_stage", [128, 1024], BF16)
        scr_y = sb("scr_y", [128, 1024], F32)
        scr_t = sb("scr_t", [128, 1024], F32)
        onec = sb("onec", [128, 1], F32)
        stat = sb("stat", [128, 32], F32)
        iot_c = sb("iot_c", [128, 1], F32)
        rr = scr_y
        rb = scr_t
        wkvT = aoutT
        abuf0 = xnw0[:, :].bitcast(BF16)[:, 0:1024]
        abuf1 = xnw1[:, :].bitcast(BF16)[:, 0:1024]
        si1 = qA[:, :].bitcast(I32)
        si2 = qB[:, :].bitcast(I32)
        si3 = krT[:, :].bitcast(I32)
        i2db = si2[:, 0:128]
        f2d = krT[:, :].bitcast(F32)[:, 0:128]
        vrecF = vrec[:, :].bitcast(F32)
        sqj_c = vrecF[:, 0:1024]
        ffo0 = vrecF[:, 1024:2048]
        ffo1 = vrecF[:, 2048:3072]
        stage_c1 = latT[:, 0:1024]

        def pad():
            I("vector", lambda e: e.tensor_copy(out=stat[:, 14:15], in_=onec[:]), "sV")

        def pad_tok():
            return I("vector", lambda e: e.tensor_copy(out=stat[:, 15:16], in_=onec[:]), "sV")

        def recip_safe(out_ap, in_ap):
            I("vector", lambda e: e.tensor_copy(out=stat[:, 14:15], in_=onec[:]), "sV")
            I("vector", lambda e: e.tensor_copy(out=stat[:, 15:16], in_=onec[:]), "sV")
            I("vector", lambda e, o=out_ap, i=in_ap: e.reciprocal(o, i), "sV")
            I("vector", lambda e: e.tensor_copy(out=stat[:, 14:15], in_=onec[:]), "sV")
            I("vector", lambda e: e.tensor_copy(out=stat[:, 15:16], in_=onec[:]), "sV")

        def ones_row(srow):
            a = onec[srow:srow + 1, 0:1]
            return bass.AP(a.tensor, a.offset, [[int(a.ap[0][0]), 1], [0, 128]])

        P0 = ps("P0", [128, 1024])
        P1 = ps("P1", [128, 1024])
        P2 = ps("P2", [128, 512])
        P3 = ps("P3", [128, 512])
        P5 = ps("P5", [128, 512])
        P4 = ps("P4", [128, 1024], BF16)

        sL = sem("sL")
        sG2 = sem("sG2")
        sT = sem("sT")
        sV = sem("sV")
        sC = sem("sC")
        sG = sem("sG")
        sS = sem("sS")
        sP = sem("sP")
        SEMS = {"sL": sL, "sG2": sG2, "sT": sT, "sV": sV, "sC": sC, "sG": sG, "sS": sS, "sP": sP}

        def Wt(eng_name, sem_name, tok):
            if tok:
                s = SEMS[sem_name]
                sc.emit(eng_name, lambda e, s=s, t=tok: e.wait_ge(s, t))

        def W(eng_name, sem_name, v=None):
            target = sc.val(sem_name) if v is None else v
            s = SEMS[sem_name]
            if target > 0:
                sc.emit(eng_name, lambda e, s=s, t=target: e.wait_ge(s, t))

        phase = ["prep"]

        def I(eng_name, fn, sem_name=None, k=1):
            strict_here = (STRICT or phase[0] in STRICT_PHASES
                           or (phase[0] + ":" + eng_name) in STRICT_PHASES)
            if strict_here and eng_name in ("tensor", "vector", "scalar"):
                for other in ("sT", "sV", "sC"):
                    own = {"tensor": "sT", "vector": "sV", "scalar": "sC"}[eng_name]
                    if other != own:
                        W(eng_name, other)
                if sem_name is None:
                    sem_name = {"tensor": "sT", "vector": "sV", "scalar": "sC"}[eng_name]
            if sem_name is None:
                sc.emit(eng_name, fn)
                return None
            s = SEMS[sem_name]
            tok = sc.inc(sem_name, k)
            sc.emit(eng_name, lambda e, s=s, k=k, fn=fn: fn(e).then_inc(s, k))
            return tok

        I("sync", lambda e: e.dma_start(out=bias_b[0:1, :], in_=bias_in[:]), "sL", 16)
        I("sync", lambda e: e.dma_start(out=wsc_col[0:1, :], in_=wsc_in[:]), "sL", 16)
        for c in range(8):
            ta = scr_y if c % 2 == 0 else xt0
            tb = scr_t if c % 2 == 0 else xt1
            I("sync", lambda e, c=c, ta=ta: e.dma_start(out=ta[:], in_=wkT_in[128 * c:128 * (c + 1), :]), "sL", 16)
            I("sync", lambda e, c=c, tb=tb: e.dma_start(out=tb[:], in_=wvT_in[128 * c:128 * (c + 1), :]), "sL", 16)
            W("vector", "sL")
            I("vector", lambda e, c=c, ta=ta, tb=tb: e.tensor_add(
                wkvT[:, 1024 * c:1024 * (c + 1)], ta[:], tb[:]), "sV")
            W("sync", "sV")

        I("gpsimd", lambda e: e.dma_start(out=wupk_sb[:, :].rearrange("p (c n) -> p c n", n=1024), in_=wupk_in.rearrange("(c p) n -> p c n", p=128)), "sG2", 16)
        I("gpsimd", lambda e: e.dma_start(out=wupv_sb[:, :].rearrange("p (c n) -> p c n", n=1024), in_=wupv_in.rearrange("(c p) n -> p c n", p=128)), "sG2", 16)
        I("gpsimd", lambda e: e.dma_start(out=abuf0.rearrange("p (c n) -> p c n", n=256), in_=wdn_in[0:512, :].rearrange("(c p) n -> p c n", p=128)), "sG2", 16)
        I("gpsimd", lambda e: e.dma_start(out=abuf1.rearrange("p (c n) -> p c n", n=256), in_=wdn_in[512:1024, :].rearrange("(c p) n -> p c n", p=128)), "sG2", 16)

        I("gpsimd", lambda e: e.iota(eneg[0:1, :], [[1, 2048]], base=0, channel_multiplier=0,
                                     allow_small_or_imprecise_dtypes=True), "sG")
        I("gpsimd", lambda e: e.iota(iot_c[:], [[1, 1]], base=0, channel_multiplier=1,
                                     allow_small_or_imprecise_dtypes=True), "sG")
        I("gpsimd", lambda e: e.iota(i2db, [[1, 128]], base=0, channel_multiplier=-1,
                                     allow_small_or_imprecise_dtypes=True), "sG")

        W("vector", "sG")
        I("vector", lambda e: e.tensor_scalar(si1[:, 0:128], i2db, 0, None, ALU.is_equal), "sV")
        I("vector", lambda e: e.tensor_copy(out=ident[:], in_=si1[:, 0:128]), "sV")
        I("vector", lambda e: e.tensor_copy(out=f2d, in_=i2db), "sV")
        I("vector", lambda e: e.tensor_scalar(f2d, f2d, 0.0, None, ALU.min), "sV")
        I("vector", lambda e: e.tensor_copy(out=epos[0:1, :], in_=eneg[0:1, :]), "sV")
        W("scalar", "sV")
        I("scalar", lambda e: e.activation(out=ftab[:], in_=f2d[:], func=AF.Exp, scale=2.0 * LAM), "sC")
        I("scalar", lambda e: e.activation(out=eneg[0:1, :], in_=eneg[0:1, :], func=AF.Exp, scale=-LAM), "sC")
        I("scalar", lambda e: e.activation(out=epos[0:1, :], in_=epos[0:1, :], func=AF.Exp, scale=LAM), "sC")
        I("scalar", lambda e: e.activation(out=stat[:, 0:1], in_=iot_c[:], func=AF.Exp, scale=LAM), "sC")
        I("scalar", lambda e: e.activation(out=stat[:, 1:2], in_=iot_c[:], func=AF.Exp, scale=-LAM), "sC")
        W("vector", "sC")
        I("vector", lambda e: e.tensor_scalar(eneg[0:1, :], eneg[0:1, :], 1.0 / 32.0, None, ALU.mult), "sV")
        I("vector", lambda e: e.tensor_scalar(epos[0:1, :], epos[0:1, :], 1.0 / 32.0, None, ALU.mult), "sV")
        for kt in range(KT):
            I("vector", lambda e, kt=kt: e.tensor_scalar(colA[:, kt:kt + 1], stat[:, 0:1],
                                                         float(math.exp(LAM * 128 * kt)), None, ALU.mult), "sV")
            I("vector", lambda e, kt=kt: e.tensor_scalar(colB[:, kt:kt + 1], stat[:, 1:2],
                                                         float(math.exp(-LAM * 128 * kt)), None, ALU.mult), "sV")
        I("gpsimd", lambda e: e.memset(onec[:], 1.0), "sG")
        I("gpsimd", lambda e: e.memset(ones_col[:], 1.0), "sG")
        W("tensor", "sV")
        W("tensor", "sG")
        W("tensor", "sL", 32)
        bcast_jobs = []
        for j in range(4):
            bcast_jobs.append((eneg, 512 * j, 512))
            bcast_jobs.append((epos, 512 * j, 512))
        bcast_jobs.append((bias_b, 0, 512))
        bcast_jobs.append((bias_b, 512, 512))
        bcast_jobs.append((wsc_col, 0, 1))
        for (dst, off, wid) in bcast_jobs:
            I("tensor", lambda e, dst=dst, off=off, wid=wid: e.matmul(
                P5[:, 0:wid], ones_row(0), dst[0:1, off:off + wid],
                start=True, stop=True), "sT")
            W("vector", "sT")
            I("vector", lambda e, dst=dst, off=off, wid=wid: e.tensor_copy(
                out=dst[:, off:off + wid], in_=P5[:, 0:wid]), "sV")
            W("tensor", "sV")
        I("gpsimd", lambda e: e.memset(
            vrec[:, :].rearrange("p (k q) -> p k q", q=VST)[:, :, 0:H * 65].rearrange(
                "p k (h j) -> p k h j", j=65)[:, :, :, 64:65], 1.0), "sG")

        if DEBUG and dbg_on("prepb"):
            W("gpsimd", "sV")
            I("gpsimd", lambda e: e.dma_start(out=d_pq[:, 0:1], in_=wsc_col[:, :], allow_slow_non_contiguous=True), "sS", 16)
            I("gpsimd", lambda e: e.dma_start(out=d_pq[:, 1:2], in_=eneg[:, 777:778], allow_slow_non_contiguous=True), "sS", 16)
            I("gpsimd", lambda e: e.dma_start(out=d_pq[:, 2:3], in_=epos[:, 777:778], allow_slow_non_contiguous=True), "sS", 16)
            I("gpsimd", lambda e: e.dma_start(out=d_pq[:, 3:4], in_=bias_b[:, 333:334], allow_slow_non_contiguous=True), "sS", 16)
        scr_i4 = scr_t[:, :].bitcast(I32)
        for c in range(8):
            cs = slice(1024 * c, 1024 * (c + 1))
            W("gpsimd", "sV")
            I("gpsimd", lambda e, c=c: e.dma_start(
                out=scr_y[:], in_=wt8_in.rearrange("(c p) n -> c p n", p=128)[c]), "sG2", 16)
            W("vector", "sG2")
            I("vector", lambda e: e.tensor_copy(out=si1, in_=scr_y[:]), "sV")
            I("vector", lambda e: e.tensor_scalar(si2, si1, 3, 15, ALU.logical_shift_right, ALU.bitwise_and), "sV")
            I("vector", lambda e: e.tensor_scalar(si3, si1, 7, None, ALU.bitwise_and), "sV")
            I("vector", lambda e: e.tensor_scalar(si3, si3, 8, None, ALU.add), "sV")
            I("vector", lambda e: e.tensor_scalar(si1, si1, 7, None, ALU.logical_shift_right), "sV")
            I("vector", lambda e: e.tensor_scalar(si1, si1, -2, 1, ALU.mult, ALU.add), "sV")
            I("vector", lambda e: e.tensor_tensor(scr_i4, si3, si2, ALU.arith_shift_left), "sV")
            I("vector", lambda e: e.tensor_scalar(si3, si3, 2, -16, ALU.mult, ALU.add), "sV")
            I("vector", lambda e: e.tensor_scalar(si2, si2, 0, None, ALU.is_equal), "sV")
            I("vector", lambda e: e.tensor_tensor(si3, si3, si2, ALU.mult), "sV")
            I("vector", lambda e: e.tensor_scalar(si2, si2, -1, 1, ALU.mult, ALU.add), "sV")
            I("vector", lambda e: e.tensor_tensor(scr_i4, scr_i4, si2, ALU.mult), "sV")
            I("vector", lambda e: e.tensor_tensor(scr_i4, scr_i4, si3, ALU.add), "sV")
            I("vector", lambda e: e.tensor_tensor(scr_i4, scr_i4, si1, ALU.mult), "sV")
            I("vector", lambda e: e.tensor_copy(out=scr_t[:], in_=scr_i4), "sV")
            I("vector", lambda e, cs=cs: e.tensor_scalar(wt_sb[:, cs], scr_t[:], float(2.0 ** -10), None, ALU.mult), "sV")

        W("tensor", "sG2", 64)
        W("tensor", "sV")
        for dt_ in range(8):
            for dd in range(8):
                wdn_src = abuf0 if dd < 4 else abuf1
                I("tensor", lambda e, dt_=dt_, dd=dd, wdn_src=wdn_src: e.matmul(
                    P5[:, 0:256],
                    wkvT[:, 1024 * dd + 128 * dt_:1024 * dd + 128 * (dt_ + 1)],
                    wdn_src[:, 256 * (dd % 4):256 * (dd % 4 + 1)],
                    start=(dd == 0), stop=(dd == 7)), "sT" if dd == 7 else None)
            W("vector", "sT")
            I("vector", lambda e, dt_=dt_: e.tensor_copy(out=weff_sb[:, 256 * dt_:256 * (dt_ + 1)], in_=P5[:, 0:256]), "sV")
            W("tensor", "sV")

        PBF = {0: None, 1: None}
        xt_last_reader = {0: None, 1: None}

        def WB(eng_name, bid):
            ent = PBF[bid]
            if ent:
                Wt(eng_name, ent[0], ent[1])

        for sq in range(NSEQ):
            phase[0] = "A"
            W("vector", "sT")
            W("scalar", "sP")
            W("scalar", "sV")
            W("scalar", "sT")
            W("tensor", "sC")
            a_apply = {}
            a_sq = {}
            a_evac = {}
            for tt in range(TT):
                xt = xt0 if tt % 2 == 0 else xt1
                stage = xn_stage if tt % 2 == 0 else krT[:, 0:1024]
                so = 4 * (tt % 2)
                dq = "sync" if tt % 2 == 0 else "gpsimd"
                Wt(dq, "sV", a_apply.get(tt - 2) or xt_last_reader[tt % 2])
                Wt(dq, "sC", a_sq.get(tt - 2))
                dl = I(dq, lambda e, sq=sq, tt=tt, xt=xt: e.dma_start(
                    out=xt[:], in_=x_in[sq, 128 * tt:128 * (tt + 1), :]), "sL", 16)
                Wt("vector", "sL", dl)
                I("vector", lambda e, xt=xt, so=so: e.reduce_sum(stat[:, so:so + 1], xt[:], AX.X), "sV")
                Wt("scalar", "sL", dl)
                a_sq[tt] = I("scalar", lambda e, xt=xt, so=so: e.activation(
                    out=scr_y[:], in_=xt[:], func=AF.Square, accum_out=stat[:, so + 1:so + 2]), "sC")
                Wt("vector", "sC", a_sq[tt])
                I("vector", lambda e, so=so: e.tensor_scalar(stat[:, so:so + 1], stat[:, so:so + 1], 1.0 / D, None, ALU.mult), "sV")
                I("vector", lambda e, so=so: e.tensor_scalar(stat[:, so + 1:so + 2], stat[:, so + 1:so + 2], 1.0 / D, None, ALU.mult), "sV")
                pad()
                I("vector", lambda e, so=so: e.tensor_tensor(stat[:, so + 2:so + 3], stat[:, so:so + 1], stat[:, so:so + 1], ALU.mult), "sV")
                pad(); pad()
                I("vector", lambda e, so=so: e.tensor_tensor(stat[:, so + 1:so + 2], stat[:, so + 1:so + 2], stat[:, so + 2:so + 3], ALU.subtract), "sV")
                pad(); pad()
                teps = I("vector", lambda e, so=so: e.tensor_scalar(stat[:, so + 1:so + 2], stat[:, so + 1:so + 2], EPS, None, ALU.add), "sV")
                Wt("scalar", "sV", teps)
                tsq = I("scalar", lambda e, so=so: e.activation(out=stat[:, so + 2:so + 3], in_=stat[:, so + 1:so + 2], func=AF.Sqrt), "sC")
                Wt("vector", "sC", tsq)
                recip_safe(stat[:, so + 3:so + 4], stat[:, so + 2:so + 3])
                a_apply[tt] = I("vector", lambda e, xt=xt, so=so, stage=stage: e.tensor_scalar(
                    stage[:], xt[:], stat[:, so:so + 1], stat[:, so + 3:so + 4], ALU.subtract, ALU.mult), "sV")
                Wt("tensor", "sV", a_apply[tt])
                Wt("tensor", "sC", a_evac.get(tt - 1))
                trt = None
                for cdx in range(8):
                    trt = I("tensor", lambda e, cdx=cdx, stage=stage: e.transpose(
                        P4[:, 128 * cdx:128 * (cdx + 1)], stage[:, 128 * cdx:128 * (cdx + 1)], ident[:]),
                        "sT" if cdx == 7 else None) or trt
                Wt("scalar", "sT", trt)
                a_evac[tt] = I("scalar", lambda e, tt=tt: e.activation(
                    out=xn1T[:, :].rearrange("p (c t) -> p c t", c=8)[:, :, 128 * tt:128 * (tt + 1)],
                    in_=P4[:, :].rearrange("p (c t) -> p c t", c=8), func=AF.Copy), "sC")


            if DEBUG and sq == 0 and dbg_on("xn1T"):
                W("gpsimd", "sV")
                I("gpsimd", lambda e: e.dma_start(out=d_xn1T[:], in_=xn1T[:, :]), "sS", 16)
            W("gpsimd", "sT")
            I("gpsimd", lambda e: e.dma_start(out=w_big[:, :].rearrange("p (c n) -> p c n", n=1024), in_=wq_in.rearrange("(c p) n -> p c n", p=128)), "sG2", 16)

            phase[0] = "B"
            if sq > 0:
                W("gpsimd", "sC")
                W("gpsimd", "sV")
                I("gpsimd", lambda e: e.memset(
                    vrec[:, :].rearrange("p (k q) -> p k q", q=VST)[:, :, 0:H * 65].rearrange(
                        "p k (h j) -> p k h j", j=65)[:, :, :, 64:65], 1.0), "sG")
            W("tensor", "sG")
            W("tensor", "sC")
            for lc in range(2):
                for tq in range(4):
                    bid = (lc * 4 + tq) % 2
                    P = P0 if bid == 0 else P1
                    WB("tensor", bid)
                    mm_tok = None
                    for dc in range(8):
                        mm_tok = I("tensor", lambda e, lc=lc, tq=tq, dc=dc, P=P: e.matmul(
                            P[:, 0:512],
                            weff_sb[:, 256 * dc + 128 * lc:256 * dc + 128 * (lc + 1)],
                            xn1T[:, 2048 * dc + 512 * tq:2048 * dc + 512 * (tq + 1)],
                            start=(dc == 0), stop=(dc == 7)), "sT" if dc == 7 else None) or mm_tok
                    Wt("vector", "sT", mm_tok)
                    PBF[bid] = ("sV", I("vector", lambda e, lc=lc, tq=tq, P=P: e.tensor_copy(
                        out=latT[:, 2048 * lc + 512 * tq:2048 * lc + 512 * (tq + 1)], in_=P[:, 0:512]), "sV"))
            for tt in range(TT):
                for half in range(2):
                    bid = (tt * 2 + half) % 2
                    P = P0 if bid == 0 else P1
                    WB("tensor", bid)
                    mm_tok = None
                    for lc in range(2):
                        mm_tok = I("tensor", lambda e, tt=tt, half=half, lc=lc, P=P: e.matmul(
                            P[:, 0:512],
                            latT[:, 2048 * lc + 128 * tt:2048 * lc + 128 * (tt + 1)],
                            wupv_sb[:, 1024 * lc + 512 * half:1024 * lc + 512 * (half + 1)],
                            start=(lc == 0), stop=(lc == 1)), "sT" if lc == 1 else None) or mm_tok
                    Wt("vector", "sT", mm_tok)
                    PBF[bid] = ("sV", I("vector", lambda e, tt=tt, half=half, P=P: e.tensor_copy(
                        out=vrec[:, :].rearrange("p (k q) -> p k q", q=VST)[
                            :, tt, 65 * 8 * half:65 * 8 * (half + 1)].rearrange(
                            "p (h j) -> p h j", j=65)[:, :, 0:64],
                        in_=P[:, 0:512].rearrange("p (h j) -> p h j", j=64)), "sV"))

            if DEBUG and sq == 0 and dbg_on("latT"):
                W("gpsimd", "sV")
                I("gpsimd", lambda e: e.dma_start(out=d_latT[:], in_=latT[:, :]), "sS", 16)
                I("gpsimd", lambda e: e.dma_start(out=d_vrec[:], in_=vrec[:, :]), "sS", 16)
                I("gpsimd", lambda e: e.dma_start(out=d_weff[:], in_=weff_sb[:, :]), "sS", 16)
                I("gpsimd", lambda e: e.dma_start(out=d_wt[:], in_=wt_sb[:, :]), "sS", 16)
            phase[0] = "attn"
            att_prev_pv = {}
            att_last_qk = {0: None}
            att_prev_sum = {0: None}
            pending_norm = [None]
            att_prev_evac = {0: sc.val("sV")}
            att_prev_p5 = {0: sc.val("sV")}
            W("tensor", "sG2")
            W("scalar", "sS")
            W("tensor", "sS")
            W("vector", "sS")
            for hp in range(8):
                prev_qk = att_last_qk[0]
                for tq in range(4):
                    bid = tq % 2
                    P = P0 if bid == 0 else P1
                    WB("tensor", bid)
                    mm_tok = None
                    for lc in range(2):
                        mm_tok = I("tensor", lambda e, hp=hp, tq=tq, lc=lc, P=P: e.matmul(
                            P[:, 0:512],
                            wupk_sb[:, 1024 * lc + 128 * hp:1024 * lc + 128 * (hp + 1)],
                            latT[:, 2048 * lc + 512 * tq:2048 * lc + 512 * (tq + 1)],
                            start=(lc == 0), stop=(lc == 1)), "sT" if lc == 1 else None) or mm_tok
                    Wt("vector", "sT", mm_tok)
                    Wt("vector", "sT", prev_qk)
                    PBF[bid] = ("sV", I("vector", lambda e, tq=tq, P=P: e.tensor_copy(
                        out=krT[:, 512 * tq:512 * (tq + 1)], in_=P[:, 0:512]), "sV"))
                for tq in range(4):
                    bid = tq % 2
                    P = P0 if bid == 0 else P1
                    WB("tensor", bid)
                    mm_tok = None
                    for dc in range(8):
                        mm_tok = I("tensor", lambda e, hp=hp, tq=tq, dc=dc, P=P: e.matmul(
                            P[:, 0:512],
                            w_big[:, 1024 * dc + 128 * hp:1024 * dc + 128 * (hp + 1)],
                            xn1T[:, 2048 * dc + 512 * tq:2048 * dc + 512 * (tq + 1)],
                            start=(dc == 0), stop=(dc == 7)), "sT" if dc == 7 else None) or mm_tok
                    Wt("vector", "sT", mm_tok)
                    Wt("vector", "sT", prev_qk)
                    I("vector", lambda e, tq=tq, P=P: e.tensor_mul(
                        qA[:, 512 * tq:512 * (tq + 1)], P[:, 0:512], eneg[:, 512 * tq:512 * (tq + 1)]), "sV")
                    PBF[bid] = ("sV", I("vector", lambda e, tq=tq, P=P: e.tensor_mul(
                        qB[:, 512 * tq:512 * (tq + 1)], P[:, 0:512], epos[:, 512 * tq:512 * (tq + 1)]), "sV"))
                pp_tok = sc.val("sV")

                if DEBUG and sq == 0 and hp == 0 and dbg_on("qA"):
                    W("gpsimd", "sV")
                    I("gpsimd", lambda e: e.dma_start(out=d_qA[:], in_=qA[:, :]), "sS", 16)
                    I("gpsimd", lambda e: e.dma_start(out=d_qB[:], in_=qB[:, :]), "sS", 16)
                    I("gpsimd", lambda e: e.dma_start(out=d_krT[:], in_=krT[:, :]), "sS", 16)
                    W("tensor", "sS")
                for qg in range(NQG):
                    qgs = QG * qg
                    t_qk = {}
                    c_exp = {}
                    t_pv = {}
                    first_qk = (qg == 0)

                    def emit_qk(kt, qgs=qgs, first_qk=first_qk):
                        k0 = 128 * kt
                        Pq = P0 if kt % 2 == 0 else P1
                        ab = abuf0 if kt % 2 == 0 else abuf1
                        regs = []
                        if k0 + 128 <= qgs:
                            regs.append(("A", 0, QG))
                        elif k0 >= qgs + QG:
                            regs.append(("B", 0, QG))
                        else:
                            rel = k0 - qgs
                            if rel > 0:
                                regs.append(("B", 0, rel))
                            regs.append(("Ad", rel, QG - rel))
                        WB("tensor", kt % 2)
                        if kt == 0 and first_qk:
                            Wt("tensor", "sV", pp_tok)
                        tok = None
                        for (side, rel, wid) in regs:
                            src = qA if side in ("A", "Ad") else qB
                            I("tensor", lambda e, kt=kt, rel=rel, wid=wid, src=src, Pq=Pq, qgs=qgs: e.matmul(
                                Pq[:, rel:rel + wid],
                                krT[0:64, 128 * kt:128 * (kt + 1)],
                                src[0:64, qgs + rel:qgs + rel + wid],
                                start=True, stop=True), None)
                            tok = I("tensor", lambda e, kt=kt, rel=rel, wid=wid, src=src, Pq=Pq, qgs=qgs: e.matmul(
                                Pq[:, 512 + rel:512 + rel + wid],
                                krT[64:128, 128 * kt:128 * (kt + 1)],
                                src[64:128, qgs + rel:qgs + rel + wid],
                                start=True, stop=True, tile_position=(64, 0)), "sT")
                        t_qk[kt] = tok
                        att_last_qk[0] = tok
                        has_diag = any(s == "Ad" for (s, _, _) in regs)
                        v_diag = None
                        if has_diag:
                            rel = k0 - qgs
                            Wt("vector", "sT", t_qk[kt])
                            I("vector", lambda e, rel=rel, Pq=Pq: e.tensor_mul(
                                Pq[:, rel:rel + 128], Pq[:, rel:rel + 128], ftab[:]), "sV")
                            v_diag = I("vector", lambda e, rel=rel, Pq=Pq: e.tensor_mul(
                                Pq[:, 512 + rel:512 + rel + 128], Pq[:, 512 + rel:512 + rel + 128], ftab[:]), "sV")
                        Wt("scalar", "sT", t_qk[kt])
                        Wt("scalar", "sT", att_prev_pv.get(kt % 2))
                        if has_diag:
                            Wt("scalar", "sV", v_diag)
                        if len(regs) == 1:
                            (side, rel, wid) = regs[0]
                            col = colA if side in ("A", "Ad") else colB
                            tok = I("scalar", lambda e, kt=kt, col=col, Pq=Pq, ab=ab: e.activation(
                                out=ab[:, 0:1024], in_=Pq[:, 0:1024],
                                func=AF.Exp, scale=col[:, kt:kt + 1]), "sC")
                        else:
                            for hh in range(2):
                                for (side, rel, wid) in regs:
                                    col = colA if side in ("A", "Ad") else colB
                                    tok = I("scalar", lambda e, rel=rel, wid=wid, kt=kt, col=col, Pq=Pq, ab=ab, hh=hh: e.activation(
                                        out=ab[:, 512 * hh + rel:512 * hh + rel + wid],
                                        in_=Pq[:, 512 * hh + rel:512 * hh + rel + wid],
                                        func=AF.Exp, scale=col[:, kt:kt + 1]), "sC")
                        c_exp[kt] = tok
                        PBF[kt % 2] = ("sC", tok)

                    def emit_pv(kt, hp=hp):
                        ab = abuf0 if kt % 2 == 0 else abuf1
                        Wt("tensor", "sC", c_exp[kt])
                        if kt == 0:
                            Wt("tensor", "sV", att_prev_evac[0])
                            Wt("tensor", "sC", att_prev_sum[0])
                        for hh in range(2):
                            Pv = P2 if hh == 0 else P3
                            ha = 2 * hp + hh
                            tok = I("tensor", lambda e, ha=ha, kt=kt, Pv=Pv, ab=ab, hh=hh: e.matmul(
                                Pv[0:65, :],
                                vrec[:, VST * kt + 65 * ha:VST * kt + 65 * ha + 65],
                                ab[:, 512 * hh:512 * (hh + 1)],
                                start=(kt == 0), stop=(kt == KT - 1)),
                                "sT" if hh == 1 else None)
                        t_pv[kt] = tok
                        att_prev_pv[kt % 2] = tok

                    emit_qk(0)
                    if pending_norm[0] is not None:
                        pending_norm[0]()
                        pending_norm[0] = None
                    for kt in range(1, KT):
                        emit_qk(kt)
                        emit_pv(kt - 1)
                    emit_pv(KT - 1)
                    stoks = []
                    for hh in range(2):
                        Pv = P2 if hh == 0 else P3
                        Wt("scalar", "sT", t_pv[KT - 1])
                        stoks.append(I("scalar", lambda e, hh=hh, Pv=Pv: e.activation(
                            out=rr[64:65, 512 * hh:512 * (hh + 1)], in_=Pv[64:65, :], func=AF.Copy), "sC"))
                    att_prev_sum[0] = stoks[-1]

                    def emit_norm(hp=hp, qgs=qgs, stoks=stoks):
                        for hh in range(2):
                            Pv = P2 if hh == 0 else P3
                            hr = slice(64 * hh, 64 * (hh + 1))
                            Wt("tensor", "sC", stoks[hh])
                            Wt("tensor", "sV", att_prev_p5[0])
                            btok = I("tensor", lambda e, hh=hh: e.matmul(
                                P5[:, 0:512], ones_row(64),
                                rr[64:65, 512 * hh:512 * (hh + 1)],
                                start=True, stop=True), "sT")
                            Wt("vector", "sT", btok)
                            recip_safe(rb[0:64, 512 * hh:512 * (hh + 1)], P5[0:64, 0:512])
                            att_prev_p5[0] = sc.val("sV")
                            att_prev_evac[0] = I("vector", lambda e, Pv=Pv, hp=hp, qgs=qgs, hh=hh, hr=hr: e.tensor_mul(
                                aoutT[hr, 2048 * hp + qgs:2048 * hp + qgs + 512],
                                Pv[0:64, :], rb[0:64, 512 * hh:512 * (hh + 1)]), "sV")
                    pending_norm[0] = emit_norm
            if pending_norm[0] is not None:
                pending_norm[0]()
                pending_norm[0] = None
            if DEBUG and sq == 0 and dbg_on("aoutT"):
                W("gpsimd", "sV")
                I("gpsimd", lambda e: e.dma_start(out=d_aoutT[:], in_=aoutT[:, :]), "sS", 16)
            W("gpsimd", "sT")
            I("gpsimd", lambda e: e.dma_start(out=w_big[:, :].rearrange("p (c n) -> p c n", n=1024), in_=wo_in.rearrange("(c p) n -> p c n", p=128)), "sG2", 16)

            phase[0] = "cffn_wo"
            W("tensor", "sG2")
            if sq == 0:
                store_tok = {}
            adds_tok = {}
            stt_tok = {}
            quant_tok = {}
            t_wo_tok = {}
            tr_tok = {}
            evac_tok = {}
            apply_tok = {}
            ffn_tok = {}
            fin_tok = {}
            sq_tok = {}

            def emit_wo(tt):
                xt = xt0 if tt % 2 == 0 else xt1
                W("sync", "sV")
                I("sync", lambda e, sq=sq, tt=tt, xt=xt: e.dma_start(
                    out=xt[:], in_=x_in[sq, 128 * tt:128 * (tt + 1), :]), "sL", 16)
                if tt == 0:
                    WB("tensor", 0)
                else:
                    Wt("tensor", "sV", adds_tok.get(tt - 1))
                tok = None
                for half in range(2):
                    for dc in range(8):
                        tok = I("tensor", lambda e, tt=tt, half=half, dc=dc: e.matmul(
                            P0[:, 512 * half:512 * (half + 1)],
                            aoutT[:, 2048 * dc + 128 * tt:2048 * dc + 128 * (tt + 1)],
                            w_big[:, 1024 * dc + 512 * half:1024 * dc + 512 * (half + 1)],
                            start=(dc == 0), stop=(dc == 7)),
                            "sT" if (half == 1 and dc == 7) else None) or tok
                t_wo_tok[tt] = tok

            def FRONT(tt):
                xt = xt0 if tt % 2 == 0 else xt1
                xnw = xnw0 if tt % 2 == 0 else xnw1
                so = 16 * (tt % 2)
                phase[0] = "cffn_wo"
                Wt("vector", "sT", t_wo_tok[tt])
                W("vector", "sL")
                Wt("vector", "sS", store_tok.get(tt % 2))
                atok = None
                for half in range(2):
                    atok = I("vector", lambda e, half=half, xt=xt, xnw=xnw: e.tensor_add(
                        xnw[:, 512 * half:512 * (half + 1)],
                        P0[:, 512 * half:512 * (half + 1)], xt[:, 512 * half:512 * (half + 1)]), "sV")
                adds_tok[tt] = atok
                xt_last_reader[tt % 2] = atok
                if tt + 1 < TT:
                    emit_wo(tt + 1)
                phase[0] = "cffn_ln"
                I("vector", lambda e, xnw=xnw, so=so: e.tensor_reduce(stat[:, so + 4:so + 5], xnw[:], AX.X, ALU.max), "sV")
                I("vector", lambda e, xnw=xnw, so=so: e.tensor_reduce(stat[:, so + 5:so + 6], xnw[:], AX.X, ALU.min), "sV")
                Wt("scalar", "sV", atok)
                I("scalar", lambda e, xnw=xnw, so=so: e.activation(out=sqj_c[:], in_=xnw[:], func=AF.Copy,
                                                                   accum_out=stat[:, so:so + 1]), "sC")
                sq_tok[tt] = I("scalar", lambda e, xnw=xnw, so=so: e.activation(out=sqj_c[:], in_=xnw[:], func=AF.Square,
                                                                                accum_out=stat[:, so + 1:so + 2]), "sC")
                Wt("vector", "sC", sq_tok[tt])
                I("vector", lambda e, so=so: e.tensor_scalar(stat[:, so:so + 1], stat[:, so:so + 1], 1.0 / D, None, ALU.mult), "sV")
                I("vector", lambda e, so=so: e.tensor_scalar(stat[:, so + 1:so + 2], stat[:, so + 1:so + 2], 1.0 / D, None, ALU.mult), "sV")
                pad()
                I("vector", lambda e, so=so: e.tensor_tensor(stat[:, so + 2:so + 3], stat[:, so:so + 1], stat[:, so:so + 1], ALU.mult), "sV")
                I("vector", lambda e, so=so: e.tensor_tensor(stat[:, so + 4:so + 5], stat[:, so + 4:so + 5], stat[:, so:so + 1], ALU.subtract), "sV")
                I("vector", lambda e, so=so: e.tensor_tensor(stat[:, so + 5:so + 6], stat[:, so:so + 1], stat[:, so + 5:so + 6], ALU.subtract), "sV")
                I("vector", lambda e, so=so: e.tensor_tensor(stat[:, so + 1:so + 2], stat[:, so + 1:so + 2], stat[:, so + 2:so + 3], ALU.subtract), "sV")
                I("vector", lambda e, so=so: e.tensor_tensor(stat[:, so + 4:so + 5], stat[:, so + 4:so + 5], stat[:, so + 5:so + 6], ALU.max), "sV")
                teps = I("vector", lambda e, so=so: e.tensor_scalar(stat[:, so + 1:so + 2], stat[:, so + 1:so + 2], EPS, None, ALU.add), "sV")
                Wt("scalar", "sV", teps)
                tsqrt = I("scalar", lambda e, so=so: e.activation(out=stat[:, so + 2:so + 3], in_=stat[:, so + 1:so + 2], func=AF.Sqrt), "sC")
                Wt("vector", "sC", tsqrt)
                recip_safe(stat[:, so + 3:so + 4], stat[:, so + 2:so + 3])
                pad(); pad()
                I("vector", lambda e, so=so: e.tensor_scalar(stat[:, so + 4:so + 5], stat[:, so + 4:so + 5], stat[:, so + 3:so + 4], 1e-12, ALU.mult, ALU.max), "sV")
                recip_safe(stat[:, so + 6:so + 7], stat[:, so + 4:so + 5])
                I("vector", lambda e, so=so: e.tensor_scalar(stat[:, so + 8:so + 9], stat[:, so + 4:so + 5], wsc_col[:, 0:1], 1.0 / 448.0, ALU.mult, ALU.mult), "sV")
                pad()
                I("vector", lambda e, so=so: e.tensor_scalar(stat[:, so + 9:so + 10], stat[:, so + 6:so + 7], stat[:, so + 3:so + 4], 448.0, ALU.mult, ALU.mult), "sV")
                pad(); pad()
                I("vector", lambda e, so=so: e.tensor_scalar(stat[:, so + 10:so + 11], stat[:, so + 9:so + 10], stat[:, so:so + 1], -1.0, ALU.mult, ALU.mult), "sV")
                pad()
                t10 = pad_tok()
                Wt("scalar", "sV", t10)
                apply_tok[tt] = I("scalar", lambda e, xnw=xnw, so=so: e.activation(
                    out=scr_y[:], in_=xnw[:], func=AF.Identity,
                    bias=stat[:, so + 10:so + 11], scale=stat[:, so + 9:so + 10]), "sC")

            def BACK1(tt):
                stage = xn_stage if tt % 2 == 0 else stage_c1
                phase[0] = "cffn_q"
                Wt("vector", "sC", apply_tok[tt])
                Wt("vector", "sT", tr_tok.get(tt - 2))
                I("vector", lambda e: e.tensor_scalar(si1, scr_y[:, :].bitcast(I32), 0x7F800000, None, ALU.bitwise_and), "sV")
                I("vector", lambda e: e.tensor_scalar(si1, si1, 0x3C800000, None, ALU.max), "sV")
                I("vector", lambda e: e.tensor_scalar(si2, si1, -1, 0x44000000, ALU.mult, ALU.add), "sV")
                I("vector", lambda e: e.tensor_scalar(si2, si2, 0x3C800000, None, ALU.add), "sV")
                I("vector", lambda e: e.tensor_tensor(scr_t[:], scr_y[:], si2.bitcast(F32), ALU.mult), "sV")
                I("vector", lambda e: e.tensor_copy(out=si3, in_=scr_t[:]), "sV")
                I("vector", lambda e: e.tensor_scalar(si2, si1, -(3 << 23), None, ALU.add), "sV")
                I("vector", lambda e: e.tensor_copy(out=scr_t[:], in_=si3), "sV")
                quant_tok[tt] = I("vector", lambda e, stage=stage: e.tensor_tensor(
                    stage[:], scr_t[:], si2.bitcast(F32), ALU.mult), "sV")
                phase[0] = "cffn_mm"
                Wt("tensor", "sV", quant_tok[tt])
                Wt("tensor", "sC", evac_tok.get(tt - 1))
                t_tr = None
                for cdx in range(8):
                    t_tr = I("tensor", lambda e, cdx=cdx, stage=stage: e.transpose(
                        P4[:, 128 * cdx:128 * (cdx + 1)], stage[:, 128 * cdx:128 * (cdx + 1)], ident[:]),
                        "sT" if cdx == 7 else None) or t_tr
                tr_tok[tt] = t_tr
                Wt("scalar", "sT", t_tr)
                evac_tok[tt] = I("scalar", lambda e, tt=tt: e.activation(
                    out=xn1T[:, :].rearrange("p (c t) -> p c t", c=8)[:, :, 128 * tt:128 * (tt + 1)],
                    in_=P4[:, :].rearrange("p (c t) -> p c t", c=8), func=AF.Copy), "sC")
                t_ffn = None
                Wt("tensor", "sC", evac_tok[tt])
                Wt("tensor", "sV", stt_tok.get(tt - 1))
                for half in range(2):
                    for kc in range(8):
                        t_ffn = I("tensor", lambda e, tt=tt, half=half, kc=kc: e.matmul(
                            P1[:, 512 * half:512 * (half + 1)],
                            xn1T[:, 2048 * kc + 128 * tt:2048 * kc + 128 * (tt + 1)],
                            wt_sb[:, 1024 * kc + 512 * half:1024 * kc + 512 * (half + 1)],
                            start=(kc == 0), stop=(kc == 7)),
                            "sT" if (half == 1 and kc == 7) else None) or t_ffn
                ffn_tok[tt] = t_ffn

            def BACK2(tt):
                xnw = xnw0 if tt % 2 == 0 else xnw1
                ffo = ffo0 if tt % 2 == 0 else ffo1
                so = 16 * (tt % 2)
                Wt("vector", "sT", ffn_tok[tt])
                Wt("vector", "sP", fin_tok.get(tt - 2))
                stok = None
                for half in range(2):
                    stok = I("vector", lambda e, half=half, ffo=ffo, so=so: e.scalar_tensor_tensor(
                        ffo[:, 512 * half:512 * (half + 1)],
                        P1[:, 512 * half:512 * (half + 1)], stat[:, so + 8:so + 9],
                        bias_b[:, 512 * half:512 * (half + 1)],
                        ALU.mult, ALU.add), "sV")
                stt_tok[tt] = stok
                Wt("gpsimd", "sV", stok)
                ftk = None
                for half in range(2):
                    ftk = I("gpsimd", lambda e, half=half, xnw=xnw, ffo=ffo: e.tensor_add(
                        xnw[:, 512 * half:512 * (half + 1)],
                        xnw[:, 512 * half:512 * (half + 1)],
                        ffo[:, 512 * half:512 * (half + 1)]), "sP")
                fin_tok[tt] = ftk
                store_tok[tt % 2] = I("gpsimd", lambda e, sq=sq, tt=tt, xnw=xnw: e.dma_start(
                    out=y_out[sq, 128 * tt:128 * (tt + 1), :], in_=xnw[:]), "sS", 16)

            emit_wo(0)
            for it in range(TT + 2):
                if it >= 2:
                    BACK2(it - 2)
                if 1 <= it <= TT:
                    BACK1(it - 1)
                if it < TT:
                    FRONT(it)
            PBF[0] = ("sV", adds_tok[TT - 1])
            PBF[1] = ("sV", stt_tok[TT - 1])
        W("gpsimd", "sS")

        block = st.enter_context(nc.Block())

        @block.sync
        def _(eng):
            for fn in sc.ops["sync"]:
                fn(eng)

        @block.tensor
        def _(eng):
            for fn in sc.ops["tensor"]:
                fn(eng)

        @block.vector
        def _(eng):
            for fn in sc.ops["vector"]:
                fn(eng)

        @block.scalar
        def _(eng):
            for fn in sc.ops["scalar"]:
                fn(eng)

        @block.gpsimd
        def _(eng):
            for fn in sc.ops["gpsimd"]:
                fn(eng)

    return nc


def kernel(L0_attn_x, L0_attn_W_q, L0_attn_W_k, L0_attn_W_v, L0_attn_W_o,
           L0_attn_W_down_kv, L0_attn_W_up_k, L0_attn_W_up_v,
           L0_ffn_weight, L0_ffn_weight_scale, L0_ffn_bias):
    x = np.ascontiguousarray(np.asarray(L0_attn_x, dtype=np.float32))
    wq = np.ascontiguousarray(np.asarray(L0_attn_W_q, dtype=np.float32))
    wkT = np.ascontiguousarray(np.asarray(L0_attn_W_k, dtype=np.float32).T)
    wvT = np.ascontiguousarray(np.asarray(L0_attn_W_v, dtype=np.float32).T)
    wdn = np.ascontiguousarray(np.asarray(L0_attn_W_down_kv, dtype=np.float32))
    wupk = np.ascontiguousarray(np.asarray(L0_attn_W_up_k, dtype=np.float32))
    wupv = np.ascontiguousarray(np.asarray(L0_attn_W_up_v, dtype=np.float32))
    wo = np.ascontiguousarray(np.asarray(L0_attn_W_o, dtype=np.float32))
    wt8 = np.ascontiguousarray(np.asarray(L0_ffn_weight).view(np.uint8).T)
    wsc = np.asarray(L0_ffn_weight_scale, dtype=np.float32).reshape(1, 1)
    bias = np.asarray(L0_ffn_bias, dtype=np.float32).reshape(1, D)

    nc = build_nc()
    in_maps = []
    for c in range(8):
        in_maps.append({
            "x_in": np.ascontiguousarray(x[2 * c:2 * c + 2]),
            "wq_in": wq, "wkT_in": wkT, "wvT_in": wvT, "wdn_in": wdn,
            "wupk_in": wupk, "wupv_in": wupv, "wo_in": wo,
            "wt8_in": wt8, "wsc_in": wsc, "bias_in": bias,
        })
    res = run_bass_kernel_spmd(nc, in_maps, core_ids=list(range(8)))
    out = np.empty((16, S, D), dtype=np.float32)
    for c in range(8):
        out[2 * c:2 * c + 2] = res.results[c]["y_out"]
    return out
